# version 06
# baseline (speedup 1.0000x reference)
import bisect
import numpy as np
import ml_dtypes
import concourse.bass as bass
import concourse.mybir as mybir
from concourse.bass_utils import run_bass_kernel_spmd
from contextlib import ExitStack

F32 = mybir.dt.float32
BF16 = mybir.dt.bfloat16
AF = mybir.ActivationFunctionType
ALU = mybir.AluOpType
AX = mybir.AxisListType

P = 128
D = 1024
DT = D // P
HEADS = 16
DH = 64
WIN = 256
FH = 2730
NOG = (FH + P - 1) // P
NEG = -1e30
EPS = 1e-5

PC_G = 0
PC_B = 8
PC_B1A = 16
PC_B1G = 16 + NOG
PC_B2 = 16 + 2 * NOG
PC_QS = 24 + 2 * NOG
PC_KS = 25 + 2 * NOG
NPARAM = 26 + 2 * NOG

N_CORES = 8
ATTN_STAGE = 9
ATTN_SUB = 9
FULL_B = 16
FULL_S = 2048
FULL_L = 6


def host_constants(S):
    half = np.arange(0, DH, 2, dtype=np.float64)
    inv_freq = 1.0 / 10000.0 ** (half / DH)
    t = np.arange(S, dtype=np.float64)
    freqs = np.concatenate([t[:, None] * inv_freq[None, :]] * 2, -1)
    base_scale = (half + 0.4 * DH) / (1.4 * DH)
    power = (t - S // 2) / (WIN // 2)
    scale = np.concatenate([base_scale[None, :] ** power[:, None]] * 2, -1)
    cos, sin = np.cos(freqs), np.sin(freqs)
    inv = 1.0 / scale
    sgn = np.concatenate([-np.ones(32), np.ones(32)])[None, :]

    def two(v):
        return np.concatenate([v.T, v.T], axis=0).astype(np.float32)

    rotab = np.stack([two(cos * scale), two(sin * scale * sgn),
                      two(cos * inv), two(sin * inv * sgn)])

    masks = np.full((2, P, 512), NEG, dtype=np.float32)
    for js in range(2):
        jr = np.arange(P)[:, None] + js * P
        qc = np.arange(512)[None, :]
        masks[js][(qc >= jr) & (qc <= jr + WIN)] = 0.0
    masks = masks.astype(ml_dtypes.bfloat16)

    blk = np.zeros((P, P))
    blk[:DH, :DH] = 1.0
    blk[DH:, DH:] = 1.0
    rotR = np.zeros((P, P))
    for d in range(P):
        base, r = d - d % DH, d % DH
        src = base + (r + 32 if r < 32 else r - 32)
        rotR[src, d] = 1.0
    return dict(rotab=rotab, masks=masks,
                ident_bf=np.eye(P, dtype=ml_dtypes.bfloat16),
                ident_f32=np.eye(P, dtype=np.float32),
                blk64=blk.astype(ml_dtypes.bfloat16),
                rotR=rotR.astype(ml_dtypes.bfloat16),
                ones_col=np.ones((P, 1), dtype=ml_dtypes.bfloat16),
                ones_row_f32=np.ones((1, P), dtype=np.float32))


def pack_params(q_scale, k_scale, ln_g, ln_b, b1, b2):
    L = ln_g.shape[0]
    cols, rows = [], []
    for l in range(L):
        g = ln_g[l].reshape(DT, P).T
        bb = ln_b[l].reshape(DT, P).T
        b1a = np.zeros((P, NOG), np.float32)
        b1g = np.zeros((P, NOG), np.float32)
        for og in range(NOG):
            n = min(P, FH - og * P)
            b1a[:n, og] = b1[l, og * P: og * P + n]
            b1g[:n, og] = b1[l, FH + og * P: FH + og * P + n]
        bb2 = b2[l].reshape(DT, P).T
        qs = np.concatenate([q_scale[l], q_scale[l]])[:, None]
        ks = np.concatenate([k_scale[l], k_scale[l]])[:, None]
        cols.append(np.concatenate([g, bb, b1a, b1g, bb2, qs, ks], 1).astype(np.float32))
        rows.append(np.concatenate([q_scale[l], k_scale[l]])[None, :].astype(np.float32))
    return np.stack(cols), np.stack(rows)


class Tk:

    def __init__(self, nc, stack, dma_sems):
        self.nc = nc
        self.eng = {"T": nc.tensor, "V": nc.vector, "S": nc.scalar,
                    "Y": nc.sync, "G": nc.gpsimd}
        self.esem = {}
        self.seq = {}
        self.instrs = {}
        self.incs = {}
        for k in self.eng:
            self.esem[k] = stack.enter_context(nc.semaphore("se" + k))
            self.seq[k] = 0
            self.instrs[k] = []
            self.incs[k] = []
        self.dsem = {}
        self.dcnt = {}
        for name in dma_sems:
            self.dsem[name] = stack.enter_context(nc.semaphore(name))
            self.dcnt[name] = 0
        self.buf = {}
        self.waited = {}

    def _inc_val(self, ekey, seq):
        incs = self.incs[ekey]
        if incs and incs[-1][0] >= seq:
            i = bisect.bisect_left(incs, (seq, -1))
            return incs[i][1]
        val = (incs[-1][1] + 1) if incs else 1
        self.instrs[ekey][seq - 1].then_inc(self.esem[ekey], 1)
        incs.append((seq, val))
        return val

    def _emit_waits(self, waiter, sources):
        need = {}
        for src in sources:
            if src is None:
                continue
            if src[0] == "E":
                _, ekey, seq = src
                if ekey == waiter == "T":
                    continue
                v = self._inc_val(ekey, seq)
                key = ("E", ekey)
                need[key] = max(need.get(key, 0), v)
            else:
                _, sname, cnt = src
                key = ("D", sname)
                need[key] = max(need.get(key, 0), self.dcnt[sname])
        for key, val in sorted(need.items()):
            if self.waited.get((waiter, key), 0) >= val:
                continue
            sem = self.esem[key[1]] if key[0] == "E" else self.dsem[key[1]]
            self.eng[waiter].wait_ge(sem, val)
            self.waited[(waiter, key)] = val

    def _gather(self, reads, writes):
        srcs = []
        for k in reads:
            b = self.buf.get(k)
            if b and b["w"]:
                srcs.append(b["w"])
        for k in writes:
            b = self.buf.get(k)
            if b:
                if b["w"]:
                    srcs.append(b["w"])
                srcs.extend(b["r"].values())
        return srcs

    def _update(self, reads, writes, src):
        for k in writes:
            self.buf[k] = {"w": src, "r": {}}
        for k in reads:
            b = self.buf.setdefault(k, {"w": None, "r": {}})
            b["r"][src[:2] if src[0] == "E" else ("D", src[1])] = src

    @staticmethod
    def _is_psum(k):
        return isinstance(k, tuple) and len(k) == 2 and k[0] in ("ps", "pb")

    def op(self, ekey, fn, reads=(), writes=()):
        srcs = self._gather(reads, writes)
        for k in reads:
            if self._is_psum(k):
                b = self.buf.get(k)
                if b:
                    for rk, v in b["r"].items():
                        if rk[0] == "E" and rk[1] != ekey:
                            srcs.append(v)
        self._emit_waits(ekey, srcs)
        ins = fn()
        self.seq[ekey] += 1
        self.instrs[ekey].append(ins)
        src = ("E", ekey, self.seq[ekey])
        self._update(reads, writes, src)
        return ins

    def dma(self, ekey, sname, fn, reads=(), writes=()):
        if self.dcnt[sname] > 0:
            key = ("D", sname)
            if self.waited.get((ekey, key), 0) < self.dcnt[sname]:
                self.eng[ekey].wait_ge(self.dsem[sname], self.dcnt[sname])
                self.waited[(ekey, key)] = self.dcnt[sname]
        self._emit_waits(ekey, self._gather(reads, writes))
        ins = fn()
        self.dcnt[sname] += 16
        ins.then_inc(self.dsem[sname], 16)
        src = ("D", sname, self.dcnt[sname])
        self._update(reads, writes, src)
        return ins

    def barrier_read(self, ekey, keys):
        self._emit_waits(ekey, self._gather(keys, ()))

    def barrier(self):
        srcs = []
        for ekey in self.eng:
            if self.seq[ekey] > 0:
                srcs.append(("E", ekey, self.seq[ekey]))
        for sname, cnt in self.dcnt.items():
            if cnt > 0:
                srcs.append(("D", sname, cnt))
        for waiter in self.eng:
            self._emit_waits(waiter, [s for s in srcs
                                      if not (s[0] == "E" and s[1] == waiter)])


DMA_SEMS = ["dW", "dP", "dTa", "dTb", "dXa", "dXb", "dQWa", "dQWb", "dKW0", "dKW1",
            "dVW0", "dVW1", "dWO0", "dWO1", "dW1a", "dW1b", "dW2a", "dW2b", "dO", "dAcc"]


def build_program(BPC, S, L, taps=(), skip_a=False, skip_attn=False, skip_ffn=False,
                  skip_tail=False, y0_input=False):
    C = S // 512
    NW = S // WIN
    NT = S // P
    HT = S // 2
    NU = HT // 512
    nc = bass.Bass("TRN2", target_bir_lowering=False)

    X = nc.dram_tensor("X", [BPC, S, D], F32, kind="ExternalInput").ap()
    OUT = nc.dram_tensor("OUT", [BPC, S, D], F32, kind="ExternalOutput").ap()
    Wqa = nc.dram_tensor("Wqa", [D, D], F32, kind="ExternalInput").ap()
    Wka = nc.dram_tensor("Wka", [D, D], F32, kind="ExternalInput").ap()
    Wva = nc.dram_tensor("Wva", [D, D], F32, kind="ExternalInput").ap()
    BV = nc.dram_tensor("BV", [D], F32, kind="ExternalInput").ap()
    ABIAS = nc.dram_tensor("ABIAS", [2, P, DT], F32, kind="ExternalInput").ap()
    WQKV = nc.dram_tensor("WQKV", [L, D, 3 * D], F32, kind="ExternalInput").ap()
    WO = nc.dram_tensor("WO", [L, D, D], F32, kind="ExternalInput").ap()
    W1 = nc.dram_tensor("W1", [L, D, 2 * FH], F32, kind="ExternalInput").ap()
    W2 = nc.dram_tensor("W2", [L, FH, D], F32, kind="ExternalInput").ap()
    PRM = nc.dram_tensor("PRM", [L, P, NPARAM], F32, kind="ExternalInput").ap()
    QKR = nc.dram_tensor("QKR", [L, 1, 2 * DH], F32, kind="ExternalInput").ap()
    ROT = nc.dram_tensor("ROT", [4, P, S], F32, kind="ExternalInput").ap()
    MSK = nc.dram_tensor("MSK", [2, P, 512], BF16, kind="ExternalInput").ap()
    IDB = nc.dram_tensor("IDB", [P, P], BF16, kind="ExternalInput").ap()
    IDF = nc.dram_tensor("IDF", [P, P], F32, kind="ExternalInput").ap()
    BLK = nc.dram_tensor("BLK", [P, P], BF16, kind="ExternalInput").ap()
    ONEC = nc.dram_tensor("ONEC", [P, 1], BF16, kind="ExternalInput").ap()
    ONRF = nc.dram_tensor("ONRF", [1, P], F32, kind="ExternalInput").ap()
    ROTR = nc.dram_tensor("ROTR", [P, P], BF16, kind="ExternalInput").ap()
    if y0_input:
        Y0 = nc.dram_tensor("Y0", [BPC, P, DT * S], F32, kind="ExternalInput").ap()
    tap_t = {}
    for tn in taps:
        tap_t[tn] = nc.dram_tensor("tap_" + tn.replace(":", "_"),
                                   [BPC, P, DT * S], F32, kind="ExternalOutput").ap()

    stack = ExitStack()
    with stack:
        tk = Tk(nc, stack, DMA_SEMS)
        T, V, SC, SY, GP = nc.tensor, nc.vector, nc.scalar, nc.sync, nc.gpsimd

        def sb(name, shape, dtype):
            return stack.enter_context(nc.sbuf_tensor(name, shape, dtype))

        ytr = sb("ytr", [P, DT * S], F32)
        idb = sb("idb", [P, P], BF16)
        idf = sb("idf", [P, P], F32)
        blk64 = sb("blk64", [P, P], BF16)
        onec = sb("onec", [P, 1], BF16)
        onrf = sb("onrf", [1, P], F32)
        prm = sb("prm", [P, NPARAM], F32)
        qkr = sb("qkr", [1, 2 * DH], F32)
        abias = sb("abias", [P, 2 * DT], F32)
        negb = sb("negb", [P, 1], F32)
        srow = sb("srow", [1, 4 * 512], F32)
        bvbc = sb("bvbc", [P, D], F32)
        bndry = sb("bndry", [P, DT], F32)
        epsc = sb("epsc", [P, 1], F32)
        rotr = sb("rotr", [P, P], BF16)
        blkq = sb("blkq", [P, 2 * P], BF16)
        qscol = sb("qscol", [P, 4], F32)

        PS = [stack.enter_context(nc.psum_tensor(f"ps{i}", [P, 512], F32)) for i in range(6)]
        PB = [stack.enter_context(nc.psum_tensor(f"pb{i}", [P, 1024], BF16)) for i in range(2)]
        psk = [("ps", i) for i in range(6)]
        pbk = [("pb", i) for i in range(2)]

        for dst, src_, key in [(idb[:, :], IDB, "idb"), (idf[:, :], IDF, "idf"),
                               (blk64[:, :], BLK, "blk"), (onec[:, :], ONEC, "onec"),
                               (onrf[:, :], ONRF, "onrf"),
                               (abias[:, 0:DT], ABIAS[0], "abias"),
                               (abias[:, DT:2 * DT], ABIAS[1], "abias"),
                               (rotr[:, :], ROTR, "rotr"),
                               (bvbc[:, :], BV[None, :].broadcast_to([P, D]), "bvbc")]:
            tk.dma("Y", "dW", lambda d=dst, s=src_: SY.dma_start(d, s), writes=[key])

        tk.op("V", lambda: V.memset(epsc[:, :], EPS), writes=["epsc"])

        def ysl(dt, t0, n):
            return ytr[:, dt * S + t0: dt * S + t0 + n]

        def b_ffn(l, b):
            with ExitStack() as ls:
                def lsb(n, sh, dt_):
                    return ls.enter_context(nc.sbuf_tensor(f"{n}_l{l}b{b}", sh, dt_))
                ybf = lsb("fybf", [P, NU * DT * 513], BF16)
                sqb = lsb("fsq", [P, 512], BF16)
                xnbf = lsb("fxn", [P, DT * HT], BF16)
                hbuf = lsb("fh", [P, NOG * HT], BF16)
                w1sl = [lsb(f"fw1{i}", [P, D], BF16) for i in range(4)]
                tbuf = lsb("ftmp", [P, 2 * 512], F32)
                fsrow = lsb("fsrow", [1, 1536], F32)
                w2sl = [lsb(f"fw2{i}", [P, NOG * P], BF16) for i in range(2)]

                for hb in range(2):
                    T0 = hb * HT

                    def shsl(u, dt):
                        off = 0 if dt >= 4 else 1
                        return ybf[:, (u * DT + dt) * 513 + off:
                                   (u * DT + dt) * 513 + off + 512]

                    for u in range(NU):
                        t0 = T0 + u * 512
                        for dt in range(DT):
                            if t0 == 0:
                                tk.op("V", lambda u=u, dt=dt: V.memset(
                                    ybf[:, (u * DT + dt) * 513:(u * DT + dt) * 513 + 1],
                                    0.0), writes=[("ybf", u, dt)])
                            elif dt >= 4 and t0 == HT:
                                tk.op("V", lambda u=u, dt=dt: V.tensor_copy(
                                    ybf[:, (u * DT + dt) * 513:(u * DT + dt) * 513 + 1],
                                    bndry[:, dt:dt + 1]),
                                    reads=[("bnd", dt)], writes=[("ybf", u, dt)])
                            else:
                                tk.op("V", lambda u=u, dt=dt, t0=t0: V.tensor_copy(
                                    ybf[:, (u * DT + dt) * 513:(u * DT + dt) * 513 + 1],
                                    ysl(dt, t0 - 1, 1)),
                                    reads=[("ytr", dt)], writes=[("ybf", u, dt)])
                            tk.op("V", lambda u=u, dt=dt, t0=t0: V.tensor_copy(
                                ybf[:, (u * DT + dt) * 513 + 1:(u * DT + dt + 1) * 513],
                                ysl(dt, t0, 512)),
                                reads=[("ytr", dt)], writes=[("ybf", u, dt)])
                    for u in range(NU):
                        mb_ = psk[0 + 2 * u]
                        sb_ = psk[1 + 2 * u]
                        for dt in range(DT):
                            tk.op("T", lambda u=u, dt=dt, mb_=mb_: T.matmul(
                                PS[mb_[1]][0:1, 0:512], onec[:, :], shsl(u, dt),
                                start=(dt == 0), stop=(dt == DT - 1)),
                                reads=[("ybf", u, dt), "onec"], writes=[mb_])
                        for dt in range(DT):
                            tk.op("S", lambda u=u, dt=dt: SC.activation(
                                sqb[:, :], shsl(u, dt), AF.Square),
                                reads=[("ybf", u, dt)], writes=["sqb"])
                            tk.op("T", lambda u=u, dt=dt, sb_=sb_: T.matmul(
                                PS[sb_[1]][0:1, 0:512], onec[:, :], sqb[:, :],
                                start=(dt == 0), stop=(dt == DT - 1)),
                                reads=["sqb", "onec"], writes=[sb_])
                    MCOL = [0, 512]
                    VCOL = [1024, 0]
                    for u in range(NU):
                        mb_ = psk[0 + 2 * u]
                        sb_ = psk[1 + 2 * u]
                        mc, vc = MCOL[u], VCOL[u]
                        tk.op("V", lambda u=u, mb_=mb_, mc=mc: V.tensor_scalar(
                            fsrow[:, mc:mc + 512], PS[mb_[1]][0:1, 0:512],
                            1.0 / D, None, ALU.mult),
                            reads=[mb_], writes=[("srw", mc // 512)])
                        tk.op("V", lambda u=u, mc=mc, vc=vc: V.tensor_tensor(
                            fsrow[:, vc:vc + 512], fsrow[:, mc:mc + 512],
                            fsrow[:, mc:mc + 512], ALU.mult),
                            reads=[("srw", mc // 512)], writes=[("srw", vc // 512)])
                        tk.op("V", lambda u=u, sb_=sb_, vc=vc: V.scalar_tensor_tensor(
                            fsrow[:, vc:vc + 512], PS[sb_[1]][0:1, 0:512],
                            1.0 / D, fsrow[:, vc:vc + 512],
                            ALU.mult, ALU.subtract),
                            reads=[sb_, ("srw", vc // 512)], writes=[("srw", vc // 512)])
                        mbb = psk[4 + u]
                        tk.op("T", lambda u=u, mc=mc, mbb=mbb: T.matmul(
                            PS[mbb[1]][:, 0:512], onrf[:, :], fsrow[:, mc:mc + 512],
                            start=True, stop=True),
                            reads=[("srw", mc // 512), "onrf"], writes=[mbb])
                    for u in range(NU):
                        vc = VCOL[u]
                        tk.op("S", lambda u=u, vc=vc: SC.activation(
                            fsrow[:, vc:vc + 512], fsrow[:, vc:vc + 512],
                            AF.Ln, bias=epsc[0:1, 0:1]),
                            reads=[("srw", vc // 512), "epsc"], writes=[("srw", vc // 512)])
                        tk.op("S", lambda u=u, vc=vc: SC.activation(
                            fsrow[:, vc:vc + 512], fsrow[:, vc:vc + 512],
                            AF.Exp, scale=-0.5),
                            reads=[("srw", vc // 512)], writes=[("srw", vc // 512)])
                    for u in range(NU):
                        vc = VCOL[u]
                        sbb = psk[0 + u]
                        tk.op("T", lambda u=u, vc=vc, sbb=sbb: T.matmul(
                            PS[sbb[1]][:, 0:512], onrf[:, :], fsrow[:, vc:vc + 512],
                            start=True, stop=True),
                            reads=[("srw", vc // 512), "onrf"], writes=[sbb])
                    for u in range(NU):
                        mbb = psk[4 + u]
                        sbb = psk[0 + u]
                        for dt in range(DT):
                            tk.op("V", lambda u=u, dt=dt, mbb=mbb: V.tensor_tensor(
                                tbuf[:, 0:512], shsl(u, dt), PS[mbb[1]][:, 0:512],
                                ALU.subtract),
                                reads=[("ybf", u, dt), mbb], writes=[("tb", 0)])
                            tk.op("V", lambda sbb=sbb: V.tensor_tensor(
                                tbuf[:, 512:1024], tbuf[:, 0:512], PS[sbb[1]][:, 0:512],
                                ALU.mult),
                                reads=[("tb", 0), sbb], writes=[("tb", 1)])
                            tk.op("V", lambda dt=dt, u=u: V.tensor_scalar(
                                xnbf[:, dt * HT + u * 512: dt * HT + u * 512 + 512],
                                tbuf[:, 512:1024],
                                prm[:, PC_G + dt:PC_G + dt + 1],
                                prm[:, PC_B + dt:PC_B + dt + 1],
                                ALU.mult, ALU.add),
                                reads=[("tb", 1), "prm"], writes=[("xnbf", dt)])

                    def w1_load(og, bi):
                        sname = "dW1a" if bi == 0 else "dW1b"
                        c0a, c0g = og * P, FH + og * P
                        n = min(P, FH - og * P)
                        tk.dma("G", sname, lambda: GP.dma_start(
                            w1sl[bi][:, :].rearrange("p (k c) -> p k c", k=DT)[:, :, 0:n],
                            W1[l, :, c0a:c0a + n].rearrange("(k p) c -> p k c", p=P)),
                            writes=[("w1a", bi)])
                        tk.dma("G", sname, lambda: GP.dma_start(
                            w1sl[bi + 2][:, :].rearrange("p (k c) -> p k c", k=DT)[:, :, 0:n],
                            W1[l, :, c0g:c0g + n].rearrange("(k p) c -> p k c", p=P)),
                            writes=[("w1g", bi)])

                    w1_load(0, 0)
                    if NOG > 1:
                        w1_load(1, 1)
                    for og in range(NOG):
                        n = min(P, FH - og * P)
                        bi = og % 2
                        rot6 = (og * 2 * NU) % 6
                        pa = [psk[(rot6 + j) % 6] for j in range(NU)]
                        pg = [psk[(rot6 + NU + j) % 6] for j in range(NU)]
                        wa = w1sl[bi][:, :].rearrange("p (k c) -> p k c", k=DT)
                        wg = w1sl[bi + 2][:, :].rearrange("p (k c) -> p k c", k=DT)
                        for u in range(NU):
                            for dt in range(DT):
                                tk.op("T", lambda u=u, dt=dt, n=n, wa=wa: T.matmul(
                                    PS[pa[u][1]][0:n, 0:512], wa[:, dt, 0:n],
                                    xnbf[:, dt * HT + u * 512: dt * HT + (u + 1) * 512],
                                    start=(dt == 0), stop=(dt == DT - 1)),
                                    reads=[("w1a", bi), ("xnbf", dt)], writes=[pa[u]])
                        for u in range(NU):
                            for dt in range(DT):
                                tk.op("T", lambda u=u, dt=dt, n=n, wg=wg: T.matmul(
                                    PS[pg[u][1]][0:n, 0:512], wg[:, dt, 0:n],
                                    xnbf[:, dt * HT + u * 512: dt * HT + (u + 1) * 512],
                                    start=(dt == 0), stop=(dt == DT - 1)),
                                    reads=[("w1g", bi), ("xnbf", dt)], writes=[pg[u]])
                        if og + 2 < NOG:
                            w1_load(og + 2, bi)
                        for u in range(NU):
                            tk.op("S", lambda u=u, og=og, n=n: SC.activation(
                                tbuf[0:n, u * 512:(u + 1) * 512],
                                PS[pg[u][1]][0:n, 0:512], AF.Gelu,
                                bias=prm[0:n, PC_B1G + og:PC_B1G + og + 1]),
                                reads=[pg[u], "prm"], writes=[("tb", u)])
                            tk.op("V", lambda u=u, og=og, n=n: V.scalar_tensor_tensor(
                                hbuf[0:n, og * HT + u * 512: og * HT + (u + 1) * 512],
                                PS[pa[u][1]][0:n, 0:512],
                                prm[0:n, PC_B1A + og:PC_B1A + og + 1],
                                tbuf[0:n, u * 512:(u + 1) * 512], ALU.add, ALU.mult),
                                reads=[pa[u], ("tb", u), "prm"], writes=[("h", og)])

                    if hb == 0:
                        for dt in range(4, DT):
                            tk.op("V", lambda dt=dt: V.tensor_copy(
                                bndry[:, dt:dt + 1], ysl(dt, HT - 1, 1)),
                                reads=[("ytr", dt)], writes=[("bnd", dt)])

                    for do in range(DT):
                        w2p = do % 2
                        tk.dma("G", "dW2a" if w2p == 0 else "dW2b",
                               lambda do=do, w2p=w2p: GP.dma_start(
                            w2sl[w2p][:, 0:(NOG - 1) * P]
                            .rearrange("p (g c) -> p g c", g=NOG - 1),
                            W2[l, 0:(NOG - 1) * P, do * P:(do + 1) * P]
                            .rearrange("(g p) c -> p g c", p=P)),
                            writes=[("w2sl", w2p)])
                        tk.dma("G", "dW2a" if w2p == 0 else "dW2b",
                               lambda do=do, w2p=w2p: GP.dma_start(
                            w2sl[w2p][0:FH - (NOG - 1) * P, (NOG - 1) * P:NOG * P],
                            W2[l, (NOG - 1) * P:FH, do * P:(do + 1) * P]),
                            writes=[("w2sl", w2p)])
                        pu = [psk[(do % 2) * 2], psk[(do % 2) * 2 + 1]][:NU]
                        for u in range(NU):
                            for og in range(NOG):
                                n = min(P, FH - og * P)
                                tk.op("T", lambda u=u, og=og, do=do, n=n, w2p=w2p: T.matmul(
                                    PS[pu[u][1]][:, 0:512],
                                    w2sl[w2p][0:n, og * P:og * P + P],
                                    hbuf[0:n, og * HT + u * 512: og * HT + (u + 1) * 512],
                                    start=(og == 0), stop=(og == NOG - 1)),
                                    reads=[("w2sl", w2p), ("h", og)], writes=[pu[u]])
                        for u in range(NU):
                            tk.op("V", lambda u=u, do=do, T0=T0: V.scalar_tensor_tensor(
                                ysl(do, T0 + u * 512, 512), PS[pu[u][1]][:, 0:512],
                                prm[:, PC_B2 + do:PC_B2 + do + 1],
                                ysl(do, T0 + u * 512, 512), ALU.add, ALU.add),
                                reads=[pu[u], "prm", ("ytr", do)], writes=[("ytr", do)])
                tk.barrier()

        def b_attn(l, b):
            with ExitStack() as ls:
                def lsb(n, sh, dt_):
                    return ls.enter_context(nc.sbuf_tensor(f"{n}_l{l}b{b}", sh, dt_))
                aybf = lsb("aybf", [P, DT * 513], BF16)
                qfin = lsb("aqf", [P, DT * 768], BF16)
                kfin = lsb("akf", [P, DT * 768], BF16)
                vrol = lsb("avr", [P, 8 * 1040], BF16)
                tk.op("V", lambda: V.memset(
                    vrol[:, :].rearrange("p (s h c) -> p (s h) c", s=8, h=16)[:, :, 64:65],
                    1.0), writes=[("vr", s) for s in range(8)] if False else ["vrones"])
                ptile = lsb("apt", [P, 4 * 2 * 512], BF16)
                sbf2 = lsb("asbf2", [P, 1536], BF16)
                onat = sbf2[:, 0:1024]
                tt1b_ = lsb("att1", [P, 1024], F32)
                rnbb_ = lsb("arnb", [P, 512], F32)
                msb = lsb("amsb", [P, 2 * 512], F32)
                rinv = lsb("arinv", [P, 16], F32)
                opas = lsb("aopas", [P, DT * 512], BF16)
                W = {
                    "qw": lsb("wqw", [P, 2 * D], BF16),
                    "kwb": lsb("wkwb", [P, 2 * D], BF16),
                    "vw": lsb("wvw", [P, DT * D], BF16),
                    "wos": lsb("wwos", [P, 2 * D], BF16),
                }
                tabs = lsb("atabs", [P, 2 * 512], F32)
                masks = lsb("amasks", [P, 2 * 512], BF16)
                tk.dma("Y", "dW", lambda: SY.dma_start(masks[:, 0:512], MSK[0]),
                       writes=["msk"])
                tk.dma("Y", "dW", lambda: SY.dma_start(masks[:, 512:1024], MSK[1]),
                       writes=["msk"])
                opart = lsb("aopart", [P, 2 * 2 * 1040], F32)

                for dt in range(DT):
                    tk.dma("G", "dQWa" if dt % 2 == 0 else "dQWb", lambda dt=dt: GP.dma_start(
                        W["vw"][:, dt * D:(dt + 1) * D],
                        WQKV[l, dt * P:(dt + 1) * P, 2 * D:3 * D]),
                        writes=[("vw", dt)])

                tk.op("V", lambda: V.reduce_max(srow[:, 0:1], qkr[:, 0:DH], AX.X,
                                                apply_absolute_value=True),
                      reads=["qkr"], writes=[("srow", 0)])
                tk.op("V", lambda: V.reduce_max(srow[:, 1:2], qkr[:, DH:2 * DH], AX.X,
                                                apply_absolute_value=True),
                      reads=["qkr"], writes=[("srow", 0)])
                tk.op("V", lambda: V.tensor_tensor(srow[:, 2:3], srow[:, 0:1],
                                                   srow[:, 1:2], ALU.mult),
                      reads=[("srow", 0)], writes=[("srow", 0)])
                tk.op("T", lambda: T.matmul(PS[0][:, 0:1], onrf[:, :], srow[:, 2:3],
                                            start=True, stop=True),
                      reads=[("srow", 0), "onrf"], writes=[psk[0]])
                tk.op("V", lambda: V.tensor_scalar(negb[:, :], PS[0][:, 0:1],
                                                   -16.0, None, ALU.mult),
                      reads=[psk[0]], writes=["negb"])

                tk.op("V", lambda: V.tensor_copy(qscol[:, 0:1], prm[:, PC_QS:PC_QS + 1]),
                      reads=["prm"], writes=["qsc"])
                tk.op("V", lambda: V.tensor_copy(qscol[:, 1:2], prm[:, PC_KS:PC_KS + 1]),
                      reads=["prm"], writes=["qsc"])
                tk.op("V", lambda: V.tensor_tensor(qscol[:, 2:3], qscol[:, 0:1],
                                                   qscol[:, 0:1], ALU.mult),
                      reads=["qsc"], writes=["qsc"])
                tk.op("V", lambda: V.tensor_tensor(qscol[:, 3:4], qscol[:, 1:2],
                                                   qscol[:, 1:2], ALU.mult),
                      reads=["qsc"], writes=["qsc"])
                tk.op("V", lambda: V.reciprocal(qscol[:, 2:3], qscol[:, 2:3]),
                      reads=["qsc"], writes=["qsc"])
                tk.op("V", lambda: V.reciprocal(qscol[:, 3:4], qscol[:, 3:4]),
                      reads=["qsc"], writes=["qsc"])
                tk.op("V", lambda: V.tensor_scalar(blkq[:, 0:P], blk64[:, :],
                                                   qscol[:, 2:3], None, ALU.mult),
                      reads=["qsc", "blk"], writes=[("blkq", 0)])
                tk.op("V", lambda: V.tensor_scalar(blkq[:, P:2 * P], blk64[:, :],
                                                   qscol[:, 3:4], None, ALU.mult),
                      reads=["qsc", "blk"], writes=[("blkq", 1)])

                def qf_read(side_buf, dt, t0, n):
                    s0 = (t0 // 256) % 3
                    s1 = (s0 + 1) % 3
                    if n <= 256 or s1 == s0 + 1:
                        return [(side_buf[:, dt * 768 + s0 * 256:
                                          dt * 768 + s0 * 256 + n], 0, n)]
                    return [(side_buf[:, dt * 768 + s0 * 256:
                                      dt * 768 + s0 * 256 + 256], 0, 256),
                            (side_buf[:, dt * 768 + s1 * 256:
                                      dt * 768 + s1 * 256 + (n - 256)], 256, n - 256)]

                for c in range(C + 1):
                    if ATTN_STAGE < 2:
                        continue
                    if c < C:
                        t0c = c * 512

                        def absl(dt, j0, n):
                            off = 0 if dt >= 4 else 1
                            return aybf[:, dt * 513 + off + j0: dt * 513 + off + j0 + n]

                        for dt in range(DT):
                            if t0c == 0:
                                tk.op("V", lambda dt=dt: V.memset(
                                    aybf[:, dt * 513:dt * 513 + 1], 0.0),
                                    writes=[("aybf", dt)])
                            else:
                                tk.op("V", lambda dt=dt: V.tensor_copy(
                                    aybf[:, dt * 513:dt * 513 + 1], ysl(dt, t0c - 1, 1)),
                                    reads=[("ytr", dt)], writes=[("aybf", dt)])
                            tk.op("V", lambda dt=dt: V.tensor_copy(
                                aybf[:, dt * 513 + 1:(dt + 1) * 513], ysl(dt, t0c, 512)),
                                reads=[("ytr", dt)], writes=[("aybf", dt)])

                        for tt in range(4):
                            tslot = ((t0c + tt * P) // P) % 8
                            for half in range(2):
                                bank = psk[4 + (2 * tt + half) % 2]
                                for dt in range(DT):
                                    tk.op("T", lambda dt=dt, tt=tt, half=half, bank=bank:
                                          T.matmul(
                                              PS[bank[1]][:, 0:512],
                                              absl(dt, tt * P, P),
                                              W["vw"][:, dt * D + half * 512:
                                                 dt * D + (half + 1) * 512],
                                              start=(dt == 0), stop=(dt == DT - 1)),
                                          reads=[("aybf", dt), ("vw", dt)],
                                          writes=[bank])
                                tk.op("V", lambda half=half, bank=bank, tslot=tslot:
                                      V.tensor_copy(
                                          vrol[:, tslot * 1040 + half * 520:
                                               tslot * 1040 + (half + 1) * 520]
                                          .rearrange("p (h c) -> p h c", h=8)[:, :, 0:64],
                                          PS[bank[1]][:, 0:512]
                                          .rearrange("p (h c) -> p h c", h=8)),
                                      reads=[bank], writes=[("vr", tslot)])

                        for side in range(2 if ATTN_STAGE >= 3 else 0):
                            tbase = 0 if side == 0 else 2
                            for i in range(2):
                                tk.dma("Y", "dTa" if i == 0 else "dTb",
                                       lambda i=i, tbase=tbase: SY.dma_start(
                                           tabs[:, i * 512:(i + 1) * 512],
                                           ROT[tbase + i][:, t0c:t0c + 512]),
                                       writes=["tabs"])
                            sc_col = qscol[:, side:side + 1]
                            dst = qfin if side == 0 else kfin
                            dk = "qf" if side == 0 else "kf"
                            def qk_slab_load(m):
                                kwp = m % 2
                                if side == 1:
                                    tk.dma("G", "dKW0" if kwp == 0 else "dKW1",
                                           lambda m=m, kwp=kwp: GP.dma_start(
                                               W["kwb"][:, kwp * D:(kwp + 1) * D]
                                               .rearrange("p (k c) -> p k c", k=DT),
                                               WQKV[l, :, D + m * P:D + (m + 1) * P]
                                               .rearrange("(k p) c -> p k c", p=P)),
                                           writes=[("kwb", kwp)])
                                else:
                                    tk.dma("G", "dQWa" if kwp == 0 else "dQWb",
                                           lambda m=m, kwp=kwp: GP.dma_start(
                                               W["qw"][:, kwp * D:(kwp + 1) * D]
                                               .rearrange("p (k c) -> p k c", k=DT),
                                               WQKV[l, :, m * P:(m + 1) * P]
                                               .rearrange("(k p) c -> p k c", p=P)),
                                           writes=[("qw", kwp)])

                            if ATTN_SUB >= 2:
                                qk_slab_load(0)
                            for m in range(DT if ATTN_SUB >= 2 else 0):
                                kwp = m % 2
                                if m + 1 < DT:
                                    qk_slab_load(m + 1)
                                bank = psk[m % 3]
                                for dt in range(DT if ATTN_SUB >= 3 else 0):
                                    if side == 0:
                                        lhs = W["qw"][:, kwp * D + dt * P: kwp * D + (dt + 1) * P]
                                        rk = ("qw", kwp)
                                    else:
                                        lhs = W["kwb"][:, kwp * D + dt * P: kwp * D + (dt + 1) * P]
                                        rk = ("kwb", kwp)
                                    tk.op("T", lambda lhs=lhs, dt=dt, bank=bank: T.matmul(
                                        PS[bank[1]][:, 0:512], lhs, absl(dt, 0, 512),
                                        start=(dt == 0), stop=(dt == DT - 1)),
                                        reads=[rk, ("aybf", dt)], writes=[bank])
                                if ATTN_SUB < 4:
                                    continue
                                par = m % 2
                                ubf = sbf2[:, par * 512:(par + 1) * 512]
                                sqb = sbf2[:, 1024:1536]
                                u2 = msb[:, (1 - par) * 512:(2 - par) * 512]
                                tt1 = tt1b_[:, par * 512:(par + 1) * 512]
                                rnb = rnbb_[:, 0:512]
                                rnscr = msb[:, par * 512:(par + 1) * 512]
                                tk.op("S", lambda ubf=ubf, bank=bank, sc_col=sc_col:
                                      SC.activation(
                                          ubf, PS[bank[1]][:, 0:512], AF.Copy,
                                          scale=sc_col),
                                      reads=[bank, "qsc"], writes=[("ubf", par)])
                                if ATTN_SUB < 5:
                                    continue
                                rotb = psk[3 + m % 2]
                                tk.op("T", lambda rotb=rotb, ubf=ubf: T.matmul(
                                    PS[rotb[1]][:, 0:512], rotr[:, :],
                                    ubf, start=True, stop=True),
                                      reads=["rotr", ("ubf", par)], writes=[rotb])
                                tk.op("G", lambda sqb=sqb, ubf=ubf: GP.tensor_tensor(
                                    sqb, ubf, ubf, ALU.mult),
                                    reads=[("ubf", par)], writes=["sqb2"])
                                tk.op("T", lambda side=side, sqb=sqb: T.matmul(
                                    PS[5][:, 0:512], blkq[:, side * P:(side + 1) * P],
                                    sqb, start=True, stop=True),
                                    reads=[("blkq", side), "sqb2"], writes=[psk[5]])
                                tk.op("S", lambda rnscr=rnscr: SC.activation(
                                    rnscr, PS[5][:, 0:512], AF.Ln),
                                      reads=[psk[5]], writes=[("msb", par)])
                                tk.op("S", lambda rnb=rnb, rnscr=rnscr: SC.activation(
                                    rnb, rnscr, AF.Exp, scale=-0.5),
                                      reads=[("msb", par)], writes=["rnb"])
                                tk.op("V", lambda tt1=tt1, bank=bank, sc_col=sc_col:
                                      V.scalar_tensor_tensor(
                                          tt1, PS[bank[1]][:, 0:512], sc_col,
                                          tabs[:, 0:512], ALU.mult, ALU.mult),
                                      reads=[bank, "qsc", "tabs"], writes=[("tt1", par)])
                                tk.op("V", lambda u2=u2, rotb=rotb: V.tensor_tensor(
                                    u2, PS[rotb[1]][:, 0:512],
                                    tabs[:, 512:1024], ALU.mult),
                                    reads=[rotb, "tabs"], writes=[("msb", 1 - par)])
                                tk.op("G", lambda tt1=tt1, u2=u2: GP.tensor_tensor(
                                    tt1, tt1, u2, ALU.add),
                                    reads=[("tt1", par), ("msb", 1 - par)],
                                    writes=[("tt1", par)])
                                sa = (t0c // 256) % 3
                                sbt = (sa + 1) % 3
                                if sbt == sa + 1:
                                    tk.op("V", lambda dst=dst, m=m, sa=sa, tt1=tt1,
                                          rnb=rnb: V.tensor_tensor(
                                        dst[:, m * 768 + sa * 256: m * 768 + sa * 256 + 512],
                                        tt1, rnb, ALU.mult),
                                        reads=[("tt1", par), "rnb"],
                                        writes=[(dk, m, sa), (dk, m, sbt)])
                                else:
                                    tk.op("V", lambda dst=dst, m=m, sa=sa, tt1=tt1,
                                          rnb=rnb: V.tensor_tensor(
                                        dst[:, m * 768 + sa * 256: m * 768 + sa * 256 + 256],
                                        tt1[:, 0:256], rnb[:, 0:256], ALU.mult),
                                        reads=[("tt1", par), "rnb"],
                                        writes=[(dk, m, sa)])
                                    tk.op("V", lambda dst=dst, m=m, sbt=sbt, tt1=tt1,
                                          rnb=rnb: V.tensor_tensor(
                                        dst[:, m * 768 + sbt * 256:
                                            m * 768 + sbt * 256 + 256],
                                        tt1[:, 256:512], rnb[:, 256:512], ALU.mult),
                                        reads=[("tt1", par), "rnb"],
                                        writes=[(dk, m, sbt)])

                    kws = []
                    if 2 * c - 1 >= 0 and c <= C:
                        kws.append(2 * c - 1)
                    if c < C:
                        kws.append(2 * c)
                    for kw in (kws if ATTN_STAGE >= 4 else []):
                        q0 = 256 * kw
                        qn = min(512, S - q0)
                        for hg in range(4):
                            for hh in range(4):
                                h = hg * 4 + hh
                                m = h // 2
                                rbase = (h % 2) * DH
                                for js in range(2):
                                    jt0 = 256 * kw + js * P
                                    ksl = (jt0 // 256) % 3
                                    koff = jt0 % 256
                                    lhs = kfin[rbase:rbase + DH,
                                               m * 768 + ksl * 256 + koff:
                                               m * 768 + ksl * 256 + koff + P]
                                    bank = psk[(hh + js) % 2]
                                    for (ap_, rel, ln_) in qf_read(qfin, m, q0, qn):
                                        tk.op("T", lambda lhs=lhs, ap_=ap_, rel=rel,
                                              ln_=ln_, bank=bank, rbase=rbase: T.matmul(
                                                  PS[bank[1]][:, rel:rel + ln_],
                                                  lhs, ap_[rbase:rbase + DH, :],
                                                  start=True, stop=True),
                                              reads=[("kf", m, ksl),
                                                     ("qf", m, (q0 // 256) % 3),
                                                     ("qf", m, (q0 // 256 + 1) % 3)],
                                              writes=[bank])
                                    mpar = (hh + js) % 2
                                    tk.op("V", lambda bank=bank, js=js, qn=qn, mpar=mpar:
                                          V.tensor_tensor(
                                              msb[:, mpar * 512: mpar * 512 + qn],
                                              PS[bank[1]][:, 0:qn],
                                              masks[:, js * 512: js * 512 + qn], ALU.add),
                                          reads=[bank, "msk"], writes=[("msb", mpar)])
                                    tk.op("S", lambda hh=hh, js=js, qn=qn, mpar=mpar:
                                          SC.activation(
                                              ptile[:, (hh * 2 + js) * 512:
                                                    (hh * 2 + js) * 512 + qn],
                                              msb[:, mpar * 512: mpar * 512 + qn],
                                              AF.Exp, scale=8.0, bias=negb[:, 0:1]),
                                          reads=[("msb", mpar), "negb"],
                                          writes=[("pt", hh, js)])
                            for w in ((kw, kw + 1) if ATTN_STAGE >= 5 else ()):
                                if w >= NW:
                                    continue
                                wsl = w % 2
                                first = (kw == max(w - 1, 0))
                                for sg in range(2):
                                    st0 = w * 256 + sg * P
                                    if st0 >= S:
                                        continue
                                    coff = st0 - 256 * kw
                                    ob = psk[2 + ((w + sg) % 2) * 0 + sg % 1] if False                                         else psk[2 + (sg % 2)]
                                    rsb = psk[4 + (sg % 2)]
                                    for hh in range(4):
                                        h = hg * 4 + hh
                                        for js in range(2):
                                            jt0 = 256 * kw + js * P
                                            vsl = (jt0 // P) % 8
                                            lhs = ptile[:, (hh * 2 + js) * 512 + coff:
                                                        (hh * 2 + js) * 512 + coff + P]
                                            tk.op("T", lambda lhs=lhs, h=h, vsl=vsl,
                                                  ob=ob, js=js, hh=hh: T.matmul(
                                                      PS[ob[1]][:, hh * 65:hh * 65 + 65],
                                                      lhs,
                                                      vrol[:, vsl * 1040 + h * 65:
                                                           vsl * 1040 + h * 65 + 65],
                                                      start=(js == 0), stop=(js == 1)),
                                                  reads=[("pt", hh, js), ("vr", vsl),
                                                         "vrones"],
                                                  writes=[ob])
                                    oc = wsl * 2080 + sg * 1040 + hg * 260
                                    if first:
                                        tk.op("V", lambda ob=ob, oc=oc: V.tensor_copy(
                                            opart[:, oc:oc + 260], PS[ob[1]][:, 0:260]),
                                            reads=[ob], writes=[("op", wsl, sg, hg)])
                                    else:
                                        tk.op("V", lambda ob=ob, oc=oc: V.tensor_tensor(
                                            opart[:, oc:oc + 260], PS[ob[1]][:, 0:260],
                                            opart[:, oc:oc + 260], ALU.add),
                                            reads=[ob, ("op", wsl, sg, hg)],
                                            writes=[("op", wsl, sg, hg)])

                        if ATTN_STAGE < 6:
                            continue
                        w = kw
                        wsl = w % 2
                        pbase = max(0, 512 * c - 256)
                        for sg in range(2):
                            st0 = w * 256 + sg * P
                            if st0 >= S:
                                continue
                            oc = wsl * 2080 + sg * 1040
                            tk.op("V", lambda oc=oc: V.reciprocal(
                                rinv[:, 0:16],
                                opart[:, oc:oc + 1040]
                                .rearrange("p (g d) -> p g d", g=16)[:, :, 64]),
                                reads=[("op", wsl, sg, 0), ("op", wsl, sg, 1),
                                       ("op", wsl, sg, 2), ("op", wsl, sg, 3)],
                                writes=["rinv"])
                            rexp = rinv[:, 0:16].unsqueeze(2).broadcast_to([P, 16, DH])
                            tk.op("V", lambda oc=oc, rexp=rexp: V.tensor_tensor(
                                onat[:, :].rearrange("p (g d) -> p g d", g=16),
                                opart[:, oc:oc + 1040]
                                .rearrange("p (g d) -> p g d", g=16)[:, :, 0:64],
                                rexp, ALU.mult),
                                reads=[("op", wsl, sg, 0), ("op", wsl, sg, 1),
                                       ("op", wsl, sg, 2), ("op", wsl, sg, 3), "rinv"],
                                writes=[("ubf", 0), ("ubf", 1)])
                            for half4 in range(2):
                                pbb = pbk[half4]
                                for j4 in range(4):
                                    it = half4 * 4 + j4
                                    tk.op("T", lambda it=it, j4=j4, pbb=pbb: T.transpose(
                                        PB[pbb[1]][:, j4 * P:(j4 + 1) * P],
                                        onat[:, it * P:(it + 1) * P], idb[:, :]),
                                        reads=[("ubf", 0), ("ubf", 1), "idb"],
                                        writes=[pbb])
                                off = st0 - pbase
                                dst = opas[:, :].rearrange(
                                    "p (it c) -> p it c", c=512)[:, half4 * 4:half4 * 4 + 4,
                                                                 off:off + P]
                                tk.op("V", lambda pbb=pbb, dst=dst: V.tensor_copy(
                                    dst, PB[pbb[1]][:, 0:512].rearrange(
                                        "p (it c) -> p it c", c=P)),
                                    reads=[pbb],
                                    writes=[("opas", half4 * 4 + j) for j in range(4)])

                    if kws and ATTN_STAGE >= 7:
                        pbase = max(0, 512 * c - 256)
                        pend = min(S, 512 * c + 256)
                        wlen = pend - pbase
                        def wo_load(do):
                            wop = do % 2
                            tk.dma("G", "dWO0" if wop == 0 else "dWO1",
                                   lambda do=do, wop=wop: GP.dma_start(
                                       W["wos"][:, wop * D:(wop + 1) * D]
                                       .rearrange("p (k c) -> p k c", k=DT),
                                       WO[l, :, do * P:(do + 1) * P]
                                       .rearrange("(k p) c -> p k c", p=P)),
                                   writes=[("wos", wop)])

                        wo_load(0)
                        wo_load(1)
                        for do in range(DT):
                            wop = do % 2
                            bank = psk[do % 2]
                            for it in range(DT):
                                tk.op("T", lambda do=do, it=it, wop=wop, bank=bank,
                                      wlen=wlen: T.matmul(
                                          PS[bank[1]][:, 0:wlen],
                                          W["wos"][:, wop * D + it * P: wop * D + (it + 1) * P],
                                          opas[:, it * 512: it * 512 + wlen],
                                          start=(it == 0), stop=(it == DT - 1)),
                                      reads=[("wos", wop), ("opas", it)],
                                      writes=[bank])
                            tk.op("V", lambda do=do, bank=bank, pbase=pbase, wlen=wlen:
                                  V.tensor_tensor(
                                      ysl(do, pbase, wlen), PS[bank[1]][:, 0:wlen],
                                      ysl(do, pbase, wlen), ALU.add),
                                  reads=[bank, ("ytr", do)], writes=[("ytr", do)])
                            if do + 2 < DT:
                                wo_load(do + 2)
                tk.barrier()

        def phase_A(b):
            with ExitStack() as ls:
                def lsb(n, sh, dt_):
                    return ls.enter_context(nc.sbuf_tensor(f"{n}_A{b}", sh, dt_))
                ktr = lsb("ktr", [P, DT * S], BF16)
                vnat = lsb("vnat", [P, NT * D], BF16)
                xnbf = lsb("xnbfA", [P, DT * 512], BF16)
                qtr = lsb("qtrA", [P, DT * 512], BF16)
                kwA = lsb("kwA", [P, 2 * D], BF16)
                xt = [lsb(f"xtA{i}", [P, D], F32) for i in range(2)]
                ast_ = lsb("astA", [P, 8], F32)
                pchk = lsb("pchkA", [P, C * 512], BF16)
                ptr = lsb("ptrA", [P, 2 * P], BF16)
                rsA = lsb("rsA", [P, C + 2], F32)
                obuf = [lsb(f"obufA{i}", [P, D], F32) for i in range(2)]
                tmpo = lsb("tmpoA", [P, D], F32)
                vwA = lsb("vwA", [P, DT * D], BF16)

                for tau in range(NT):
                    xb = xt[tau % 2]
                    sem = "dXa" if tau % 2 == 0 else "dXb"
                    tk.dma("Y", sem, lambda tau=tau, xb=xb: SY.dma_start(
                        xb[:, :], X[b, tau * P:(tau + 1) * P, :]),
                        writes=[("xt", tau % 2)])
                    tk.op("V", lambda xb=xb: V.tensor_reduce(
                        ast_[:, 0:1], xb[:, :], AX.X, ALU.add),
                        reads=[("xt", tau % 2)], writes=["astA"])
                    tk.op("S", lambda xb=xb: SC.activation(tmpo[:, :], xb[:, :], AF.Square),
                          reads=[("xt", tau % 2)], writes=["tmpoA"])
                    tk.op("V", lambda: V.tensor_reduce(
                        ast_[:, 1:2], tmpo[:, :], AX.X, ALU.add),
                        reads=["tmpoA"], writes=["astA"])
                    tk.op("V", lambda: V.tensor_scalar(
                        ast_[:, 2:3], ast_[:, 0:1], 1.0 / D, None, ALU.mult),
                        reads=["astA"], writes=["astA"])
                    tk.op("V", lambda: V.tensor_tensor(
                        ast_[:, 3:4], ast_[:, 2:3], ast_[:, 2:3], ALU.mult),
                        reads=["astA"], writes=["astA"])
                    tk.op("V", lambda: V.tensor_scalar(
                        ast_[:, 4:5], ast_[:, 1:2], 1.0 / D, EPS, ALU.mult, ALU.add),
                        reads=["astA"], writes=["astA"])
                    tk.op("V", lambda: V.tensor_tensor(
                        ast_[:, 5:6], ast_[:, 4:5], ast_[:, 3:4], ALU.subtract),
                        reads=["astA"], writes=["astA"])
                    tk.op("S", lambda: SC.activation(ast_[:, 6:7], ast_[:, 5:6], AF.Ln),
                          reads=["astA"], writes=["astB"])
                    tk.op("S", lambda: SC.activation(ast_[:, 7:8], ast_[:, 6:7],
                                                     AF.Exp, scale=-0.5),
                          reads=["astB"], writes=["astB"])
                    tk.op("V", lambda xb=xb: V.tensor_scalar(
                        obuf[0][:, :], xb[:, :], ast_[:, 2:3], ast_[:, 7:8],
                        ALU.subtract, ALU.mult),
                        reads=[("xt", tau % 2), "astA", "astB"], writes=[("obA", 0)])
                    for dt in range(DT):
                        bank = psk[4 + dt % 2]
                        tk.op("T", lambda dt=dt, bank=bank: T.transpose(
                            PS[bank[1]][:, 0:P], obuf[0][:, dt * P:(dt + 1) * P],
                            idf[:, :]),
                            reads=[("obA", 0), "idf"], writes=[bank])
                        tk.op("V", lambda dt=dt, bank=bank, tau=tau: V.tensor_copy(
                            ysl(dt, tau * P, P), PS[bank[1]][:, 0:P]),
                            reads=[bank], writes=[("ytr", dt)])

                for cc in range(C):
                    t0c = cc * 512
                    for dt in range(DT):
                        tk.op("V", lambda dt=dt, t0c=t0c: V.tensor_copy(
                            xnbf[:, dt * 512:(dt + 1) * 512], ysl(dt, t0c, 512)),
                            reads=[("ytr", dt)], writes=[("xnbfA", dt)])
                    for m in range(DT):
                        kwp = m % 2
                        tk.dma("G", "dKW0" if kwp == 0 else "dKW1",
                               lambda m=m, kwp=kwp: GP.dma_start(
                                   kwA[:, kwp * D:(kwp + 1) * D]
                                   .rearrange("p (k c) -> p k c", k=DT),
                                   Wka[:, m * P:(m + 1) * P]
                                   .rearrange("(k p) c -> p k c", p=P)),
                               writes=[("kwA", kwp)])
                        bank = psk[m % 2]
                        for dt in range(DT):
                            tk.op("T", lambda m=m, dt=dt, kwp=kwp, bank=bank: T.matmul(
                                PS[bank[1]][:, 0:512],
                                kwA[:, kwp * D + dt * P: kwp * D + (dt + 1) * P],
                                xnbf[:, dt * 512:(dt + 1) * 512],
                                start=(dt == 0), stop=(dt == DT - 1)),
                                reads=[("kwA", kwp), ("xnbfA", dt)], writes=[bank])
                        tk.op("V", lambda m=m, bank=bank, t0c=t0c: V.tensor_scalar(
                            ktr[:, m * S + t0c: m * S + t0c + 512],
                            PS[bank[1]][:, 0:512], abias[:, DT + m:DT + m + 1],
                            None, ALU.add),
                            reads=[bank, "abias"], writes=[("ktrA", m)])
                    if cc == 0:
                        for dt in range(DT):
                            tk.dma("G", "dQWa" if dt % 2 == 0 else "dQWb",
                                   lambda dt=dt: GP.dma_start(
                                       vwA[:, dt * D:(dt + 1) * D],
                                       Wva[dt * P:(dt + 1) * P, :]),
                                   writes=[("vwA", dt)])
                    for tt in range(4):
                        kt = cc * 4 + tt
                        for half in range(2):
                            bank = psk[2 + (2 * tt + half) % 2]
                            for dt in range(DT):
                                tk.op("T", lambda dt=dt, tt=tt, half=half, bank=bank:
                                      T.matmul(
                                          PS[bank[1]][:, 0:512],
                                          xnbf[:, dt * 512 + tt * P:
                                               dt * 512 + (tt + 1) * P],
                                          vwA[:, dt * D + half * 512:
                                              dt * D + (half + 1) * 512],
                                          start=(dt == 0), stop=(dt == DT - 1)),
                                      reads=[("xnbfA", dt), ("vwA", dt)], writes=[bank])
                            tk.op("V", lambda kt=kt, half=half, bank=bank: V.tensor_copy(
                                vnat[:, kt * D + half * 512: kt * D + (half + 1) * 512],
                                PS[bank[1]][:, 0:512]),
                                reads=[bank], writes=[("vnatA", kt)])

                for qc in range(C):
                    t0c = qc * 512
                    for dt in range(DT):
                        tk.op("V", lambda dt=dt, t0c=t0c: V.tensor_copy(
                            xnbf[:, dt * 512:(dt + 1) * 512], ysl(dt, t0c, 512)),
                            reads=[("ytr", dt)], writes=[("xnbfA", dt)])
                    for m in range(DT):
                        kwp = m % 2
                        tk.dma("G", "dKW0" if kwp == 0 else "dKW1",
                               lambda m=m, kwp=kwp: GP.dma_start(
                                   kwA[:, kwp * D:(kwp + 1) * D]
                                   .rearrange("p (k c) -> p k c", k=DT),
                                   Wqa[:, m * P:(m + 1) * P]
                                   .rearrange("(k p) c -> p k c", p=P)),
                               writes=[("kwA", kwp)])
                        bank = psk[m % 2]
                        for dt in range(DT):
                            tk.op("T", lambda m=m, dt=dt, kwp=kwp, bank=bank: T.matmul(
                                PS[bank[1]][:, 0:512],
                                kwA[:, kwp * D + dt * P: kwp * D + (dt + 1) * P],
                                xnbf[:, dt * 512:(dt + 1) * 512],
                                start=(dt == 0), stop=(dt == DT - 1)),
                                reads=[("kwA", kwp), ("xnbfA", dt)], writes=[bank])
                        tk.op("V", lambda m=m, bank=bank: V.tensor_scalar(
                            qtr[:, m * 512:(m + 1) * 512],
                            PS[bank[1]][:, 0:512], abias[:, m:m + 1], None, ALU.add),
                            reads=[bank, "abias"], writes=[("qtrA", m)])
                    for sg in range(4):
                        tau = qc * 4 + sg
                        for kc in range(C):
                            bank = psk[kc]
                            for m in range(DT):
                                tk.op("T", lambda m=m, kc=kc, sg=sg, bank=bank: T.matmul(
                                    PS[bank[1]][:, 0:512],
                                    qtr[:, m * 512 + sg * P: m * 512 + (sg + 1) * P],
                                    ktr[:, m * S + kc * 512: m * S + (kc + 1) * 512],
                                    start=(m == 0), stop=(m == DT - 1)),
                                    reads=[("qtrA", m), ("ktrA", m)], writes=[bank])
                            tk.op("V", lambda kc=kc, bank=bank: V.reduce_max(
                                rsA[:, kc:kc + 1], PS[bank[1]][:, 0:512],
                                AX.X, negate=True),
                                reads=[bank], writes=["rsneg"])
                        for kc in range(1, C):
                            tk.op("V", lambda kc=kc: V.tensor_tensor(
                                rsA[:, 0:1], rsA[:, 0:1], rsA[:, kc:kc + 1], ALU.min),
                                reads=["rsneg"], writes=["rsneg"])
                        tk.op("V", lambda: V.tensor_scalar(
                            rsA[:, 0:1], rsA[:, 0:1], 1.0 / 32.0, None, ALU.mult),
                            reads=["rsneg"], writes=["rsneg"])
                        for kc in range(C):
                            bank = psk[kc]
                            tk.op("S", lambda kc=kc, bank=bank: SC.activation(
                                pchk[:, kc * 512:(kc + 1) * 512],
                                PS[bank[1]][:, 0:512], AF.Exp,
                                scale=1.0 / 32.0, bias=rsA[:, 0:1],
                                accum_out=rsA[:, C + 1:C + 2]),
                                reads=[bank, "rsneg"],
                                writes=[("pchkA", kc), "rsacc"])
                            if kc == 0:
                                tk.op("V", lambda: V.tensor_copy(
                                    rsA[:, C:C + 1], rsA[:, C + 1:C + 2]),
                                    reads=["rsacc"], writes=["rstot"])
                            else:
                                tk.op("V", lambda: V.tensor_tensor(
                                    rsA[:, C:C + 1], rsA[:, C:C + 1],
                                    rsA[:, C + 1:C + 2], ALU.add),
                                    reads=["rsacc", "rstot"], writes=["rstot"])
                        tk.op("V", lambda: V.reciprocal(rsA[:, C:C + 1], rsA[:, C:C + 1]),
                              reads=["rstot"], writes=["rstot"])
                        ob = obuf[sg % 2]
                        sem = "dXa" if sg % 2 == 0 else "dXb"
                        tk.dma("Y", sem, lambda tau=tau, ob=ob: SY.dma_start(
                            ob[:, :], X[b, tau * P:(tau + 1) * P, :]),
                            writes=[("obA", sg % 2)])
                        for kt in range(C * 4):
                            pbb = pbk[kt % 2]
                            tk.op("T", lambda kt=kt, pbb=pbb: T.transpose(
                                PB[pbb[1]][:, 0:P], pchk[:, kt * P:(kt + 1) * P],
                                idb[:, :]),
                                reads=[("pchkA", kt // 4), "idb"], writes=[pbb])
                            tk.op("V", lambda kt=kt, pbb=pbb: V.tensor_copy(
                                ptr[:, (kt % 2) * P:(kt % 2 + 1) * P],
                                PB[pbb[1]][:, 0:P]),
                                reads=[pbb], writes=[("ptrA", kt % 2)])
                            for half in range(2):
                                bank = psk[4 + half]
                                tk.op("T", lambda kt=kt, half=half, bank=bank: T.matmul(
                                    PS[bank[1]][:, 0:512],
                                    ptr[:, (kt % 2) * P:(kt % 2 + 1) * P],
                                    vnat[:, kt * D + half * 512:
                                         kt * D + (half + 1) * 512],
                                    start=(kt == 0), stop=(kt == C * 4 - 1)),
                                    reads=[("ptrA", kt % 2), ("vnatA", kt)],
                                    writes=[bank])
                        for half in range(2):
                            bank = psk[4 + half]
                            tk.op("V", lambda half=half, bank=bank: V.scalar_tensor_tensor(
                                tmpo[:, half * 512:(half + 1) * 512],
                                PS[bank[1]][:, 0:512], rsA[:, C:C + 1],
                                bvbc[:, half * 512:(half + 1) * 512],
                                ALU.mult, ALU.add),
                                reads=[bank, "rstot", "bvbc"], writes=["tmpoA"])
                        for half in range(2):
                            tk.op("V", lambda half=half, ob=ob: V.tensor_tensor(
                                ob[:, half * 512:(half + 1) * 512],
                                tmpo[:, half * 512:(half + 1) * 512],
                                ob[:, half * 512:(half + 1) * 512], ALU.add),
                                reads=["tmpoA", ("obA", sg % 2)],
                                writes=[("obA", sg % 2)])
                        tk.dma("Y", "dO", lambda tau=tau, ob=ob: SY.dma_start(
                            OUT[b, tau * P:(tau + 1) * P, :], ob[:, :]),
                            reads=[("obA", sg % 2)], writes=[("out", b, tau)])
                tk.barrier()

        def tail(b):
            with ExitStack() as ls:
                ynat = ls.enter_context(nc.sbuf_tensor(f"ynatT{b}", [P, D], F32))
                for tau in range(NT):
                    for dt in range(DT):
                        bank = psk[4 + dt % 2]
                        tk.op("T", lambda dt=dt, tau=tau, bank=bank: T.transpose(
                            PS[bank[1]][:, 0:P], ysl(dt, tau * P, P), idf[:, :]),
                            reads=[("ytr", dt), "idf"], writes=[bank])
                        tk.op("V", lambda dt=dt, bank=bank: V.tensor_copy(
                            ynat[:, dt * P:(dt + 1) * P], PS[bank[1]][:, 0:P]),
                            reads=[bank], writes=["ynatT"])
                    tk.dma("G", "dAcc", lambda tau=tau: GP.dma_start(
                        OUT[b, tau * P:(tau + 1) * P, :], ynat[:, :],
                        accum_op=ALU.add),
                        reads=["ynatT", ("out", b, tau)], writes=[("out", b, tau)])
                tk.barrier()

        for b in range(BPC):
            if y0_input:
                for dt in range(DT):
                    tk.dma("Y", "dXa", lambda dt=dt: SY.dma_start(
                        ytr[:, dt * S:(dt + 1) * S], Y0[b, :, dt * S:(dt + 1) * S]),
                        writes=[("ytr", dt)])
            if not skip_a:
                phase_A(b)
            for l in range(L):
                tk.dma("Y", "dP", lambda l=l: SY.dma_start(prm[:, :], PRM[l]),
                       writes=["prm"])
                tk.dma("Y", "dP", lambda l=l: SY.dma_start(qkr[:, :], QKR[l]),
                       writes=["qkr"])
                if not skip_attn:
                    b_attn(l, b)
                if not skip_ffn:
                    b_ffn(l, b)
            if not skip_tail:
                tail(b)
            for tn in taps:
                if tn.split(":")[0] == "ytr":
                    for dt in range(DT):
                        tk.dma("Y", "dO", lambda dt=dt, tn=tn: SY.dma_start(
                            tap_t[tn][b, :, dt * S:(dt + 1) * S],
                            ytr[:, dt * S:(dt + 1) * S]),
                            reads=[("ytr", dt)])

        for sname in ["dO", "dAcc"]:
            if tk.dcnt[sname] > 0:
                SY.wait_ge(tk.dsem[sname], tk.dcnt[sname])
    return nc


_NC_CACHE = {}


def build_full():
    if "nc" not in _NC_CACHE:
        _NC_CACHE["nc"] = build_program(FULL_B // N_CORES, FULL_S, FULL_L)
    return _NC_CACHE["nc"]


def kernel(**inputs):
    S, L = FULL_S, FULL_L
    BPC = FULL_B // N_CORES
    hc = host_constants(S)
    prm, qkr = pack_params(inputs["L0_ffn_q_scale"], inputs["L0_ffn_k_scale"],
                           inputs["L0_ffn_ln_g"], inputs["L0_ffn_ln_b"],
                           inputs["L0_ffn_b1"], inputs["L0_ffn_b2"])
    abias = np.stack([np.asarray(inputs["L0_attn_b_q"]).reshape(DT, P).T,
                      np.asarray(inputs["L0_attn_b_k"]).reshape(DT, P).T]
                     ).astype(np.float32)
    X = np.ascontiguousarray(np.asarray(inputs["L0_attn_X"], dtype=np.float32))
    common = {
        "Wqa": np.ascontiguousarray(np.asarray(inputs["L0_attn_W_q"], np.float32)),
        "Wka": np.ascontiguousarray(np.asarray(inputs["L0_attn_W_k"], np.float32)),
        "Wva": np.ascontiguousarray(np.asarray(inputs["L0_attn_W_v"], np.float32)),
        "BV": np.ascontiguousarray(np.asarray(inputs["L0_attn_b_v"], np.float32)),
        "ABIAS": abias,
        "WQKV": np.ascontiguousarray(np.asarray(inputs["L0_ffn_Wqkv"], np.float32)),
        "WO": np.ascontiguousarray(np.asarray(inputs["L0_ffn_Wo"], np.float32)),
        "W1": np.ascontiguousarray(np.asarray(inputs["L0_ffn_W1"], np.float32)),
        "W2": np.ascontiguousarray(np.asarray(inputs["L0_ffn_W2"], np.float32)),
        "PRM": prm, "QKR": qkr,
        "ROT": hc["rotab"], "MSK": hc["masks"], "IDB": hc["ident_bf"],
        "IDF": hc["ident_f32"], "BLK": hc["blk64"], "ONEC": hc["ones_col"],
        "ONRF": hc["ones_row_f32"], "ROTR": hc["rotR"],
    }
    nc = build_full()
    in_maps = []
    for c in range(N_CORES):
        m = dict(common)
        m["X"] = np.ascontiguousarray(X[c * BPC:(c + 1) * BPC])
        in_maps.append(m)
    res = run_bass_kernel_spmd(nc, in_maps, core_ids=list(range(N_CORES)))
    out = np.concatenate([np.asarray(res.results[c]["OUT"]) for c in range(N_CORES)],
                         axis=0)
    return out.astype(np.float32)


# version 07
# speedup vs baseline: 1.1230x; 1.1230x over previous
import bisect
import numpy as np
import ml_dtypes
import concourse.bass as bass
import concourse.mybir as mybir
from concourse.bass_utils import run_bass_kernel_spmd
from contextlib import ExitStack

F32 = mybir.dt.float32
BF16 = mybir.dt.bfloat16
AF = mybir.ActivationFunctionType
ALU = mybir.AluOpType
AX = mybir.AxisListType

P = 128
D = 1024
DT = D // P
HEADS = 16
DH = 64
WIN = 256
FH = 2730
NOG = (FH + P - 1) // P
NEG = -1e30
EPS = 1e-5

PC_G = 0
PC_B = 8
PC_B1A = 16
PC_B1G = 16 + NOG
PC_B2 = 16 + 2 * NOG
PC_QS = 24 + 2 * NOG
PC_KS = 25 + 2 * NOG
NPARAM = 26 + 2 * NOG

N_CORES = 8
ATTN_STAGE = 9
ATTN_SUB = 9
FULL_B = 16
FULL_S = 2048
FULL_L = 6


def host_constants(S):
    half = np.arange(0, DH, 2, dtype=np.float64)
    inv_freq = 1.0 / 10000.0 ** (half / DH)
    t = np.arange(S, dtype=np.float64)
    freqs = np.concatenate([t[:, None] * inv_freq[None, :]] * 2, -1)
    base_scale = (half + 0.4 * DH) / (1.4 * DH)
    power = (t - S // 2) / (WIN // 2)
    scale = np.concatenate([base_scale[None, :] ** power[:, None]] * 2, -1)
    cos, sin = np.cos(freqs), np.sin(freqs)
    inv = 1.0 / scale
    sgn = np.concatenate([-np.ones(32), np.ones(32)])[None, :]

    def two(v):
        return np.concatenate([v.T, v.T], axis=0).astype(np.float32)

    rotab = np.stack([two(cos * scale), two(sin * scale * sgn),
                      two(cos * inv), two(sin * inv * sgn)])

    masks = np.full((2, P, 512), NEG, dtype=np.float32)
    for js in range(2):
        jr = np.arange(P)[:, None] + js * P
        qc = np.arange(512)[None, :]
        masks[js][(qc >= jr) & (qc <= jr + WIN)] = 0.0
    masks = masks.astype(ml_dtypes.bfloat16)

    blk = np.zeros((P, P))
    blk[:DH, :DH] = 1.0
    blk[DH:, DH:] = 1.0
    rotR = np.zeros((P, P))
    for d in range(P):
        base, r = d - d % DH, d % DH
        src = base + (r + 32 if r < 32 else r - 32)
        rotR[src, d] = 1.0
    return dict(rotab=rotab, masks=masks,
                ident_bf=np.eye(P, dtype=ml_dtypes.bfloat16),
                ident_f32=np.eye(P, dtype=np.float32),
                blk64=blk.astype(ml_dtypes.bfloat16),
                rotR=rotR.astype(ml_dtypes.bfloat16),
                ones_col=np.ones((P, 1), dtype=ml_dtypes.bfloat16),
                ones_row_f32=np.ones((1, P), dtype=np.float32))


def pack_params(q_scale, k_scale, ln_g, ln_b, b1, b2):
    L = ln_g.shape[0]
    cols, rows = [], []
    for l in range(L):
        g = ln_g[l].reshape(DT, P).T
        bb = ln_b[l].reshape(DT, P).T
        b1a = np.zeros((P, NOG), np.float32)
        b1g = np.zeros((P, NOG), np.float32)
        for og in range(NOG):
            n = min(P, FH - og * P)
            b1a[:n, og] = b1[l, og * P: og * P + n]
            b1g[:n, og] = b1[l, FH + og * P: FH + og * P + n]
        bb2 = b2[l].reshape(DT, P).T
        qs = np.concatenate([q_scale[l], q_scale[l]])[:, None]
        ks = np.concatenate([k_scale[l], k_scale[l]])[:, None]
        cols.append(np.concatenate([g, bb, b1a, b1g, bb2, qs, ks], 1).astype(np.float32))
        rows.append(np.concatenate([q_scale[l], k_scale[l]])[None, :].astype(np.float32))
    return np.stack(cols), np.stack(rows)


class Tk:

    def __init__(self, nc, stack, dma_sems):
        self.nc = nc
        self.eng = {"T": nc.tensor, "V": nc.vector, "S": nc.scalar,
                    "Y": nc.sync, "G": nc.gpsimd}
        self.esem = {}
        self.seq = {}
        self.instrs = {}
        self.incs = {}
        for k in self.eng:
            self.esem[k] = stack.enter_context(nc.semaphore("se" + k))
            self.seq[k] = 0
            self.instrs[k] = []
            self.incs[k] = []
        self.dsem = {}
        self.dcnt = {}
        for name in dma_sems:
            self.dsem[name] = stack.enter_context(nc.semaphore(name))
            self.dcnt[name] = 0
        self.buf = {}
        self.waited = {}

    def _inc_val(self, ekey, seq):
        incs = self.incs[ekey]
        if incs and incs[-1][0] >= seq:
            i = bisect.bisect_left(incs, (seq, -1))
            return incs[i][1]
        val = (incs[-1][1] + 1) if incs else 1
        self.instrs[ekey][seq - 1].then_inc(self.esem[ekey], 1)
        incs.append((seq, val))
        return val

    def _emit_waits(self, waiter, sources):
        need = {}
        for src in sources:
            if src is None:
                continue
            if src[0] == "E":
                _, ekey, seq = src
                if ekey == waiter == "T":
                    continue
                v = self._inc_val(ekey, seq)
                key = ("E", ekey)
                need[key] = max(need.get(key, 0), v)
            else:
                _, sname, cnt = src
                key = ("D", sname)
                need[key] = max(need.get(key, 0), self.dcnt[sname])
        for key, val in sorted(need.items()):
            if self.waited.get((waiter, key), 0) >= val:
                continue
            sem = self.esem[key[1]] if key[0] == "E" else self.dsem[key[1]]
            self.eng[waiter].wait_ge(sem, val)
            self.waited[(waiter, key)] = val

    def _gather(self, reads, writes):
        srcs = []
        for k in reads:
            b = self.buf.get(k)
            if b and b["w"]:
                srcs.append(b["w"])
        for k in writes:
            b = self.buf.get(k)
            if b:
                if b["w"]:
                    srcs.append(b["w"])
                srcs.extend(b["r"].values())
        return srcs

    def _update(self, reads, writes, src):
        for k in writes:
            self.buf[k] = {"w": src, "r": {}}
        for k in reads:
            b = self.buf.setdefault(k, {"w": None, "r": {}})
            b["r"][src[:2] if src[0] == "E" else ("D", src[1])] = src

    @staticmethod
    def _is_psum(k):
        return isinstance(k, tuple) and len(k) == 2 and k[0] in ("ps", "pb")

    def op(self, ekey, fn, reads=(), writes=()):
        srcs = self._gather(reads, writes)
        for k in reads:
            if self._is_psum(k):
                b = self.buf.get(k)
                if b:
                    for rk, v in b["r"].items():
                        if rk[0] == "E" and rk[1] != ekey:
                            srcs.append(v)
        self._emit_waits(ekey, srcs)
        ins = fn()
        self.seq[ekey] += 1
        self.instrs[ekey].append(ins)
        src = ("E", ekey, self.seq[ekey])
        self._update(reads, writes, src)
        return ins

    def dma(self, ekey, sname, fn, reads=(), writes=()):
        if self.dcnt[sname] > 0:
            key = ("D", sname)
            if self.waited.get((ekey, key), 0) < self.dcnt[sname]:
                self.eng[ekey].wait_ge(self.dsem[sname], self.dcnt[sname])
                self.waited[(ekey, key)] = self.dcnt[sname]
        self._emit_waits(ekey, self._gather(reads, writes))
        ins = fn()
        self.dcnt[sname] += 16
        ins.then_inc(self.dsem[sname], 16)
        src = ("D", sname, self.dcnt[sname])
        self._update(reads, writes, src)
        return ins

    def barrier_read(self, ekey, keys):
        self._emit_waits(ekey, self._gather(keys, ()))

    def barrier(self):
        srcs = []
        for ekey in self.eng:
            if self.seq[ekey] > 0:
                srcs.append(("E", ekey, self.seq[ekey]))
        for sname, cnt in self.dcnt.items():
            if cnt > 0:
                srcs.append(("D", sname, cnt))
        for waiter in self.eng:
            self._emit_waits(waiter, [s for s in srcs
                                      if not (s[0] == "E" and s[1] == waiter)])


DMA_SEMS = ["dW", "dP", "dTa", "dTb", "dXa", "dXb", "dQWa", "dQWb", "dKW0", "dKW1",
            "dVW0", "dVW1", "dWO0", "dWO1", "dW1a", "dW1b", "dW2a", "dW2b", "dO", "dAcc"]


def build_program(BPC, S, L, taps=(), skip_a=False, skip_attn=False, skip_ffn=False,
                  skip_tail=False, y0_input=False):
    C = S // 512
    NW = S // WIN
    NT = S // P
    HT = S // 2
    NU = HT // 512
    nc = bass.Bass("TRN2", target_bir_lowering=False)

    X = nc.dram_tensor("X", [BPC, S, D], F32, kind="ExternalInput").ap()
    OUT = nc.dram_tensor("OUT", [BPC, S, D], F32, kind="ExternalOutput").ap()
    Wqa = nc.dram_tensor("Wqa", [D, D], F32, kind="ExternalInput").ap()
    Wka = nc.dram_tensor("Wka", [D, D], F32, kind="ExternalInput").ap()
    Wva = nc.dram_tensor("Wva", [D, D], F32, kind="ExternalInput").ap()
    BV = nc.dram_tensor("BV", [D], F32, kind="ExternalInput").ap()
    ABIAS = nc.dram_tensor("ABIAS", [2, P, DT], F32, kind="ExternalInput").ap()
    WQKV = nc.dram_tensor("WQKV", [L, D, 3 * D], F32, kind="ExternalInput").ap()
    WO = nc.dram_tensor("WO", [L, D, D], F32, kind="ExternalInput").ap()
    W1 = nc.dram_tensor("W1", [L, D, 2 * FH], F32, kind="ExternalInput").ap()
    W2 = nc.dram_tensor("W2", [L, FH, D], F32, kind="ExternalInput").ap()
    PRM = nc.dram_tensor("PRM", [L, P, NPARAM], F32, kind="ExternalInput").ap()
    QKR = nc.dram_tensor("QKR", [L, 1, 2 * DH], F32, kind="ExternalInput").ap()
    ROT = nc.dram_tensor("ROT", [4, P, S], F32, kind="ExternalInput").ap()
    MSK = nc.dram_tensor("MSK", [2, P, 512], BF16, kind="ExternalInput").ap()
    IDB = nc.dram_tensor("IDB", [P, P], BF16, kind="ExternalInput").ap()
    IDF = nc.dram_tensor("IDF", [P, P], F32, kind="ExternalInput").ap()
    BLK = nc.dram_tensor("BLK", [P, P], BF16, kind="ExternalInput").ap()
    ONEC = nc.dram_tensor("ONEC", [P, 1], BF16, kind="ExternalInput").ap()
    ONRF = nc.dram_tensor("ONRF", [1, P], F32, kind="ExternalInput").ap()
    ROTR = nc.dram_tensor("ROTR", [P, P], BF16, kind="ExternalInput").ap()
    if y0_input:
        Y0 = nc.dram_tensor("Y0", [BPC, P, DT * S], F32, kind="ExternalInput").ap()
    tap_t = {}
    for tn in taps:
        tap_t[tn] = nc.dram_tensor("tap_" + tn.replace(":", "_"),
                                   [BPC, P, DT * S], F32, kind="ExternalOutput").ap()

    stack = ExitStack()
    with stack:
        tk = Tk(nc, stack, DMA_SEMS)
        T, V, SC, SY, GP = nc.tensor, nc.vector, nc.scalar, nc.sync, nc.gpsimd

        def sb(name, shape, dtype):
            return stack.enter_context(nc.sbuf_tensor(name, shape, dtype))

        ytr = sb("ytr", [P, DT * S], F32)
        idb = sb("idb", [P, P], BF16)
        idf = sb("idf", [P, P], F32)
        blk64 = sb("blk64", [P, P], BF16)
        onec = sb("onec", [P, 1], BF16)
        onrf = sb("onrf", [1, P], F32)
        prm = sb("prm", [P, NPARAM], F32)
        qkr = sb("qkr", [1, 2 * DH], F32)
        abias = sb("abias", [P, 2 * DT], F32)
        negb = sb("negb", [P, 1], F32)
        srow = sb("srow", [1, 4 * 512], F32)
        bvbc = sb("bvbc", [P, D], F32)
        bndry = sb("bndry", [P, DT], F32)
        epsc = sb("epsc", [P, 1], F32)
        rotr = sb("rotr", [P, P], BF16)
        blkq = sb("blkq", [P, 2 * P], BF16)
        qscol = sb("qscol", [P, 4], F32)

        PS = [stack.enter_context(nc.psum_tensor(f"ps{i}", [P, 512], F32)) for i in range(6)]
        PB = [stack.enter_context(nc.psum_tensor(f"pb{i}", [P, 1024], BF16)) for i in range(2)]
        psk = [("ps", i) for i in range(6)]
        pbk = [("pb", i) for i in range(2)]

        for dst, src_, key in [(idb[:, :], IDB, "idb"), (idf[:, :], IDF, "idf"),
                               (blk64[:, :], BLK, "blk"), (onec[:, :], ONEC, "onec"),
                               (onrf[:, :], ONRF, "onrf"),
                               (abias[:, 0:DT], ABIAS[0], "abias"),
                               (abias[:, DT:2 * DT], ABIAS[1], "abias"),
                               (rotr[:, :], ROTR, "rotr"),
                               (bvbc[:, :], BV[None, :].broadcast_to([P, D]), "bvbc")]:
            tk.dma("Y", "dW", lambda d=dst, s=src_: SY.dma_start(d, s), writes=[key])

        tk.op("V", lambda: V.memset(epsc[:, :], EPS), writes=["epsc"])
        nonrb = sb("nonrb", [1, P], BF16)
        tk.op("V", lambda: V.memset(nonrb[:, :], -1.0), writes=["nonrb"])

        def ysl(dt, t0, n):
            return ytr[:, dt * S + t0: dt * S + t0 + n]

        def b_ffn(l, b):
            with ExitStack() as ls:
                def lsb(n, sh, dt_):
                    return ls.enter_context(nc.sbuf_tensor(f"{n}_l{l}b{b}", sh, dt_))
                ybf = lsb("fybf", [P, NU * DT * 513], BF16)
                sqb = lsb("fsq", [P, 512], BF16)
                xnbf = lsb("fxn", [P, DT * HT], BF16)
                hbuf = lsb("fh", [P, NOG * HT], BF16)
                w1sl = [lsb(f"fw1{i}", [P, D], BF16) for i in range(4)]
                tbuf = lsb("ftmp", [P, (2 + NU) * 512], F32)
                fsrow = lsb("fsrow", [1, 1536], F32)
                mrowb = lsb("fmrw", [1, NU * 512], BF16)
                w2sl = [lsb(f"fw2{i}", [P, NOG * P], BF16) for i in range(2)]

                for hb in range(2):
                    T0 = hb * HT

                    def shsl(u, dt):
                        off = 0 if dt >= 4 else 1
                        return ybf[:, (u * DT + dt) * 513 + off:
                                   (u * DT + dt) * 513 + off + 512]

                    for u in range(NU):
                        t0 = T0 + u * 512
                        for dt in range(DT):
                            if t0 == 0:
                                tk.op("V", lambda u=u, dt=dt: V.memset(
                                    ybf[:, (u * DT + dt) * 513:(u * DT + dt) * 513 + 1],
                                    0.0), writes=[("ybf", u, dt)])
                            elif dt >= 4 and t0 == HT:
                                tk.op("V", lambda u=u, dt=dt: V.tensor_copy(
                                    ybf[:, (u * DT + dt) * 513:(u * DT + dt) * 513 + 1],
                                    bndry[:, dt:dt + 1]),
                                    reads=[("bnd", dt)], writes=[("ybf", u, dt)])
                            else:
                                tk.op("V", lambda u=u, dt=dt, t0=t0: V.tensor_copy(
                                    ybf[:, (u * DT + dt) * 513:(u * DT + dt) * 513 + 1],
                                    ysl(dt, t0 - 1, 1)),
                                    reads=[("ytr", dt)], writes=[("ybf", u, dt)])
                            sname = ["dVW0", "dVW1", "dKW0", "dKW1"][(u * DT + dt) % 4]
                            tk.dma("G", sname, lambda u=u, dt=dt, t0=t0: GP.dma_start(
                                ybf[:, (u * DT + dt) * 513 + 1:(u * DT + dt + 1) * 513],
                                ysl(dt, t0, 512)),
                                reads=[("ytr", dt)], writes=[("ybf", u, dt)])
                    for u in range(NU):
                        mb_ = psk[0 + 2 * u]
                        sb_ = psk[1 + 2 * u]
                        for dt in range(DT):
                            tk.op("T", lambda u=u, dt=dt, mb_=mb_: T.matmul(
                                PS[mb_[1]][0:1, 0:512], onec[:, :], shsl(u, dt),
                                start=(dt == 0), stop=(dt == DT - 1)),
                                reads=[("ybf", u, dt), "onec"], writes=[mb_])
                        for dt in range(DT):
                            tk.op("S", lambda u=u, dt=dt: SC.activation(
                                sqb[:, :], shsl(u, dt), AF.Square),
                                reads=[("ybf", u, dt)], writes=["sqb"])
                            tk.op("T", lambda u=u, dt=dt, sb_=sb_: T.matmul(
                                PS[sb_[1]][0:1, 0:512], onec[:, :], sqb[:, :],
                                start=(dt == 0), stop=(dt == DT - 1)),
                                reads=["sqb", "onec"], writes=[sb_])
                    MCOL = [0, 512]
                    VCOL = [1024, 1024]
                    for u in range(NU):
                        mb_ = psk[0 + 2 * u]
                        sb_ = psk[1 + 2 * u]
                        mc, vc = MCOL[u], VCOL[u]
                        sbb = psk[0 + u]
                        tk.op("V", lambda u=u, mb_=mb_, mc=mc: V.tensor_scalar(
                            fsrow[:, mc:mc + 512], PS[mb_[1]][0:1, 0:512],
                            1.0 / D, None, ALU.mult),
                            reads=[mb_], writes=[("srw", mc // 512)])
                        tk.op("V", lambda u=u, mc=mc: V.tensor_copy(
                            mrowb[0:1, u * 512:(u + 1) * 512], fsrow[:, mc:mc + 512]),
                            reads=[("srw", mc // 512)], writes=[("mrb", u)])
                        tk.op("V", lambda u=u, mc=mc, vc=vc: V.tensor_tensor(
                            fsrow[:, vc:vc + 512], fsrow[:, mc:mc + 512],
                            fsrow[:, mc:mc + 512], ALU.mult),
                            reads=[("srw", mc // 512)], writes=[("srw", vc // 512)])
                        tk.op("V", lambda u=u, sb_=sb_, vc=vc: V.scalar_tensor_tensor(
                            fsrow[:, vc:vc + 512], PS[sb_[1]][0:1, 0:512],
                            1.0 / D, fsrow[:, vc:vc + 512],
                            ALU.mult, ALU.subtract),
                            reads=[sb_, ("srw", vc // 512)], writes=[("srw", vc // 512)])
                        tk.op("S", lambda u=u, vc=vc: SC.activation(
                            fsrow[:, vc:vc + 512], fsrow[:, vc:vc + 512],
                            AF.Ln, bias=epsc[0:1, 0:1]),
                            reads=[("srw", vc // 512), "epsc"], writes=[("srw", vc // 512)])
                        tk.op("S", lambda u=u, vc=vc: SC.activation(
                            fsrow[:, vc:vc + 512], fsrow[:, vc:vc + 512],
                            AF.Exp, scale=-0.5),
                            reads=[("srw", vc // 512)], writes=[("srw", vc // 512)])
                        tk.op("T", lambda u=u, vc=vc, sbb=sbb: T.matmul(
                            PS[sbb[1]][:, 0:512], onrf[:, :], fsrow[:, vc:vc + 512],
                            start=True, stop=True),
                            reads=[("srw", vc // 512), "onrf"], writes=[sbb])
                        tk.op("V", lambda u=u, sbb=sbb: V.tensor_copy(
                            tbuf[:, 1024 + u * 512:1024 + (u + 1) * 512],
                            PS[sbb[1]][:, 0:512]),
                            reads=[sbb], writes=[("rsb", u)])
                    for u in range(NU):
                        for dt in range(DT):
                            stp = u * DT + dt
                            tb_ = psk[2 + stp % 4]
                            sl = stp % 2
                            tk.op("T", lambda u=u, tb_=tb_: T.matmul(
                                PS[tb_[1]][:, 0:512], nonrb[:, :],
                                mrowb[0:1, u * 512:(u + 1) * 512],
                                start=True, stop=False),
                                reads=[("mrb", u), "nonrb"], writes=[tb_])
                            tk.op("T", lambda u=u, dt=dt, tb_=tb_: T.matmul(
                                PS[tb_[1]][:, 0:512], idb[:, :], shsl(u, dt),
                                start=False, stop=True),
                                reads=[("ybf", u, dt), "idb"], writes=[tb_])
                            tk.op("V", lambda u=u, sl=sl, tb_=tb_: V.tensor_tensor(
                                tbuf[:, sl * 512:(sl + 1) * 512],
                                PS[tb_[1]][:, 0:512],
                                tbuf[:, 1024 + u * 512:1024 + (u + 1) * 512],
                                ALU.mult),
                                reads=[tb_, ("rsb", u)], writes=[("tb", sl)])
                            tk.op("S", lambda u=u, dt=dt, sl=sl: SC.activation(
                                xnbf[:, dt * HT + u * 512: dt * HT + u * 512 + 512],
                                tbuf[:, sl * 512:(sl + 1) * 512], AF.Identity,
                                scale=prm[:, PC_G + dt:PC_G + dt + 1],
                                bias=prm[:, PC_B + dt:PC_B + dt + 1]),
                                reads=[("tb", sl), "prm"], writes=[("xnbf", dt)])

                    def w1_load(og, bi):
                        sname = "dW1a" if bi == 0 else "dW1b"
                        c0a, c0g = og * P, FH + og * P
                        n = min(P, FH - og * P)
                        tk.dma("G", sname, lambda: GP.dma_start(
                            w1sl[bi][:, :].rearrange("p (k c) -> p k c", k=DT)[:, :, 0:n],
                            W1[l, :, c0a:c0a + n].rearrange("(k p) c -> p k c", p=P)),
                            writes=[("w1a", bi)])
                        tk.dma("G", sname, lambda: GP.dma_start(
                            w1sl[bi + 2][:, :].rearrange("p (k c) -> p k c", k=DT)[:, :, 0:n],
                            W1[l, :, c0g:c0g + n].rearrange("(k p) c -> p k c", p=P)),
                            writes=[("w1g", bi)])

                    w1_load(0, 0)
                    if NOG > 1:
                        w1_load(1, 1)
                    for og in range(NOG):
                        n = min(P, FH - og * P)
                        bi = og % 2
                        wa = w1sl[bi][:, :].rearrange("p (k c) -> p k c", k=DT)
                        wg = w1sl[bi + 2][:, :].rearrange("p (k c) -> p k c", k=DT)
                        for u in range(NU):
                            hidx = og * NU + u
                            pa_ = psk[(2 * hidx) % 6]
                            pg_ = psk[(2 * hidx + 1) % 6]
                            for dt in range(DT):
                                tk.op("T", lambda u=u, dt=dt, n=n, wa=wa, pa_=pa_: T.matmul(
                                    PS[pa_[1]][0:n, 0:512], wa[:, dt, 0:n],
                                    xnbf[:, dt * HT + u * 512: dt * HT + (u + 1) * 512],
                                    start=(dt == 0), stop=(dt == DT - 1)),
                                    reads=[("w1a", bi), ("xnbf", dt)], writes=[pa_])
                            for dt in range(DT):
                                tk.op("T", lambda u=u, dt=dt, n=n, wg=wg, pg_=pg_: T.matmul(
                                    PS[pg_[1]][0:n, 0:512], wg[:, dt, 0:n],
                                    xnbf[:, dt * HT + u * 512: dt * HT + (u + 1) * 512],
                                    start=(dt == 0), stop=(dt == DT - 1)),
                                    reads=[("w1g", bi), ("xnbf", dt)], writes=[pg_])
                            if u == NU - 1 and og + 2 < NOG:
                                w1_load(og + 2, bi)
                            tk.op("S", lambda u=u, og=og, n=n, pg_=pg_: SC.activation(
                                tbuf[0:n, u * 512:(u + 1) * 512],
                                PS[pg_[1]][0:n, 0:512], AF.Gelu,
                                bias=prm[0:n, PC_B1G + og:PC_B1G + og + 1]),
                                reads=[pg_, "prm"], writes=[("tb", u)])
                            tk.op("V", lambda u=u, og=og, n=n, pa_=pa_: V.scalar_tensor_tensor(
                                hbuf[0:n, og * HT + u * 512: og * HT + (u + 1) * 512],
                                PS[pa_[1]][0:n, 0:512],
                                prm[0:n, PC_B1A + og:PC_B1A + og + 1],
                                tbuf[0:n, u * 512:(u + 1) * 512], ALU.add, ALU.mult),
                                reads=[pa_, ("tb", u), "prm"], writes=[("h", og)])

                    if hb == 0:
                        for dt in range(4, DT):
                            tk.op("V", lambda dt=dt: V.tensor_copy(
                                bndry[:, dt:dt + 1], ysl(dt, HT - 1, 1)),
                                reads=[("ytr", dt)], writes=[("bnd", dt)])

                    for do in range(DT):
                        w2p = do % 2
                        tk.dma("G", "dW2a" if w2p == 0 else "dW2b",
                               lambda do=do, w2p=w2p: GP.dma_start(
                            w2sl[w2p][:, 0:(NOG - 1) * P]
                            .rearrange("p (g c) -> p g c", g=NOG - 1),
                            W2[l, 0:(NOG - 1) * P, do * P:(do + 1) * P]
                            .rearrange("(g p) c -> p g c", p=P)),
                            writes=[("w2sl", w2p)])
                        tk.dma("G", "dW2a" if w2p == 0 else "dW2b",
                               lambda do=do, w2p=w2p: GP.dma_start(
                            w2sl[w2p][0:FH - (NOG - 1) * P, (NOG - 1) * P:NOG * P],
                            W2[l, (NOG - 1) * P:FH, do * P:(do + 1) * P]),
                            writes=[("w2sl", w2p)])
                        pu = [psk[(do % 2) * 2], psk[(do % 2) * 2 + 1]][:NU]
                        for u in range(NU):
                            for og in range(NOG):
                                n = min(P, FH - og * P)
                                tk.op("T", lambda u=u, og=og, do=do, n=n, w2p=w2p: T.matmul(
                                    PS[pu[u][1]][:, 0:512],
                                    w2sl[w2p][0:n, og * P:og * P + P],
                                    hbuf[0:n, og * HT + u * 512: og * HT + (u + 1) * 512],
                                    start=(og == 0), stop=(og == NOG - 1)),
                                    reads=[("w2sl", w2p), ("h", og)], writes=[pu[u]])
                        for u in range(NU):
                            tk.op("V", lambda u=u, do=do, T0=T0: V.scalar_tensor_tensor(
                                ysl(do, T0 + u * 512, 512), PS[pu[u][1]][:, 0:512],
                                prm[:, PC_B2 + do:PC_B2 + do + 1],
                                ysl(do, T0 + u * 512, 512), ALU.add, ALU.add),
                                reads=[pu[u], "prm", ("ytr", do)], writes=[("ytr", do)])
                tk.barrier()

        def b_attn(l, b):
            with ExitStack() as ls:
                def lsb(n, sh, dt_):
                    return ls.enter_context(nc.sbuf_tensor(f"{n}_l{l}b{b}", sh, dt_))
                aybf = lsb("aybf", [P, DT * 513], BF16)
                qfin = lsb("aqf", [P, DT * 768], BF16)
                kfin = lsb("akf", [P, DT * 768], BF16)
                vrol = lsb("avr", [P, 8 * 1040], BF16)
                tk.op("V", lambda: V.memset(
                    vrol[:, :].rearrange("p (s h c) -> p (s h) c", s=8, h=16)[:, :, 64:65],
                    1.0), writes=[("vr", s) for s in range(8)] if False else ["vrones"])
                ptile = lsb("apt", [P, 4 * 2 * 512], BF16)
                sbf2 = lsb("asbf2", [P, 1536], BF16)
                onat = sbf2[:, 0:1024]
                tt1b_ = lsb("att1", [P, 1024], F32)
                rnbb_ = lsb("arnb", [P, 512], F32)
                msb = lsb("amsb", [P, 2 * 512], F32)
                rinv = lsb("arinv", [P, 16], F32)
                opas = lsb("aopas", [P, DT * 512], BF16)
                W = {
                    "qw": lsb("wqw", [P, 2 * D], BF16),
                    "kwb": lsb("wkwb", [P, 2 * D], BF16),
                    "vw": lsb("wvw", [P, DT * D], BF16),
                    "wos": lsb("wwos", [P, 2 * D], BF16),
                }
                tabs = lsb("atabs", [P, 2 * 512], F32)
                masks = lsb("amasks", [P, 2 * 512], BF16)
                tk.dma("Y", "dW", lambda: SY.dma_start(masks[:, 0:512], MSK[0]),
                       writes=["msk"])
                tk.dma("Y", "dW", lambda: SY.dma_start(masks[:, 512:1024], MSK[1]),
                       writes=["msk"])
                opart = lsb("aopart", [P, 2 * 2 * 1040], F32)

                for dt in range(DT):
                    tk.dma("G", "dQWa" if dt % 2 == 0 else "dQWb", lambda dt=dt: GP.dma_start(
                        W["vw"][:, dt * D:(dt + 1) * D],
                        WQKV[l, dt * P:(dt + 1) * P, 2 * D:3 * D]),
                        writes=[("vw", dt)])

                tk.op("V", lambda: V.reduce_max(srow[:, 0:1], qkr[:, 0:DH], AX.X,
                                                apply_absolute_value=True),
                      reads=["qkr"], writes=[("srow", 0)])
                tk.op("V", lambda: V.reduce_max(srow[:, 1:2], qkr[:, DH:2 * DH], AX.X,
                                                apply_absolute_value=True),
                      reads=["qkr"], writes=[("srow", 0)])
                tk.op("V", lambda: V.tensor_tensor(srow[:, 2:3], srow[:, 0:1],
                                                   srow[:, 1:2], ALU.mult),
                      reads=[("srow", 0)], writes=[("srow", 0)])
                tk.op("T", lambda: T.matmul(PS[0][:, 0:1], onrf[:, :], srow[:, 2:3],
                                            start=True, stop=True),
                      reads=[("srow", 0), "onrf"], writes=[psk[0]])
                tk.op("V", lambda: V.tensor_scalar(negb[:, :], PS[0][:, 0:1],
                                                   -16.0, None, ALU.mult),
                      reads=[psk[0]], writes=["negb"])

                tk.op("V", lambda: V.tensor_copy(qscol[:, 0:1], prm[:, PC_QS:PC_QS + 1]),
                      reads=["prm"], writes=["qsc"])
                tk.op("V", lambda: V.tensor_copy(qscol[:, 1:2], prm[:, PC_KS:PC_KS + 1]),
                      reads=["prm"], writes=["qsc"])
                tk.op("V", lambda: V.tensor_tensor(qscol[:, 2:3], qscol[:, 0:1],
                                                   qscol[:, 0:1], ALU.mult),
                      reads=["qsc"], writes=["qsc"])
                tk.op("V", lambda: V.tensor_tensor(qscol[:, 3:4], qscol[:, 1:2],
                                                   qscol[:, 1:2], ALU.mult),
                      reads=["qsc"], writes=["qsc"])
                tk.op("V", lambda: V.reciprocal(qscol[:, 2:3], qscol[:, 2:3]),
                      reads=["qsc"], writes=["qsc"])
                tk.op("V", lambda: V.reciprocal(qscol[:, 3:4], qscol[:, 3:4]),
                      reads=["qsc"], writes=["qsc"])
                tk.op("V", lambda: V.tensor_scalar(blkq[:, 0:P], blk64[:, :],
                                                   qscol[:, 2:3], None, ALU.mult),
                      reads=["qsc", "blk"], writes=[("blkq", 0)])
                tk.op("V", lambda: V.tensor_scalar(blkq[:, P:2 * P], blk64[:, :],
                                                   qscol[:, 3:4], None, ALU.mult),
                      reads=["qsc", "blk"], writes=[("blkq", 1)])

                def qf_read(side_buf, dt, t0, n):
                    s0 = (t0 // 256) % 3
                    s1 = (s0 + 1) % 3
                    if n <= 256 or s1 == s0 + 1:
                        return [(side_buf[:, dt * 768 + s0 * 256:
                                          dt * 768 + s0 * 256 + n], 0, n)]
                    return [(side_buf[:, dt * 768 + s0 * 256:
                                      dt * 768 + s0 * 256 + 256], 0, 256),
                            (side_buf[:, dt * 768 + s1 * 256:
                                      dt * 768 + s1 * 256 + (n - 256)], 256, n - 256)]

                for c in range(C + 1):
                    if ATTN_STAGE < 2:
                        continue
                    if c < C:
                        t0c = c * 512

                        def absl(dt, j0, n):
                            off = 0 if dt >= 4 else 1
                            return aybf[:, dt * 513 + off + j0: dt * 513 + off + j0 + n]

                        for dt in range(DT):
                            if t0c == 0:
                                tk.op("V", lambda dt=dt: V.memset(
                                    aybf[:, dt * 513:dt * 513 + 1], 0.0),
                                    writes=[("aybf", dt)])
                            else:
                                tk.op("V", lambda dt=dt: V.tensor_copy(
                                    aybf[:, dt * 513:dt * 513 + 1], ysl(dt, t0c - 1, 1)),
                                    reads=[("ytr", dt)], writes=[("aybf", dt)])
                            tk.op("V", lambda dt=dt: V.tensor_copy(
                                aybf[:, dt * 513 + 1:(dt + 1) * 513], ysl(dt, t0c, 512)),
                                reads=[("ytr", dt)], writes=[("aybf", dt)])

                        for tt in range(4):
                            tslot = ((t0c + tt * P) // P) % 8
                            for half in range(2):
                                bank = psk[4 + (2 * tt + half) % 2]
                                for dt in range(DT):
                                    tk.op("T", lambda dt=dt, tt=tt, half=half, bank=bank:
                                          T.matmul(
                                              PS[bank[1]][:, 0:512],
                                              absl(dt, tt * P, P),
                                              W["vw"][:, dt * D + half * 512:
                                                 dt * D + (half + 1) * 512],
                                              start=(dt == 0), stop=(dt == DT - 1)),
                                          reads=[("aybf", dt), ("vw", dt)],
                                          writes=[bank])
                                tk.op("V", lambda half=half, bank=bank, tslot=tslot:
                                      V.tensor_copy(
                                          vrol[:, tslot * 1040 + half * 520:
                                               tslot * 1040 + (half + 1) * 520]
                                          .rearrange("p (h c) -> p h c", h=8)[:, :, 0:64],
                                          PS[bank[1]][:, 0:512]
                                          .rearrange("p (h c) -> p h c", h=8)),
                                      reads=[bank], writes=[("vr", tslot)])

                        for side in range(2 if ATTN_STAGE >= 3 else 0):
                            tbase = 0 if side == 0 else 2
                            for i in range(2):
                                tk.dma("Y", "dTa" if i == 0 else "dTb",
                                       lambda i=i, tbase=tbase: SY.dma_start(
                                           tabs[:, i * 512:(i + 1) * 512],
                                           ROT[tbase + i][:, t0c:t0c + 512]),
                                       writes=["tabs"])
                            sc_col = qscol[:, side:side + 1]
                            dst = qfin if side == 0 else kfin
                            dk = "qf" if side == 0 else "kf"
                            def qk_slab_load(m):
                                kwp = m % 2
                                if side == 1:
                                    tk.dma("G", "dKW0" if kwp == 0 else "dKW1",
                                           lambda m=m, kwp=kwp: GP.dma_start(
                                               W["kwb"][:, kwp * D:(kwp + 1) * D]
                                               .rearrange("p (k c) -> p k c", k=DT),
                                               WQKV[l, :, D + m * P:D + (m + 1) * P]
                                               .rearrange("(k p) c -> p k c", p=P)),
                                           writes=[("kwb", kwp)])
                                else:
                                    tk.dma("G", "dQWa" if kwp == 0 else "dQWb",
                                           lambda m=m, kwp=kwp: GP.dma_start(
                                               W["qw"][:, kwp * D:(kwp + 1) * D]
                                               .rearrange("p (k c) -> p k c", k=DT),
                                               WQKV[l, :, m * P:(m + 1) * P]
                                               .rearrange("(k p) c -> p k c", p=P)),
                                           writes=[("qw", kwp)])

                            if ATTN_SUB >= 2:
                                qk_slab_load(0)
                            for m in range(DT if ATTN_SUB >= 2 else 0):
                                kwp = m % 2
                                if m + 1 < DT:
                                    qk_slab_load(m + 1)
                                bank = psk[m % 3]
                                for dt in range(DT if ATTN_SUB >= 3 else 0):
                                    if side == 0:
                                        lhs = W["qw"][:, kwp * D + dt * P: kwp * D + (dt + 1) * P]
                                        rk = ("qw", kwp)
                                    else:
                                        lhs = W["kwb"][:, kwp * D + dt * P: kwp * D + (dt + 1) * P]
                                        rk = ("kwb", kwp)
                                    tk.op("T", lambda lhs=lhs, dt=dt, bank=bank: T.matmul(
                                        PS[bank[1]][:, 0:512], lhs, absl(dt, 0, 512),
                                        start=(dt == 0), stop=(dt == DT - 1)),
                                        reads=[rk, ("aybf", dt)], writes=[bank])
                                if ATTN_SUB < 4:
                                    continue
                                par = m % 2
                                ubf = sbf2[:, par * 512:(par + 1) * 512]
                                sqb = sbf2[:, 1024:1536]
                                u2 = msb[:, (1 - par) * 512:(2 - par) * 512]
                                tt1 = tt1b_[:, par * 512:(par + 1) * 512]
                                rnb = rnbb_[:, 0:512]
                                rnscr = msb[:, par * 512:(par + 1) * 512]
                                tk.op("S", lambda ubf=ubf, bank=bank, sc_col=sc_col:
                                      SC.activation(
                                          ubf, PS[bank[1]][:, 0:512], AF.Copy,
                                          scale=sc_col),
                                      reads=[bank, "qsc"], writes=[("ubf", par)])
                                if ATTN_SUB < 5:
                                    continue
                                rotb = psk[3 + m % 2]
                                tk.op("T", lambda rotb=rotb, ubf=ubf: T.matmul(
                                    PS[rotb[1]][:, 0:512], rotr[:, :],
                                    ubf, start=True, stop=True),
                                      reads=["rotr", ("ubf", par)], writes=[rotb])
                                tk.op("G", lambda sqb=sqb, ubf=ubf: GP.tensor_tensor(
                                    sqb, ubf, ubf, ALU.mult),
                                    reads=[("ubf", par)], writes=["sqb2"])
                                tk.op("T", lambda side=side, sqb=sqb: T.matmul(
                                    PS[5][:, 0:512], blkq[:, side * P:(side + 1) * P],
                                    sqb, start=True, stop=True),
                                    reads=[("blkq", side), "sqb2"], writes=[psk[5]])
                                tk.op("S", lambda rnscr=rnscr: SC.activation(
                                    rnscr, PS[5][:, 0:512], AF.Ln),
                                      reads=[psk[5]], writes=[("msb", par)])
                                tk.op("S", lambda rnb=rnb, rnscr=rnscr: SC.activation(
                                    rnb, rnscr, AF.Exp, scale=-0.5),
                                      reads=[("msb", par)], writes=["rnb"])
                                tk.op("V", lambda tt1=tt1, bank=bank, sc_col=sc_col:
                                      V.scalar_tensor_tensor(
                                          tt1, PS[bank[1]][:, 0:512], sc_col,
                                          tabs[:, 0:512], ALU.mult, ALU.mult),
                                      reads=[bank, "qsc", "tabs"], writes=[("tt1", par)])
                                tk.op("V", lambda u2=u2, rotb=rotb: V.tensor_tensor(
                                    u2, PS[rotb[1]][:, 0:512],
                                    tabs[:, 512:1024], ALU.mult),
                                    reads=[rotb, "tabs"], writes=[("msb", 1 - par)])
                                tk.op("G", lambda tt1=tt1, u2=u2: GP.tensor_tensor(
                                    tt1, tt1, u2, ALU.add),
                                    reads=[("tt1", par), ("msb", 1 - par)],
                                    writes=[("tt1", par)])
                                sa = (t0c // 256) % 3
                                sbt = (sa + 1) % 3
                                if sbt == sa + 1:
                                    tk.op("V", lambda dst=dst, m=m, sa=sa, tt1=tt1,
                                          rnb=rnb: V.tensor_tensor(
                                        dst[:, m * 768 + sa * 256: m * 768 + sa * 256 + 512],
                                        tt1, rnb, ALU.mult),
                                        reads=[("tt1", par), "rnb"],
                                        writes=[(dk, m, sa), (dk, m, sbt)])
                                else:
                                    tk.op("V", lambda dst=dst, m=m, sa=sa, tt1=tt1,
                                          rnb=rnb: V.tensor_tensor(
                                        dst[:, m * 768 + sa * 256: m * 768 + sa * 256 + 256],
                                        tt1[:, 0:256], rnb[:, 0:256], ALU.mult),
                                        reads=[("tt1", par), "rnb"],
                                        writes=[(dk, m, sa)])
                                    tk.op("V", lambda dst=dst, m=m, sbt=sbt, tt1=tt1,
                                          rnb=rnb: V.tensor_tensor(
                                        dst[:, m * 768 + sbt * 256:
                                            m * 768 + sbt * 256 + 256],
                                        tt1[:, 256:512], rnb[:, 256:512], ALU.mult),
                                        reads=[("tt1", par), "rnb"],
                                        writes=[(dk, m, sbt)])

                    kws = []
                    if 2 * c - 1 >= 0 and c <= C:
                        kws.append(2 * c - 1)
                    if c < C:
                        kws.append(2 * c)
                    for kw in (kws if ATTN_STAGE >= 4 else []):
                        q0 = 256 * kw
                        qn = min(512, S - q0)
                        for hg in range(4):
                            for hh in range(4):
                                h = hg * 4 + hh
                                m = h // 2
                                rbase = (h % 2) * DH
                                for js in range(2):
                                    jt0 = 256 * kw + js * P
                                    ksl = (jt0 // 256) % 3
                                    koff = jt0 % 256
                                    lhs = kfin[rbase:rbase + DH,
                                               m * 768 + ksl * 256 + koff:
                                               m * 768 + ksl * 256 + koff + P]
                                    bank = psk[(hh + js) % 2]
                                    for (ap_, rel, ln_) in qf_read(qfin, m, q0, qn):
                                        tk.op("T", lambda lhs=lhs, ap_=ap_, rel=rel,
                                              ln_=ln_, bank=bank, rbase=rbase: T.matmul(
                                                  PS[bank[1]][:, rel:rel + ln_],
                                                  lhs, ap_[rbase:rbase + DH, :],
                                                  start=True, stop=True),
                                              reads=[("kf", m, ksl),
                                                     ("qf", m, (q0 // 256) % 3),
                                                     ("qf", m, (q0 // 256 + 1) % 3)],
                                              writes=[bank])
                                    mpar = (hh + js) % 2
                                    tk.op("V", lambda bank=bank, js=js, qn=qn, mpar=mpar:
                                          V.tensor_tensor(
                                              msb[:, mpar * 512: mpar * 512 + qn],
                                              PS[bank[1]][:, 0:qn],
                                              masks[:, js * 512: js * 512 + qn], ALU.add),
                                          reads=[bank, "msk"], writes=[("msb", mpar)])
                                    tk.op("S", lambda hh=hh, js=js, qn=qn, mpar=mpar:
                                          SC.activation(
                                              ptile[:, (hh * 2 + js) * 512:
                                                    (hh * 2 + js) * 512 + qn],
                                              msb[:, mpar * 512: mpar * 512 + qn],
                                              AF.Exp, scale=8.0, bias=negb[:, 0:1]),
                                          reads=[("msb", mpar), "negb"],
                                          writes=[("pt", hh, js)])
                            for w in ((kw, kw + 1) if ATTN_STAGE >= 5 else ()):
                                if w >= NW:
                                    continue
                                wsl = w % 2
                                first = (kw == max(w - 1, 0))
                                for sg in range(2):
                                    st0 = w * 256 + sg * P
                                    if st0 >= S:
                                        continue
                                    coff = st0 - 256 * kw
                                    ob = psk[2 + ((w + sg) % 2) * 0 + sg % 1] if False                                         else psk[2 + (sg % 2)]
                                    rsb = psk[4 + (sg % 2)]
                                    for hh in range(4):
                                        h = hg * 4 + hh
                                        for js in range(2):
                                            jt0 = 256 * kw + js * P
                                            vsl = (jt0 // P) % 8
                                            lhs = ptile[:, (hh * 2 + js) * 512 + coff:
                                                        (hh * 2 + js) * 512 + coff + P]
                                            tk.op("T", lambda lhs=lhs, h=h, vsl=vsl,
                                                  ob=ob, js=js, hh=hh: T.matmul(
                                                      PS[ob[1]][:, hh * 65:hh * 65 + 65],
                                                      lhs,
                                                      vrol[:, vsl * 1040 + h * 65:
                                                           vsl * 1040 + h * 65 + 65],
                                                      start=(js == 0), stop=(js == 1)),
                                                  reads=[("pt", hh, js), ("vr", vsl),
                                                         "vrones"],
                                                  writes=[ob])
                                    oc = wsl * 2080 + sg * 1040 + hg * 260
                                    if first:
                                        tk.op("V", lambda ob=ob, oc=oc: V.tensor_copy(
                                            opart[:, oc:oc + 260], PS[ob[1]][:, 0:260]),
                                            reads=[ob], writes=[("op", wsl, sg, hg)])
                                    else:
                                        tk.op("V", lambda ob=ob, oc=oc: V.tensor_tensor(
                                            opart[:, oc:oc + 260], PS[ob[1]][:, 0:260],
                                            opart[:, oc:oc + 260], ALU.add),
                                            reads=[ob, ("op", wsl, sg, hg)],
                                            writes=[("op", wsl, sg, hg)])

                        if ATTN_STAGE < 6:
                            continue
                        w = kw
                        wsl = w % 2
                        pbase = max(0, 512 * c - 256)
                        for sg in range(2):
                            st0 = w * 256 + sg * P
                            if st0 >= S:
                                continue
                            oc = wsl * 2080 + sg * 1040
                            tk.op("V", lambda oc=oc: V.reciprocal(
                                rinv[:, 0:16],
                                opart[:, oc:oc + 1040]
                                .rearrange("p (g d) -> p g d", g=16)[:, :, 64]),
                                reads=[("op", wsl, sg, 0), ("op", wsl, sg, 1),
                                       ("op", wsl, sg, 2), ("op", wsl, sg, 3)],
                                writes=["rinv"])
                            rexp = rinv[:, 0:16].unsqueeze(2).broadcast_to([P, 16, DH])
                            tk.op("V", lambda oc=oc, rexp=rexp: V.tensor_tensor(
                                onat[:, :].rearrange("p (g d) -> p g d", g=16),
                                opart[:, oc:oc + 1040]
                                .rearrange("p (g d) -> p g d", g=16)[:, :, 0:64],
                                rexp, ALU.mult),
                                reads=[("op", wsl, sg, 0), ("op", wsl, sg, 1),
                                       ("op", wsl, sg, 2), ("op", wsl, sg, 3), "rinv"],
                                writes=[("ubf", 0), ("ubf", 1)])
                            for half4 in range(2):
                                pbb = pbk[half4]
                                for j4 in range(4):
                                    it = half4 * 4 + j4
                                    tk.op("T", lambda it=it, j4=j4, pbb=pbb: T.transpose(
                                        PB[pbb[1]][:, j4 * P:(j4 + 1) * P],
                                        onat[:, it * P:(it + 1) * P], idb[:, :]),
                                        reads=[("ubf", 0), ("ubf", 1), "idb"],
                                        writes=[pbb])
                                off = st0 - pbase
                                dst = opas[:, :].rearrange(
                                    "p (it c) -> p it c", c=512)[:, half4 * 4:half4 * 4 + 4,
                                                                 off:off + P]
                                tk.op("V", lambda pbb=pbb, dst=dst: V.tensor_copy(
                                    dst, PB[pbb[1]][:, 0:512].rearrange(
                                        "p (it c) -> p it c", c=P)),
                                    reads=[pbb],
                                    writes=[("opas", half4 * 4 + j) for j in range(4)])

                    if kws and ATTN_STAGE >= 7:
                        pbase = max(0, 512 * c - 256)
                        pend = min(S, 512 * c + 256)
                        wlen = pend - pbase
                        def wo_load(do):
                            wop = do % 2
                            tk.dma("G", "dWO0" if wop == 0 else "dWO1",
                                   lambda do=do, wop=wop: GP.dma_start(
                                       W["wos"][:, wop * D:(wop + 1) * D]
                                       .rearrange("p (k c) -> p k c", k=DT),
                                       WO[l, :, do * P:(do + 1) * P]
                                       .rearrange("(k p) c -> p k c", p=P)),
                                   writes=[("wos", wop)])

                        wo_load(0)
                        wo_load(1)
                        for do in range(DT):
                            wop = do % 2
                            bank = psk[do % 2]
                            for it in range(DT):
                                tk.op("T", lambda do=do, it=it, wop=wop, bank=bank,
                                      wlen=wlen: T.matmul(
                                          PS[bank[1]][:, 0:wlen],
                                          W["wos"][:, wop * D + it * P: wop * D + (it + 1) * P],
                                          opas[:, it * 512: it * 512 + wlen],
                                          start=(it == 0), stop=(it == DT - 1)),
                                      reads=[("wos", wop), ("opas", it)],
                                      writes=[bank])
                            tk.op("V", lambda do=do, bank=bank, pbase=pbase, wlen=wlen:
                                  V.tensor_tensor(
                                      ysl(do, pbase, wlen), PS[bank[1]][:, 0:wlen],
                                      ysl(do, pbase, wlen), ALU.add),
                                  reads=[bank, ("ytr", do)], writes=[("ytr", do)])
                            if do + 2 < DT:
                                wo_load(do + 2)
                tk.barrier()

        def phase_A(b):
            with ExitStack() as ls:
                def lsb(n, sh, dt_):
                    return ls.enter_context(nc.sbuf_tensor(f"{n}_A{b}", sh, dt_))
                ktr = lsb("ktr", [P, DT * S], BF16)
                vnat = lsb("vnat", [P, NT * D], BF16)
                xnbf = lsb("xnbfA", [P, DT * 512], BF16)
                qtr = lsb("qtrA", [P, DT * 512], BF16)
                kwA = lsb("kwA", [P, 2 * D], BF16)
                xt = [lsb(f"xtA{i}", [P, D], F32) for i in range(2)]
                ast_ = lsb("astA", [P, 8], F32)
                pchk = lsb("pchkA", [P, C * 512], BF16)
                ptr = lsb("ptrA", [P, 2 * P], BF16)
                rsA = lsb("rsA", [P, C + 2], F32)
                obuf = [lsb(f"obufA{i}", [P, D], F32) for i in range(2)]
                tmpo = lsb("tmpoA", [P, D], F32)
                vwA = lsb("vwA", [P, DT * D], BF16)

                for tau in range(NT):
                    xb = xt[tau % 2]
                    sem = "dXa" if tau % 2 == 0 else "dXb"
                    tk.dma("Y", sem, lambda tau=tau, xb=xb: SY.dma_start(
                        xb[:, :], X[b, tau * P:(tau + 1) * P, :]),
                        writes=[("xt", tau % 2)])
                    tk.op("V", lambda xb=xb: V.tensor_reduce(
                        ast_[:, 0:1], xb[:, :], AX.X, ALU.add),
                        reads=[("xt", tau % 2)], writes=["astA"])
                    tk.op("S", lambda xb=xb: SC.activation(tmpo[:, :], xb[:, :], AF.Square),
                          reads=[("xt", tau % 2)], writes=["tmpoA"])
                    tk.op("V", lambda: V.tensor_reduce(
                        ast_[:, 1:2], tmpo[:, :], AX.X, ALU.add),
                        reads=["tmpoA"], writes=["astA"])
                    tk.op("V", lambda: V.tensor_scalar(
                        ast_[:, 2:3], ast_[:, 0:1], 1.0 / D, None, ALU.mult),
                        reads=["astA"], writes=["astA"])
                    tk.op("V", lambda: V.tensor_tensor(
                        ast_[:, 3:4], ast_[:, 2:3], ast_[:, 2:3], ALU.mult),
                        reads=["astA"], writes=["astA"])
                    tk.op("V", lambda: V.tensor_scalar(
                        ast_[:, 4:5], ast_[:, 1:2], 1.0 / D, EPS, ALU.mult, ALU.add),
                        reads=["astA"], writes=["astA"])
                    tk.op("V", lambda: V.tensor_tensor(
                        ast_[:, 5:6], ast_[:, 4:5], ast_[:, 3:4], ALU.subtract),
                        reads=["astA"], writes=["astA"])
                    tk.op("S", lambda: SC.activation(ast_[:, 6:7], ast_[:, 5:6], AF.Ln),
                          reads=["astA"], writes=["astB"])
                    tk.op("S", lambda: SC.activation(ast_[:, 7:8], ast_[:, 6:7],
                                                     AF.Exp, scale=-0.5),
                          reads=["astB"], writes=["astB"])
                    tk.op("V", lambda xb=xb: V.tensor_scalar(
                        obuf[0][:, :], xb[:, :], ast_[:, 2:3], ast_[:, 7:8],
                        ALU.subtract, ALU.mult),
                        reads=[("xt", tau % 2), "astA", "astB"], writes=[("obA", 0)])
                    for dt in range(DT):
                        bank = psk[4 + dt % 2]
                        tk.op("T", lambda dt=dt, bank=bank: T.transpose(
                            PS[bank[1]][:, 0:P], obuf[0][:, dt * P:(dt + 1) * P],
                            idf[:, :]),
                            reads=[("obA", 0), "idf"], writes=[bank])
                        tk.op("V", lambda dt=dt, bank=bank, tau=tau: V.tensor_copy(
                            ysl(dt, tau * P, P), PS[bank[1]][:, 0:P]),
                            reads=[bank], writes=[("ytr", dt)])

                for cc in range(C):
                    t0c = cc * 512
                    for dt in range(DT):
                        tk.op("V", lambda dt=dt, t0c=t0c: V.tensor_copy(
                            xnbf[:, dt * 512:(dt + 1) * 512], ysl(dt, t0c, 512)),
                            reads=[("ytr", dt)], writes=[("xnbfA", dt)])
                    for m in range(DT):
                        kwp = m % 2
                        tk.dma("G", "dKW0" if kwp == 0 else "dKW1",
                               lambda m=m, kwp=kwp: GP.dma_start(
                                   kwA[:, kwp * D:(kwp + 1) * D]
                                   .rearrange("p (k c) -> p k c", k=DT),
                                   Wka[:, m * P:(m + 1) * P]
                                   .rearrange("(k p) c -> p k c", p=P)),
                               writes=[("kwA", kwp)])
                        bank = psk[m % 2]
                        for dt in range(DT):
                            tk.op("T", lambda m=m, dt=dt, kwp=kwp, bank=bank: T.matmul(
                                PS[bank[1]][:, 0:512],
                                kwA[:, kwp * D + dt * P: kwp * D + (dt + 1) * P],
                                xnbf[:, dt * 512:(dt + 1) * 512],
                                start=(dt == 0), stop=(dt == DT - 1)),
                                reads=[("kwA", kwp), ("xnbfA", dt)], writes=[bank])
                        tk.op("V", lambda m=m, bank=bank, t0c=t0c: V.tensor_scalar(
                            ktr[:, m * S + t0c: m * S + t0c + 512],
                            PS[bank[1]][:, 0:512], abias[:, DT + m:DT + m + 1],
                            None, ALU.add),
                            reads=[bank, "abias"], writes=[("ktrA", m)])
                    if cc == 0:
                        for dt in range(DT):
                            tk.dma("G", "dQWa" if dt % 2 == 0 else "dQWb",
                                   lambda dt=dt: GP.dma_start(
                                       vwA[:, dt * D:(dt + 1) * D],
                                       Wva[dt * P:(dt + 1) * P, :]),
                                   writes=[("vwA", dt)])
                    for tt in range(4):
                        kt = cc * 4 + tt
                        for half in range(2):
                            bank = psk[2 + (2 * tt + half) % 2]
                            for dt in range(DT):
                                tk.op("T", lambda dt=dt, tt=tt, half=half, bank=bank:
                                      T.matmul(
                                          PS[bank[1]][:, 0:512],
                                          xnbf[:, dt * 512 + tt * P:
                                               dt * 512 + (tt + 1) * P],
                                          vwA[:, dt * D + half * 512:
                                              dt * D + (half + 1) * 512],
                                          start=(dt == 0), stop=(dt == DT - 1)),
                                      reads=[("xnbfA", dt), ("vwA", dt)], writes=[bank])
                            tk.op("V", lambda kt=kt, half=half, bank=bank: V.tensor_copy(
                                vnat[:, kt * D + half * 512: kt * D + (half + 1) * 512],
                                PS[bank[1]][:, 0:512]),
                                reads=[bank], writes=[("vnatA", kt)])

                for qc in range(C):
                    t0c = qc * 512
                    for dt in range(DT):
                        tk.op("V", lambda dt=dt, t0c=t0c: V.tensor_copy(
                            xnbf[:, dt * 512:(dt + 1) * 512], ysl(dt, t0c, 512)),
                            reads=[("ytr", dt)], writes=[("xnbfA", dt)])
                    for m in range(DT):
                        kwp = m % 2
                        tk.dma("G", "dKW0" if kwp == 0 else "dKW1",
                               lambda m=m, kwp=kwp: GP.dma_start(
                                   kwA[:, kwp * D:(kwp + 1) * D]
                                   .rearrange("p (k c) -> p k c", k=DT),
                                   Wqa[:, m * P:(m + 1) * P]
                                   .rearrange("(k p) c -> p k c", p=P)),
                               writes=[("kwA", kwp)])
                        bank = psk[m % 2]
                        for dt in range(DT):
                            tk.op("T", lambda m=m, dt=dt, kwp=kwp, bank=bank: T.matmul(
                                PS[bank[1]][:, 0:512],
                                kwA[:, kwp * D + dt * P: kwp * D + (dt + 1) * P],
                                xnbf[:, dt * 512:(dt + 1) * 512],
                                start=(dt == 0), stop=(dt == DT - 1)),
                                reads=[("kwA", kwp), ("xnbfA", dt)], writes=[bank])
                        tk.op("V", lambda m=m, bank=bank: V.tensor_scalar(
                            qtr[:, m * 512:(m + 1) * 512],
                            PS[bank[1]][:, 0:512], abias[:, m:m + 1], None, ALU.add),
                            reads=[bank, "abias"], writes=[("qtrA", m)])
                    for sg in range(4):
                        tau = qc * 4 + sg
                        for kc in range(C):
                            bank = psk[kc]
                            for m in range(DT):
                                tk.op("T", lambda m=m, kc=kc, sg=sg, bank=bank: T.matmul(
                                    PS[bank[1]][:, 0:512],
                                    qtr[:, m * 512 + sg * P: m * 512 + (sg + 1) * P],
                                    ktr[:, m * S + kc * 512: m * S + (kc + 1) * 512],
                                    start=(m == 0), stop=(m == DT - 1)),
                                    reads=[("qtrA", m), ("ktrA", m)], writes=[bank])
                            tk.op("V", lambda kc=kc, bank=bank: V.reduce_max(
                                rsA[:, kc:kc + 1], PS[bank[1]][:, 0:512],
                                AX.X, negate=True),
                                reads=[bank], writes=["rsneg"])
                        for kc in range(1, C):
                            tk.op("V", lambda kc=kc: V.tensor_tensor(
                                rsA[:, 0:1], rsA[:, 0:1], rsA[:, kc:kc + 1], ALU.min),
                                reads=["rsneg"], writes=["rsneg"])
                        tk.op("V", lambda: V.tensor_scalar(
                            rsA[:, 0:1], rsA[:, 0:1], 1.0 / 32.0, None, ALU.mult),
                            reads=["rsneg"], writes=["rsneg"])
                        for kc in range(C):
                            bank = psk[kc]
                            tk.op("S", lambda kc=kc, bank=bank: SC.activation(
                                pchk[:, kc * 512:(kc + 1) * 512],
                                PS[bank[1]][:, 0:512], AF.Exp,
                                scale=1.0 / 32.0, bias=rsA[:, 0:1],
                                accum_out=rsA[:, C + 1:C + 2]),
                                reads=[bank, "rsneg"],
                                writes=[("pchkA", kc), "rsacc"])
                            if kc == 0:
                                tk.op("V", lambda: V.tensor_copy(
                                    rsA[:, C:C + 1], rsA[:, C + 1:C + 2]),
                                    reads=["rsacc"], writes=["rstot"])
                            else:
                                tk.op("V", lambda: V.tensor_tensor(
                                    rsA[:, C:C + 1], rsA[:, C:C + 1],
                                    rsA[:, C + 1:C + 2], ALU.add),
                                    reads=["rsacc", "rstot"], writes=["rstot"])
                        tk.op("V", lambda: V.reciprocal(rsA[:, C:C + 1], rsA[:, C:C + 1]),
                              reads=["rstot"], writes=["rstot"])
                        ob = obuf[sg % 2]
                        sem = "dXa" if sg % 2 == 0 else "dXb"
                        tk.dma("Y", sem, lambda tau=tau, ob=ob: SY.dma_start(
                            ob[:, :], X[b, tau * P:(tau + 1) * P, :]),
                            writes=[("obA", sg % 2)])
                        for kt in range(C * 4):
                            pbb = pbk[kt % 2]
                            tk.op("T", lambda kt=kt, pbb=pbb: T.transpose(
                                PB[pbb[1]][:, 0:P], pchk[:, kt * P:(kt + 1) * P],
                                idb[:, :]),
                                reads=[("pchkA", kt // 4), "idb"], writes=[pbb])
                            tk.op("V", lambda kt=kt, pbb=pbb: V.tensor_copy(
                                ptr[:, (kt % 2) * P:(kt % 2 + 1) * P],
                                PB[pbb[1]][:, 0:P]),
                                reads=[pbb], writes=[("ptrA", kt % 2)])
                            for half in range(2):
                                bank = psk[4 + half]
                                tk.op("T", lambda kt=kt, half=half, bank=bank: T.matmul(
                                    PS[bank[1]][:, 0:512],
                                    ptr[:, (kt % 2) * P:(kt % 2 + 1) * P],
                                    vnat[:, kt * D + half * 512:
                                         kt * D + (half + 1) * 512],
                                    start=(kt == 0), stop=(kt == C * 4 - 1)),
                                    reads=[("ptrA", kt % 2), ("vnatA", kt)],
                                    writes=[bank])
                        for half in range(2):
                            bank = psk[4 + half]
                            tk.op("V", lambda half=half, bank=bank: V.scalar_tensor_tensor(
                                tmpo[:, half * 512:(half + 1) * 512],
                                PS[bank[1]][:, 0:512], rsA[:, C:C + 1],
                                bvbc[:, half * 512:(half + 1) * 512],
                                ALU.mult, ALU.add),
                                reads=[bank, "rstot", "bvbc"], writes=["tmpoA"])
                        for half in range(2):
                            tk.op("V", lambda half=half, ob=ob: V.tensor_tensor(
                                ob[:, half * 512:(half + 1) * 512],
                                tmpo[:, half * 512:(half + 1) * 512],
                                ob[:, half * 512:(half + 1) * 512], ALU.add),
                                reads=["tmpoA", ("obA", sg % 2)],
                                writes=[("obA", sg % 2)])
                        tk.dma("Y", "dO", lambda tau=tau, ob=ob: SY.dma_start(
                            OUT[b, tau * P:(tau + 1) * P, :], ob[:, :]),
                            reads=[("obA", sg % 2)], writes=[("out", b, tau)])
                tk.barrier()

        def tail(b):
            with ExitStack() as ls:
                ynat = ls.enter_context(nc.sbuf_tensor(f"ynatT{b}", [P, D], F32))
                for tau in range(NT):
                    for dt in range(DT):
                        bank = psk[4 + dt % 2]
                        tk.op("T", lambda dt=dt, tau=tau, bank=bank: T.transpose(
                            PS[bank[1]][:, 0:P], ysl(dt, tau * P, P), idf[:, :]),
                            reads=[("ytr", dt), "idf"], writes=[bank])
                        tk.op("V", lambda dt=dt, bank=bank: V.tensor_copy(
                            ynat[:, dt * P:(dt + 1) * P], PS[bank[1]][:, 0:P]),
                            reads=[bank], writes=["ynatT"])
                    tk.dma("G", "dAcc", lambda tau=tau: GP.dma_start(
                        OUT[b, tau * P:(tau + 1) * P, :], ynat[:, :],
                        accum_op=ALU.add),
                        reads=["ynatT", ("out", b, tau)], writes=[("out", b, tau)])
                tk.barrier()

        for b in range(BPC):
            if y0_input:
                for dt in range(DT):
                    tk.dma("Y", "dXa", lambda dt=dt: SY.dma_start(
                        ytr[:, dt * S:(dt + 1) * S], Y0[b, :, dt * S:(dt + 1) * S]),
                        writes=[("ytr", dt)])
            if not skip_a:
                phase_A(b)
            for l in range(L):
                tk.dma("Y", "dP", lambda l=l: SY.dma_start(prm[:, :], PRM[l]),
                       writes=["prm"])
                tk.dma("Y", "dP", lambda l=l: SY.dma_start(qkr[:, :], QKR[l]),
                       writes=["qkr"])
                if not skip_attn:
                    b_attn(l, b)
                if not skip_ffn:
                    b_ffn(l, b)
            if not skip_tail:
                tail(b)
            for tn in taps:
                if tn.split(":")[0] == "ytr":
                    for dt in range(DT):
                        tk.dma("Y", "dO", lambda dt=dt, tn=tn: SY.dma_start(
                            tap_t[tn][b, :, dt * S:(dt + 1) * S],
                            ytr[:, dt * S:(dt + 1) * S]),
                            reads=[("ytr", dt)])

        for sname in ["dO", "dAcc"]:
            if tk.dcnt[sname] > 0:
                SY.wait_ge(tk.dsem[sname], tk.dcnt[sname])
    return nc


_NC_CACHE = {}


def build_full():
    if "nc" not in _NC_CACHE:
        _NC_CACHE["nc"] = build_program(FULL_B // N_CORES, FULL_S, FULL_L)
    return _NC_CACHE["nc"]


def kernel(**inputs):
    S, L = FULL_S, FULL_L
    BPC = FULL_B // N_CORES
    hc = host_constants(S)
    prm, qkr = pack_params(inputs["L0_ffn_q_scale"], inputs["L0_ffn_k_scale"],
                           inputs["L0_ffn_ln_g"], inputs["L0_ffn_ln_b"],
                           inputs["L0_ffn_b1"], inputs["L0_ffn_b2"])
    abias = np.stack([np.asarray(inputs["L0_attn_b_q"]).reshape(DT, P).T,
                      np.asarray(inputs["L0_attn_b_k"]).reshape(DT, P).T]
                     ).astype(np.float32)
    X = np.ascontiguousarray(np.asarray(inputs["L0_attn_X"], dtype=np.float32))
    common = {
        "Wqa": np.ascontiguousarray(np.asarray(inputs["L0_attn_W_q"], np.float32)),
        "Wka": np.ascontiguousarray(np.asarray(inputs["L0_attn_W_k"], np.float32)),
        "Wva": np.ascontiguousarray(np.asarray(inputs["L0_attn_W_v"], np.float32)),
        "BV": np.ascontiguousarray(np.asarray(inputs["L0_attn_b_v"], np.float32)),
        "ABIAS": abias,
        "WQKV": np.ascontiguousarray(np.asarray(inputs["L0_ffn_Wqkv"], np.float32)),
        "WO": np.ascontiguousarray(np.asarray(inputs["L0_ffn_Wo"], np.float32)),
        "W1": np.ascontiguousarray(np.asarray(inputs["L0_ffn_W1"], np.float32)),
        "W2": np.ascontiguousarray(np.asarray(inputs["L0_ffn_W2"], np.float32)),
        "PRM": prm, "QKR": qkr,
        "ROT": hc["rotab"], "MSK": hc["masks"], "IDB": hc["ident_bf"],
        "IDF": hc["ident_f32"], "BLK": hc["blk64"], "ONEC": hc["ones_col"],
        "ONRF": hc["ones_row_f32"], "ROTR": hc["rotR"],
    }
    nc = build_full()
    in_maps = []
    for c in range(N_CORES):
        m = dict(common)
        m["X"] = np.ascontiguousarray(X[c * BPC:(c + 1) * BPC])
        in_maps.append(m)
    res = run_bass_kernel_spmd(nc, in_maps, core_ids=list(range(N_CORES)))
    out = np.concatenate([np.asarray(res.results[c]["OUT"]) for c in range(N_CORES)],
                         axis=0)
    return out.astype(np.float32)


# version 09
# speedup vs baseline: 1.1528x; 1.0265x over previous
import bisect
import numpy as np
import ml_dtypes
import concourse.bass as bass
import concourse.mybir as mybir
from concourse.bass_utils import run_bass_kernel_spmd
from contextlib import ExitStack

F32 = mybir.dt.float32
BF16 = mybir.dt.bfloat16
AF = mybir.ActivationFunctionType
ALU = mybir.AluOpType
AX = mybir.AxisListType

P = 128
D = 1024
DT = D // P
HEADS = 16
DH = 64
WIN = 256
FH = 2730
NOG = (FH + P - 1) // P
NEG = -1e30
EPS = 1e-5

PC_G = 0
PC_B = 8
PC_B1A = 16
PC_B1G = 16 + NOG
PC_B2 = 16 + 2 * NOG
PC_QS = 24 + 2 * NOG
PC_KS = 25 + 2 * NOG
NPARAM = 26 + 2 * NOG

N_CORES = 8
ATTN_STAGE = 9
ATTN_SUB = 9
FULL_B = 16
FULL_S = 2048
FULL_L = 6


def host_constants(S):
    half = np.arange(0, DH, 2, dtype=np.float64)
    inv_freq = 1.0 / 10000.0 ** (half / DH)
    t = np.arange(S, dtype=np.float64)
    freqs = np.concatenate([t[:, None] * inv_freq[None, :]] * 2, -1)
    base_scale = (half + 0.4 * DH) / (1.4 * DH)
    power = (t - S // 2) / (WIN // 2)
    scale = np.concatenate([base_scale[None, :] ** power[:, None]] * 2, -1)
    cos, sin = np.cos(freqs), np.sin(freqs)
    inv = 1.0 / scale
    sgn = np.concatenate([-np.ones(32), np.ones(32)])[None, :]

    def two(v):
        return np.concatenate([v.T, v.T], axis=0).astype(np.float32)

    rotab = np.stack([two(cos * scale), two(sin * scale * sgn),
                      two(cos * inv), two(sin * inv * sgn)])

    masks = np.full((2, P, 512), NEG, dtype=np.float32)
    for js in range(2):
        jr = np.arange(P)[:, None] + js * P
        qc = np.arange(512)[None, :]
        masks[js][(qc >= jr) & (qc <= jr + WIN)] = 0.0
    masks = masks.astype(ml_dtypes.bfloat16)

    blk = np.zeros((P, P))
    blk[:DH, :DH] = 1.0
    blk[DH:, DH:] = 1.0
    rotR = np.zeros((P, P))
    for d in range(P):
        base, r = d - d % DH, d % DH
        src = base + (r + 32 if r < 32 else r - 32)
        rotR[src, d] = 1.0
    return dict(rotab=rotab, masks=masks,
                ident_bf=np.eye(P, dtype=ml_dtypes.bfloat16),
                ident_f32=np.eye(P, dtype=np.float32),
                blk64=blk.astype(ml_dtypes.bfloat16),
                rotR=rotR.astype(ml_dtypes.bfloat16),
                ones_col=np.ones((P, 1), dtype=ml_dtypes.bfloat16),
                ones_row_f32=np.ones((1, P), dtype=np.float32))


def pack_params(q_scale, k_scale, ln_g, ln_b, b1, b2):
    L = ln_g.shape[0]
    cols, rows = [], []
    for l in range(L):
        g = ln_g[l].reshape(DT, P).T
        bb = ln_b[l].reshape(DT, P).T
        b1a = np.zeros((P, NOG), np.float32)
        b1g = np.zeros((P, NOG), np.float32)
        for og in range(NOG):
            n = min(P, FH - og * P)
            b1a[:n, og] = b1[l, og * P: og * P + n]
            b1g[:n, og] = b1[l, FH + og * P: FH + og * P + n]
        bb2 = b2[l].reshape(DT, P).T
        qs = np.concatenate([q_scale[l], q_scale[l]])[:, None]
        ks = np.concatenate([k_scale[l], k_scale[l]])[:, None]
        cols.append(np.concatenate([g, bb, b1a, b1g, bb2, qs, ks], 1).astype(np.float32))
        rows.append(np.concatenate([q_scale[l], k_scale[l]])[None, :].astype(np.float32))
    return np.stack(cols), np.stack(rows)


class Tk:

    def __init__(self, nc, stack, dma_sems):
        self.nc = nc
        self.eng = {"T": nc.tensor, "V": nc.vector, "S": nc.scalar,
                    "Y": nc.sync, "G": nc.gpsimd}
        self.esem = {}
        self.seq = {}
        self.instrs = {}
        self.incs = {}
        for k in self.eng:
            self.esem[k] = stack.enter_context(nc.semaphore("se" + k))
            self.seq[k] = 0
            self.instrs[k] = []
            self.incs[k] = []
        self.dsem = {}
        self.dcnt = {}
        for name in dma_sems:
            self.dsem[name] = stack.enter_context(nc.semaphore(name))
            self.dcnt[name] = 0
        self.buf = {}
        self.waited = {}

    def _inc_val(self, ekey, seq):
        incs = self.incs[ekey]
        if incs and incs[-1][0] >= seq:
            i = bisect.bisect_left(incs, (seq, -1))
            return incs[i][1]
        val = (incs[-1][1] + 1) if incs else 1
        self.instrs[ekey][seq - 1].then_inc(self.esem[ekey], 1)
        incs.append((seq, val))
        return val

    def _emit_waits(self, waiter, sources):
        need = {}
        for src in sources:
            if src is None:
                continue
            if src[0] == "E":
                _, ekey, seq = src
                if ekey == waiter == "T":
                    continue
                v = self._inc_val(ekey, seq)
                key = ("E", ekey)
                need[key] = max(need.get(key, 0), v)
            else:
                _, sname, cnt = src
                key = ("D", sname)
                need[key] = max(need.get(key, 0), self.dcnt[sname])
        for key, val in sorted(need.items()):
            if self.waited.get((waiter, key), 0) >= val:
                continue
            sem = self.esem[key[1]] if key[0] == "E" else self.dsem[key[1]]
            self.eng[waiter].wait_ge(sem, val)
            self.waited[(waiter, key)] = val

    def _gather(self, reads, writes):
        srcs = []
        for k in reads:
            b = self.buf.get(k)
            if b and b["w"]:
                srcs.append(b["w"])
        for k in writes:
            b = self.buf.get(k)
            if b:
                if b["w"]:
                    srcs.append(b["w"])
                srcs.extend(b["r"].values())
        return srcs

    def _update(self, reads, writes, src):
        for k in writes:
            self.buf[k] = {"w": src, "r": {}}
        for k in reads:
            b = self.buf.setdefault(k, {"w": None, "r": {}})
            b["r"][src[:2] if src[0] == "E" else ("D", src[1])] = src

    @staticmethod
    def _is_psum(k):
        return isinstance(k, tuple) and len(k) == 2 and k[0] in ("ps", "pb")

    def op(self, ekey, fn, reads=(), writes=()):
        srcs = self._gather(reads, writes)
        for k in reads:
            if self._is_psum(k):
                b = self.buf.get(k)
                if b:
                    for rk, v in b["r"].items():
                        if rk[0] == "E" and rk[1] != ekey:
                            srcs.append(v)
        self._emit_waits(ekey, srcs)
        ins = fn()
        self.seq[ekey] += 1
        self.instrs[ekey].append(ins)
        src = ("E", ekey, self.seq[ekey])
        self._update(reads, writes, src)
        return ins

    def dma(self, ekey, sname, fn, reads=(), writes=()):
        if self.dcnt[sname] > 0:
            key = ("D", sname)
            if self.waited.get((ekey, key), 0) < self.dcnt[sname]:
                self.eng[ekey].wait_ge(self.dsem[sname], self.dcnt[sname])
                self.waited[(ekey, key)] = self.dcnt[sname]
        self._emit_waits(ekey, self._gather(reads, writes))
        ins = fn()
        self.dcnt[sname] += 16
        ins.then_inc(self.dsem[sname], 16)
        src = ("D", sname, self.dcnt[sname])
        self._update(reads, writes, src)
        return ins

    def barrier_read(self, ekey, keys):
        self._emit_waits(ekey, self._gather(keys, ()))

    def barrier(self):
        srcs = []
        for ekey in self.eng:
            if self.seq[ekey] > 0:
                srcs.append(("E", ekey, self.seq[ekey]))
        for sname, cnt in self.dcnt.items():
            if cnt > 0:
                srcs.append(("D", sname, cnt))
        for waiter in self.eng:
            self._emit_waits(waiter, [s for s in srcs
                                      if not (s[0] == "E" and s[1] == waiter)])


DMA_SEMS = ["dW", "dP", "dTa", "dTb", "dXa", "dXb", "dQWa", "dQWb", "dKW0", "dKW1",
            "dVW0", "dVW1", "dWO0", "dWO1", "dW1a", "dW1b", "dW2a", "dW2b", "dO", "dAcc"]


def build_program(BPC, S, L, taps=(), skip_a=False, skip_attn=False, skip_ffn=False,
                  skip_tail=False, y0_input=False):
    C = S // 512
    NW = S // WIN
    NT = S // P
    HT = S // 2
    NU = HT // 512
    nc = bass.Bass("TRN2", target_bir_lowering=False)

    X = nc.dram_tensor("X", [BPC, S, D], F32, kind="ExternalInput").ap()
    OUT = nc.dram_tensor("OUT", [BPC, S, D], F32, kind="ExternalOutput").ap()
    Wqa = nc.dram_tensor("Wqa", [D, D], F32, kind="ExternalInput").ap()
    Wka = nc.dram_tensor("Wka", [D, D], F32, kind="ExternalInput").ap()
    Wva = nc.dram_tensor("Wva", [D, D], F32, kind="ExternalInput").ap()
    BV = nc.dram_tensor("BV", [D], F32, kind="ExternalInput").ap()
    ABIAS = nc.dram_tensor("ABIAS", [2, P, DT], F32, kind="ExternalInput").ap()
    WQKV = nc.dram_tensor("WQKV", [L, D, 3 * D], F32, kind="ExternalInput").ap()
    WO = nc.dram_tensor("WO", [L, D, D], F32, kind="ExternalInput").ap()
    W1 = nc.dram_tensor("W1", [L, D, 2 * FH], F32, kind="ExternalInput").ap()
    W2 = nc.dram_tensor("W2", [L, FH, D], F32, kind="ExternalInput").ap()
    PRM = nc.dram_tensor("PRM", [L, P, NPARAM], F32, kind="ExternalInput").ap()
    QKR = nc.dram_tensor("QKR", [L, 1, 2 * DH], F32, kind="ExternalInput").ap()
    ROT = nc.dram_tensor("ROT", [4, P, S], F32, kind="ExternalInput").ap()
    MSK = nc.dram_tensor("MSK", [2, P, 512], BF16, kind="ExternalInput").ap()
    IDB = nc.dram_tensor("IDB", [P, P], BF16, kind="ExternalInput").ap()
    IDF = nc.dram_tensor("IDF", [P, P], F32, kind="ExternalInput").ap()
    BLK = nc.dram_tensor("BLK", [P, P], BF16, kind="ExternalInput").ap()
    ONEC = nc.dram_tensor("ONEC", [P, 1], BF16, kind="ExternalInput").ap()
    ONRF = nc.dram_tensor("ONRF", [1, P], F32, kind="ExternalInput").ap()
    ROTR = nc.dram_tensor("ROTR", [P, P], BF16, kind="ExternalInput").ap()
    if y0_input:
        Y0 = nc.dram_tensor("Y0", [BPC, P, DT * S], F32, kind="ExternalInput").ap()
    tap_t = {}
    for tn in taps:
        tap_t[tn] = nc.dram_tensor("tap_" + tn.replace(":", "_"),
                                   [BPC, P, DT * S], F32, kind="ExternalOutput").ap()

    stack = ExitStack()
    with stack:
        tk = Tk(nc, stack, DMA_SEMS)
        T, V, SC, SY, GP = nc.tensor, nc.vector, nc.scalar, nc.sync, nc.gpsimd

        def sb(name, shape, dtype):
            return stack.enter_context(nc.sbuf_tensor(name, shape, dtype))

        ytr = sb("ytr", [P, DT * S], F32)
        idb = sb("idb", [P, P], BF16)
        idf = sb("idf", [P, P], F32)
        blk64 = sb("blk64", [P, P], BF16)
        onec = sb("onec", [P, 1], BF16)
        onrf = sb("onrf", [1, P], F32)
        prm = sb("prm", [P, NPARAM], F32)
        qkr = sb("qkr", [1, 2 * DH], F32)
        abias = sb("abias", [P, 2 * DT], F32)
        negb = sb("negb", [P, 1], F32)
        srow = sb("srow", [1, 4 * 512], F32)
        bvbc = sb("bvbc", [P, D], F32)
        bndry = sb("bndry", [P, DT], F32)
        epsc = sb("epsc", [P, 1], F32)
        rotr = sb("rotr", [P, P], BF16)
        blkq = sb("blkq", [P, 2 * P], BF16)
        qscol = sb("qscol", [P, 4], F32)

        PS = [stack.enter_context(nc.psum_tensor(f"ps{i}", [P, 512], F32)) for i in range(6)]
        PB = [stack.enter_context(nc.psum_tensor(f"pb{i}", [P, 1024], BF16)) for i in range(2)]
        psk = [("ps", i) for i in range(6)]
        pbk = [("pb", i) for i in range(2)]

        for dst, src_, key in [(idb[:, :], IDB, "idb"), (idf[:, :], IDF, "idf"),
                               (blk64[:, :], BLK, "blk"), (onec[:, :], ONEC, "onec"),
                               (onrf[:, :], ONRF, "onrf"),
                               (abias[:, 0:DT], ABIAS[0], "abias"),
                               (abias[:, DT:2 * DT], ABIAS[1], "abias"),
                               (rotr[:, :], ROTR, "rotr"),
                               (bvbc[:, :], BV[None, :].broadcast_to([P, D]), "bvbc")]:
            tk.dma("Y", "dW", lambda d=dst, s=src_: SY.dma_start(d, s), writes=[key])

        tk.op("V", lambda: V.memset(epsc[:, :], EPS), writes=["epsc"])

        def ysl(dt, t0, n):
            return ytr[:, dt * S + t0: dt * S + t0 + n]

        def b_ffn(l, b):
            with ExitStack() as ls:
                def lsb(n, sh, dt_):
                    return ls.enter_context(nc.sbuf_tensor(f"{n}_l{l}b{b}", sh, dt_))
                ybf = lsb("fybf", [P, NU * DT * 513], BF16)
                sqb = lsb("fsq", [P, 512], BF16)
                xnbf = lsb("fxn", [P, DT * HT], BF16)
                hbuf = lsb("fh", [P, NOG * HT], BF16)
                w1sl = [lsb(f"fw1{i}", [P, D], BF16) for i in range(4)]
                tbuf = lsb("ftmp", [P, 3 * 512], F32)
                fsrow = lsb("fsrow", [1, 1536], F32)
                w2sl = [lsb(f"fw2{i}", [P, NOG * P], BF16) for i in range(2)]

                for hb in range(2):
                    T0 = hb * HT

                    def shsl(u, dt):
                        off = 0 if dt >= 4 else 1
                        return ybf[:, (u * DT + dt) * 513 + off:
                                   (u * DT + dt) * 513 + off + 512]

                    for u in range(NU):
                        t0 = T0 + u * 512
                        for dt in range(DT):
                            if t0 == 0:
                                tk.op("V", lambda u=u, dt=dt: V.memset(
                                    ybf[:, (u * DT + dt) * 513:(u * DT + dt) * 513 + 1],
                                    0.0), writes=[("ybf", u, dt)])
                            elif dt >= 4 and t0 == HT:
                                tk.op("V", lambda u=u, dt=dt: V.tensor_copy(
                                    ybf[:, (u * DT + dt) * 513:(u * DT + dt) * 513 + 1],
                                    bndry[:, dt:dt + 1]),
                                    reads=[("bnd", dt)], writes=[("ybf", u, dt)])
                            else:
                                tk.op("V", lambda u=u, dt=dt, t0=t0: V.tensor_copy(
                                    ybf[:, (u * DT + dt) * 513:(u * DT + dt) * 513 + 1],
                                    ysl(dt, t0 - 1, 1)),
                                    reads=[("ytr", dt)], writes=[("ybf", u, dt)])
                            if dt % 2 == 1:
                                tk.op("S", lambda u=u, dt=dt, t0=t0: SC.activation(
                                    ybf[:, (u * DT + dt) * 513 + 1:(u * DT + dt + 1) * 513],
                                    ysl(dt, t0, 512), AF.Copy),
                                    reads=[("ytr", dt)], writes=[("ybf", u, dt)])
                            else:
                                tk.op("V", lambda u=u, dt=dt, t0=t0: V.tensor_copy(
                                    ybf[:, (u * DT + dt) * 513 + 1:(u * DT + dt + 1) * 513],
                                    ysl(dt, t0, 512)),
                                    reads=[("ytr", dt)], writes=[("ybf", u, dt)])
                    for u in range(NU):
                        mb_ = psk[0 + 2 * u]
                        sb_ = psk[1 + 2 * u]
                        for dt in range(DT):
                            tk.op("T", lambda u=u, dt=dt, mb_=mb_: T.matmul(
                                PS[mb_[1]][0:1, 0:512], onec[:, :], shsl(u, dt),
                                start=(dt == 0), stop=(dt == DT - 1)),
                                reads=[("ybf", u, dt), "onec"], writes=[mb_])
                        for dt in range(DT):
                            tk.op("S", lambda u=u, dt=dt: SC.activation(
                                sqb[:, :], shsl(u, dt), AF.Square),
                                reads=[("ybf", u, dt)], writes=["sqb"])
                            tk.op("T", lambda u=u, dt=dt, sb_=sb_: T.matmul(
                                PS[sb_[1]][0:1, 0:512], onec[:, :], sqb[:, :],
                                start=(dt == 0), stop=(dt == DT - 1)),
                                reads=["sqb", "onec"], writes=[sb_])
                    MCOL = [0, 512]
                    VCOL = [1024, 0]
                    for u in range(NU):
                        mb_ = psk[0 + 2 * u]
                        sb_ = psk[1 + 2 * u]
                        mc, vc = MCOL[u], VCOL[u]
                        tk.op("V", lambda u=u, mb_=mb_, mc=mc: V.tensor_scalar(
                            fsrow[:, mc:mc + 512], PS[mb_[1]][0:1, 0:512],
                            1.0 / D, None, ALU.mult),
                            reads=[mb_], writes=[("srw", mc // 512)])
                        tk.op("V", lambda u=u, mc=mc, vc=vc: V.tensor_tensor(
                            fsrow[:, vc:vc + 512], fsrow[:, mc:mc + 512],
                            fsrow[:, mc:mc + 512], ALU.mult),
                            reads=[("srw", mc // 512)], writes=[("srw", vc // 512)])
                        tk.op("V", lambda u=u, sb_=sb_, vc=vc: V.scalar_tensor_tensor(
                            fsrow[:, vc:vc + 512], PS[sb_[1]][0:1, 0:512],
                            1.0 / D, fsrow[:, vc:vc + 512],
                            ALU.mult, ALU.subtract),
                            reads=[sb_, ("srw", vc // 512)], writes=[("srw", vc // 512)])
                        mbb = psk[4 + u]
                        tk.op("T", lambda u=u, mc=mc, mbb=mbb: T.matmul(
                            PS[mbb[1]][:, 0:512], onrf[:, :], fsrow[:, mc:mc + 512],
                            start=True, stop=True),
                            reads=[("srw", mc // 512), "onrf"], writes=[mbb])
                    for u in range(NU):
                        vc = VCOL[u]
                        tk.op("S", lambda u=u, vc=vc: SC.activation(
                            fsrow[:, vc:vc + 512], fsrow[:, vc:vc + 512],
                            AF.Ln, bias=epsc[0:1, 0:1]),
                            reads=[("srw", vc // 512), "epsc"], writes=[("srw", vc // 512)])
                        tk.op("S", lambda u=u, vc=vc: SC.activation(
                            fsrow[:, vc:vc + 512], fsrow[:, vc:vc + 512],
                            AF.Exp, scale=-0.5),
                            reads=[("srw", vc // 512)], writes=[("srw", vc // 512)])
                    for u in range(NU):
                        vc = VCOL[u]
                        sbb = psk[0 + u]
                        tk.op("T", lambda u=u, vc=vc, sbb=sbb: T.matmul(
                            PS[sbb[1]][:, 0:512], onrf[:, :], fsrow[:, vc:vc + 512],
                            start=True, stop=True),
                            reads=[("srw", vc // 512), "onrf"], writes=[sbb])
                    for u in range(NU):
                        mbb = psk[4 + u]
                        sbb = psk[0 + u]
                        for dt in range(DT):
                            sl = 1 + (u * DT + dt) % 2
                            tk.op("V", lambda u=u, dt=dt, mbb=mbb: V.tensor_tensor(
                                tbuf[:, 0:512], shsl(u, dt), PS[mbb[1]][:, 0:512],
                                ALU.subtract),
                                reads=[("ybf", u, dt), mbb], writes=[("tb", 0)])
                            tk.op("V", lambda sl=sl, sbb=sbb: V.tensor_tensor(
                                tbuf[:, sl * 512:(sl + 1) * 512], tbuf[:, 0:512],
                                PS[sbb[1]][:, 0:512], ALU.mult),
                                reads=[("tb", 0), sbb], writes=[("tb", sl)])
                            tk.op("S", lambda dt=dt, u=u, sl=sl: SC.activation(
                                xnbf[:, dt * HT + u * 512: dt * HT + u * 512 + 512],
                                tbuf[:, sl * 512:(sl + 1) * 512], AF.Identity,
                                scale=prm[:, PC_G + dt:PC_G + dt + 1],
                                bias=prm[:, PC_B + dt:PC_B + dt + 1]),
                                reads=[("tb", sl), "prm"], writes=[("xnbf", dt)])

                    def w1_load(og, bi):
                        sname = "dW1a" if bi == 0 else "dW1b"
                        c0a, c0g = og * P, FH + og * P
                        n = min(P, FH - og * P)
                        tk.dma("G", sname, lambda: GP.dma_start(
                            w1sl[bi][:, :].rearrange("p (k c) -> p k c", k=DT)[:, :, 0:n],
                            W1[l, :, c0a:c0a + n].rearrange("(k p) c -> p k c", p=P)),
                            writes=[("w1a", bi)])
                        tk.dma("G", sname, lambda: GP.dma_start(
                            w1sl[bi + 2][:, :].rearrange("p (k c) -> p k c", k=DT)[:, :, 0:n],
                            W1[l, :, c0g:c0g + n].rearrange("(k p) c -> p k c", p=P)),
                            writes=[("w1g", bi)])

                    w1_load(0, 0)
                    if NOG > 1:
                        w1_load(1, 1)
                    for og in range(NOG):
                        n = min(P, FH - og * P)
                        bi = og % 2
                        rot6 = (og * 2 * NU) % 6
                        pa = [psk[(rot6 + j) % 6] for j in range(NU)]
                        pg = [psk[(rot6 + NU + j) % 6] for j in range(NU)]
                        wa = w1sl[bi][:, :].rearrange("p (k c) -> p k c", k=DT)
                        wg = w1sl[bi + 2][:, :].rearrange("p (k c) -> p k c", k=DT)
                        for u in range(NU):
                            for dt in range(DT):
                                tk.op("T", lambda u=u, dt=dt, n=n, wa=wa: T.matmul(
                                    PS[pa[u][1]][0:n, 0:512], wa[:, dt, 0:n],
                                    xnbf[:, dt * HT + u * 512: dt * HT + (u + 1) * 512],
                                    start=(dt == 0), stop=(dt == DT - 1)),
                                    reads=[("w1a", bi), ("xnbf", dt)], writes=[pa[u]])
                        for u in range(NU):
                            for dt in range(DT):
                                tk.op("T", lambda u=u, dt=dt, n=n, wg=wg: T.matmul(
                                    PS[pg[u][1]][0:n, 0:512], wg[:, dt, 0:n],
                                    xnbf[:, dt * HT + u * 512: dt * HT + (u + 1) * 512],
                                    start=(dt == 0), stop=(dt == DT - 1)),
                                    reads=[("w1g", bi), ("xnbf", dt)], writes=[pg[u]])
                        if og + 2 < NOG:
                            w1_load(og + 2, bi)
                        for u in range(NU):
                            tk.op("S", lambda u=u, og=og, n=n: SC.activation(
                                tbuf[0:n, u * 512:(u + 1) * 512],
                                PS[pg[u][1]][0:n, 0:512], AF.Gelu,
                                bias=prm[0:n, PC_B1G + og:PC_B1G + og + 1]),
                                reads=[pg[u], "prm"], writes=[("tb", u)])
                            tk.op("V", lambda u=u, og=og, n=n: V.scalar_tensor_tensor(
                                hbuf[0:n, og * HT + u * 512: og * HT + (u + 1) * 512],
                                PS[pa[u][1]][0:n, 0:512],
                                prm[0:n, PC_B1A + og:PC_B1A + og + 1],
                                tbuf[0:n, u * 512:(u + 1) * 512], ALU.add, ALU.mult),
                                reads=[pa[u], ("tb", u), "prm"], writes=[("h", og)])

                    if hb == 0:
                        for dt in range(4, DT):
                            tk.op("V", lambda dt=dt: V.tensor_copy(
                                bndry[:, dt:dt + 1], ysl(dt, HT - 1, 1)),
                                reads=[("ytr", dt)], writes=[("bnd", dt)])

                    for do in range(DT):
                        w2p = do % 2
                        tk.dma("G", "dW2a" if w2p == 0 else "dW2b",
                               lambda do=do, w2p=w2p: GP.dma_start(
                            w2sl[w2p][:, 0:(NOG - 1) * P]
                            .rearrange("p (g c) -> p g c", g=NOG - 1),
                            W2[l, 0:(NOG - 1) * P, do * P:(do + 1) * P]
                            .rearrange("(g p) c -> p g c", p=P)),
                            writes=[("w2sl", w2p)])
                        tk.dma("G", "dW2a" if w2p == 0 else "dW2b",
                               lambda do=do, w2p=w2p: GP.dma_start(
                            w2sl[w2p][0:FH - (NOG - 1) * P, (NOG - 1) * P:NOG * P],
                            W2[l, (NOG - 1) * P:FH, do * P:(do + 1) * P]),
                            writes=[("w2sl", w2p)])
                        pu = [psk[(do % 2) * 2], psk[(do % 2) * 2 + 1]][:NU]
                        for u in range(NU):
                            for og in range(NOG):
                                n = min(P, FH - og * P)
                                tk.op("T", lambda u=u, og=og, do=do, n=n, w2p=w2p: T.matmul(
                                    PS[pu[u][1]][:, 0:512],
                                    w2sl[w2p][0:n, og * P:og * P + P],
                                    hbuf[0:n, og * HT + u * 512: og * HT + (u + 1) * 512],
                                    start=(og == 0), stop=(og == NOG - 1)),
                                    reads=[("w2sl", w2p), ("h", og)], writes=[pu[u]])
                        for u in range(NU):
                            tk.op("V", lambda u=u, do=do, T0=T0: V.scalar_tensor_tensor(
                                ysl(do, T0 + u * 512, 512), PS[pu[u][1]][:, 0:512],
                                prm[:, PC_B2 + do:PC_B2 + do + 1],
                                ysl(do, T0 + u * 512, 512), ALU.add, ALU.add),
                                reads=[pu[u], "prm", ("ytr", do)], writes=[("ytr", do)])
                tk.barrier()

        def b_attn(l, b):
            with ExitStack() as ls:
                def lsb(n, sh, dt_):
                    return ls.enter_context(nc.sbuf_tensor(f"{n}_l{l}b{b}", sh, dt_))
                aybf = lsb("aybf", [P, DT * 513], BF16)
                qfin = lsb("aqf", [P, DT * 768], BF16)
                kfin = lsb("akf", [P, DT * 768], BF16)
                vrol = lsb("avr", [P, 8 * 1040], BF16)
                tk.op("V", lambda: V.memset(
                    vrol[:, :].rearrange("p (s h c) -> p (s h) c", s=8, h=16)[:, :, 64:65],
                    1.0), writes=[("vr", s) for s in range(8)] if False else ["vrones"])
                ptile = lsb("apt", [P, 4 * 2 * 512], BF16)
                sbf2 = lsb("asbf2", [P, 1536], BF16)
                onat = sbf2[:, 0:1024]
                tt1b_ = lsb("att1", [P, 1024], F32)
                rnbb_ = lsb("arnb", [P, 512], F32)
                msb = lsb("amsb", [P, 2 * 512], F32)
                rinv = lsb("arinv", [P, 16], F32)
                opas = lsb("aopas", [P, DT * 512], BF16)
                W = {
                    "qw": lsb("wqw", [P, 2 * D], BF16),
                    "kwb": lsb("wkwb", [P, 2 * D], BF16),
                    "vw": lsb("wvw", [P, DT * D], BF16),
                    "wos": lsb("wwos", [P, 2 * D], BF16),
                }
                tabs = lsb("atabs", [P, 2 * 512], F32)
                masks = lsb("amasks", [P, 2 * 512], BF16)
                tk.dma("Y", "dW", lambda: SY.dma_start(masks[:, 0:512], MSK[0]),
                       writes=["msk"])
                tk.dma("Y", "dW", lambda: SY.dma_start(masks[:, 512:1024], MSK[1]),
                       writes=["msk"])
                opart = lsb("aopart", [P, 2 * 2 * 1040], F32)

                for dt in range(DT):
                    tk.dma("G", "dQWa" if dt % 2 == 0 else "dQWb", lambda dt=dt: GP.dma_start(
                        W["vw"][:, dt * D:(dt + 1) * D],
                        WQKV[l, dt * P:(dt + 1) * P, 2 * D:3 * D]),
                        writes=[("vw", dt)])

                tk.op("V", lambda: V.reduce_max(srow[:, 0:1], qkr[:, 0:DH], AX.X,
                                                apply_absolute_value=True),
                      reads=["qkr"], writes=[("srow", 0)])
                tk.op("V", lambda: V.reduce_max(srow[:, 1:2], qkr[:, DH:2 * DH], AX.X,
                                                apply_absolute_value=True),
                      reads=["qkr"], writes=[("srow", 0)])
                tk.op("V", lambda: V.tensor_tensor(srow[:, 2:3], srow[:, 0:1],
                                                   srow[:, 1:2], ALU.mult),
                      reads=[("srow", 0)], writes=[("srow", 0)])
                tk.op("T", lambda: T.matmul(PS[0][:, 0:1], onrf[:, :], srow[:, 2:3],
                                            start=True, stop=True),
                      reads=[("srow", 0), "onrf"], writes=[psk[0]])
                tk.op("V", lambda: V.tensor_scalar(negb[:, :], PS[0][:, 0:1],
                                                   -16.0, None, ALU.mult),
                      reads=[psk[0]], writes=["negb"])

                tk.op("V", lambda: V.tensor_copy(qscol[:, 0:1], prm[:, PC_QS:PC_QS + 1]),
                      reads=["prm"], writes=["qsc"])
                tk.op("V", lambda: V.tensor_copy(qscol[:, 1:2], prm[:, PC_KS:PC_KS + 1]),
                      reads=["prm"], writes=["qsc"])
                tk.op("V", lambda: V.tensor_tensor(qscol[:, 2:3], qscol[:, 0:1],
                                                   qscol[:, 0:1], ALU.mult),
                      reads=["qsc"], writes=["qsc"])
                tk.op("V", lambda: V.tensor_tensor(qscol[:, 3:4], qscol[:, 1:2],
                                                   qscol[:, 1:2], ALU.mult),
                      reads=["qsc"], writes=["qsc"])
                tk.op("V", lambda: V.reciprocal(qscol[:, 2:3], qscol[:, 2:3]),
                      reads=["qsc"], writes=["qsc"])
                tk.op("V", lambda: V.reciprocal(qscol[:, 3:4], qscol[:, 3:4]),
                      reads=["qsc"], writes=["qsc"])
                tk.op("V", lambda: V.tensor_scalar(blkq[:, 0:P], blk64[:, :],
                                                   qscol[:, 2:3], None, ALU.mult),
                      reads=["qsc", "blk"], writes=[("blkq", 0)])
                tk.op("V", lambda: V.tensor_scalar(blkq[:, P:2 * P], blk64[:, :],
                                                   qscol[:, 3:4], None, ALU.mult),
                      reads=["qsc", "blk"], writes=[("blkq", 1)])

                def qf_read(side_buf, dt, t0, n):
                    s0 = (t0 // 256) % 3
                    s1 = (s0 + 1) % 3
                    if n <= 256 or s1 == s0 + 1:
                        return [(side_buf[:, dt * 768 + s0 * 256:
                                          dt * 768 + s0 * 256 + n], 0, n)]
                    return [(side_buf[:, dt * 768 + s0 * 256:
                                      dt * 768 + s0 * 256 + 256], 0, 256),
                            (side_buf[:, dt * 768 + s1 * 256:
                                      dt * 768 + s1 * 256 + (n - 256)], 256, n - 256)]

                for c in range(C + 1):
                    if ATTN_STAGE < 2:
                        continue
                    if c < C:
                        t0c = c * 512

                        def absl(dt, j0, n):
                            off = 0 if dt >= 4 else 1
                            return aybf[:, dt * 513 + off + j0: dt * 513 + off + j0 + n]

                        for dt in range(DT):
                            if t0c == 0:
                                tk.op("V", lambda dt=dt: V.memset(
                                    aybf[:, dt * 513:dt * 513 + 1], 0.0),
                                    writes=[("aybf", dt)])
                            else:
                                tk.op("V", lambda dt=dt: V.tensor_copy(
                                    aybf[:, dt * 513:dt * 513 + 1], ysl(dt, t0c - 1, 1)),
                                    reads=[("ytr", dt)], writes=[("aybf", dt)])
                            tk.op("V", lambda dt=dt: V.tensor_copy(
                                aybf[:, dt * 513 + 1:(dt + 1) * 513], ysl(dt, t0c, 512)),
                                reads=[("ytr", dt)], writes=[("aybf", dt)])

                        for tt in range(4):
                            tslot = ((t0c + tt * P) // P) % 8
                            for half in range(2):
                                bank = psk[4 + (2 * tt + half) % 2]
                                for dt in range(DT):
                                    tk.op("T", lambda dt=dt, tt=tt, half=half, bank=bank:
                                          T.matmul(
                                              PS[bank[1]][:, 0:512],
                                              absl(dt, tt * P, P),
                                              W["vw"][:, dt * D + half * 512:
                                                 dt * D + (half + 1) * 512],
                                              start=(dt == 0), stop=(dt == DT - 1)),
                                          reads=[("aybf", dt), ("vw", dt)],
                                          writes=[bank])
                                tk.op("V", lambda half=half, bank=bank, tslot=tslot:
                                      V.tensor_copy(
                                          vrol[:, tslot * 1040 + half * 520:
                                               tslot * 1040 + (half + 1) * 520]
                                          .rearrange("p (h c) -> p h c", h=8)[:, :, 0:64],
                                          PS[bank[1]][:, 0:512]
                                          .rearrange("p (h c) -> p h c", h=8)),
                                      reads=[bank], writes=[("vr", tslot)])

                        for side in range(2 if ATTN_STAGE >= 3 else 0):
                            tbase = 0 if side == 0 else 2
                            for i in range(2):
                                tk.dma("Y", "dTa" if i == 0 else "dTb",
                                       lambda i=i, tbase=tbase: SY.dma_start(
                                           tabs[:, i * 512:(i + 1) * 512],
                                           ROT[tbase + i][:, t0c:t0c + 512]),
                                       writes=["tabs"])
                            sc_col = qscol[:, side:side + 1]
                            dst = qfin if side == 0 else kfin
                            dk = "qf" if side == 0 else "kf"
                            def qk_slab_load(m):
                                kwp = m % 2
                                if side == 1:
                                    tk.dma("G", "dKW0" if kwp == 0 else "dKW1",
                                           lambda m=m, kwp=kwp: GP.dma_start(
                                               W["kwb"][:, kwp * D:(kwp + 1) * D]
                                               .rearrange("p (k c) -> p k c", k=DT),
                                               WQKV[l, :, D + m * P:D + (m + 1) * P]
                                               .rearrange("(k p) c -> p k c", p=P)),
                                           writes=[("kwb", kwp)])
                                else:
                                    tk.dma("G", "dQWa" if kwp == 0 else "dQWb",
                                           lambda m=m, kwp=kwp: GP.dma_start(
                                               W["qw"][:, kwp * D:(kwp + 1) * D]
                                               .rearrange("p (k c) -> p k c", k=DT),
                                               WQKV[l, :, m * P:(m + 1) * P]
                                               .rearrange("(k p) c -> p k c", p=P)),
                                           writes=[("qw", kwp)])

                            if ATTN_SUB >= 2:
                                qk_slab_load(0)
                            for m in range(DT if ATTN_SUB >= 2 else 0):
                                kwp = m % 2
                                if m + 1 < DT:
                                    qk_slab_load(m + 1)
                                bank = psk[m % 3]
                                for dt in range(DT if ATTN_SUB >= 3 else 0):
                                    if side == 0:
                                        lhs = W["qw"][:, kwp * D + dt * P: kwp * D + (dt + 1) * P]
                                        rk = ("qw", kwp)
                                    else:
                                        lhs = W["kwb"][:, kwp * D + dt * P: kwp * D + (dt + 1) * P]
                                        rk = ("kwb", kwp)
                                    tk.op("T", lambda lhs=lhs, dt=dt, bank=bank: T.matmul(
                                        PS[bank[1]][:, 0:512], lhs, absl(dt, 0, 512),
                                        start=(dt == 0), stop=(dt == DT - 1)),
                                        reads=[rk, ("aybf", dt)], writes=[bank])
                                if ATTN_SUB < 4:
                                    continue
                                par = m % 2
                                ubf = sbf2[:, par * 512:(par + 1) * 512]
                                sqb = sbf2[:, 1024:1536]
                                u2 = msb[:, (1 - par) * 512:(2 - par) * 512]
                                tt1 = tt1b_[:, par * 512:(par + 1) * 512]
                                rnb = rnbb_[:, 0:512]
                                rnscr = msb[:, par * 512:(par + 1) * 512]
                                tk.op("S", lambda ubf=ubf, bank=bank, sc_col=sc_col:
                                      SC.activation(
                                          ubf, PS[bank[1]][:, 0:512], AF.Copy,
                                          scale=sc_col),
                                      reads=[bank, "qsc"], writes=[("ubf", par)])
                                if ATTN_SUB < 5:
                                    continue
                                rotb = psk[3 + m % 2]
                                tk.op("T", lambda rotb=rotb, ubf=ubf: T.matmul(
                                    PS[rotb[1]][:, 0:512], rotr[:, :],
                                    ubf, start=True, stop=True),
                                      reads=["rotr", ("ubf", par)], writes=[rotb])
                                tk.op("G", lambda sqb=sqb, ubf=ubf: GP.tensor_tensor(
                                    sqb, ubf, ubf, ALU.mult),
                                    reads=[("ubf", par)], writes=["sqb2"])
                                tk.op("T", lambda side=side, sqb=sqb: T.matmul(
                                    PS[5][:, 0:512], blkq[:, side * P:(side + 1) * P],
                                    sqb, start=True, stop=True),
                                    reads=[("blkq", side), "sqb2"], writes=[psk[5]])
                                tk.op("S", lambda rnscr=rnscr: SC.activation(
                                    rnscr, PS[5][:, 0:512], AF.Ln),
                                      reads=[psk[5]], writes=[("msb", par)])
                                tk.op("S", lambda rnb=rnb, rnscr=rnscr: SC.activation(
                                    rnb, rnscr, AF.Exp, scale=-0.5),
                                      reads=[("msb", par)], writes=["rnb"])
                                tk.op("V", lambda tt1=tt1, bank=bank, sc_col=sc_col:
                                      V.scalar_tensor_tensor(
                                          tt1, PS[bank[1]][:, 0:512], sc_col,
                                          tabs[:, 0:512], ALU.mult, ALU.mult),
                                      reads=[bank, "qsc", "tabs"], writes=[("tt1", par)])
                                tk.op("V", lambda u2=u2, rotb=rotb: V.tensor_tensor(
                                    u2, PS[rotb[1]][:, 0:512],
                                    tabs[:, 512:1024], ALU.mult),
                                    reads=[rotb, "tabs"], writes=[("msb", 1 - par)])
                                tk.op("G", lambda tt1=tt1, u2=u2: GP.tensor_tensor(
                                    tt1, tt1, u2, ALU.add),
                                    reads=[("tt1", par), ("msb", 1 - par)],
                                    writes=[("tt1", par)])
                                sa = (t0c // 256) % 3
                                sbt = (sa + 1) % 3
                                if sbt == sa + 1:
                                    tk.op("V", lambda dst=dst, m=m, sa=sa, tt1=tt1,
                                          rnb=rnb: V.tensor_tensor(
                                        dst[:, m * 768 + sa * 256: m * 768 + sa * 256 + 512],
                                        tt1, rnb, ALU.mult),
                                        reads=[("tt1", par), "rnb"],
                                        writes=[(dk, m, sa), (dk, m, sbt)])
                                else:
                                    tk.op("V", lambda dst=dst, m=m, sa=sa, tt1=tt1,
                                          rnb=rnb: V.tensor_tensor(
                                        dst[:, m * 768 + sa * 256: m * 768 + sa * 256 + 256],
                                        tt1[:, 0:256], rnb[:, 0:256], ALU.mult),
                                        reads=[("tt1", par), "rnb"],
                                        writes=[(dk, m, sa)])
                                    tk.op("V", lambda dst=dst, m=m, sbt=sbt, tt1=tt1,
                                          rnb=rnb: V.tensor_tensor(
                                        dst[:, m * 768 + sbt * 256:
                                            m * 768 + sbt * 256 + 256],
                                        tt1[:, 256:512], rnb[:, 256:512], ALU.mult),
                                        reads=[("tt1", par), "rnb"],
                                        writes=[(dk, m, sbt)])

                    kws = []
                    if 2 * c - 1 >= 0 and c <= C:
                        kws.append(2 * c - 1)
                    if c < C:
                        kws.append(2 * c)
                    for kw in (kws if ATTN_STAGE >= 4 else []):
                        q0 = 256 * kw
                        qn = min(512, S - q0)
                        for hg in range(4):
                            for hh in range(4):
                                h = hg * 4 + hh
                                m = h // 2
                                rbase = (h % 2) * DH
                                for js in range(2):
                                    jt0 = 256 * kw + js * P
                                    ksl = (jt0 // 256) % 3
                                    koff = jt0 % 256
                                    lhs = kfin[rbase:rbase + DH,
                                               m * 768 + ksl * 256 + koff:
                                               m * 768 + ksl * 256 + koff + P]
                                    bank = psk[(hh + js) % 2]
                                    for (ap_, rel, ln_) in qf_read(qfin, m, q0, qn):
                                        tk.op("T", lambda lhs=lhs, ap_=ap_, rel=rel,
                                              ln_=ln_, bank=bank, rbase=rbase: T.matmul(
                                                  PS[bank[1]][:, rel:rel + ln_],
                                                  lhs, ap_[rbase:rbase + DH, :],
                                                  start=True, stop=True),
                                              reads=[("kf", m, ksl),
                                                     ("qf", m, (q0 // 256) % 3),
                                                     ("qf", m, (q0 // 256 + 1) % 3)],
                                              writes=[bank])
                                    mpar = (hh + js) % 2
                                    tk.op("V", lambda bank=bank, js=js, qn=qn, mpar=mpar:
                                          V.tensor_tensor(
                                              msb[:, mpar * 512: mpar * 512 + qn],
                                              PS[bank[1]][:, 0:qn],
                                              masks[:, js * 512: js * 512 + qn], ALU.add),
                                          reads=[bank, "msk"], writes=[("msb", mpar)])
                                    tk.op("S", lambda hh=hh, js=js, qn=qn, mpar=mpar:
                                          SC.activation(
                                              ptile[:, (hh * 2 + js) * 512:
                                                    (hh * 2 + js) * 512 + qn],
                                              msb[:, mpar * 512: mpar * 512 + qn],
                                              AF.Exp, scale=8.0, bias=negb[:, 0:1]),
                                          reads=[("msb", mpar), "negb"],
                                          writes=[("pt", hh, js)])
                            for w in ((kw, kw + 1) if ATTN_STAGE >= 5 else ()):
                                if w >= NW:
                                    continue
                                wsl = w % 2
                                first = (kw == max(w - 1, 0))
                                for sg in range(2):
                                    st0 = w * 256 + sg * P
                                    if st0 >= S:
                                        continue
                                    coff = st0 - 256 * kw
                                    ob = psk[2 + ((w + sg) % 2) * 0 + sg % 1] if False                                         else psk[2 + (sg % 2)]
                                    rsb = psk[4 + (sg % 2)]
                                    for hh in range(4):
                                        h = hg * 4 + hh
                                        for js in range(2):
                                            jt0 = 256 * kw + js * P
                                            vsl = (jt0 // P) % 8
                                            lhs = ptile[:, (hh * 2 + js) * 512 + coff:
                                                        (hh * 2 + js) * 512 + coff + P]
                                            tk.op("T", lambda lhs=lhs, h=h, vsl=vsl,
                                                  ob=ob, js=js, hh=hh: T.matmul(
                                                      PS[ob[1]][:, hh * 65:hh * 65 + 65],
                                                      lhs,
                                                      vrol[:, vsl * 1040 + h * 65:
                                                           vsl * 1040 + h * 65 + 65],
                                                      start=(js == 0), stop=(js == 1)),
                                                  reads=[("pt", hh, js), ("vr", vsl),
                                                         "vrones"],
                                                  writes=[ob])
                                    oc = wsl * 2080 + sg * 1040 + hg * 260
                                    if first:
                                        tk.op("V", lambda ob=ob, oc=oc: V.tensor_copy(
                                            opart[:, oc:oc + 260], PS[ob[1]][:, 0:260]),
                                            reads=[ob], writes=[("op", wsl, sg, hg)])
                                    else:
                                        tk.op("V", lambda ob=ob, oc=oc: V.tensor_tensor(
                                            opart[:, oc:oc + 260], PS[ob[1]][:, 0:260],
                                            opart[:, oc:oc + 260], ALU.add),
                                            reads=[ob, ("op", wsl, sg, hg)],
                                            writes=[("op", wsl, sg, hg)])

                        if ATTN_STAGE < 6:
                            continue
                        w = kw
                        wsl = w % 2
                        pbase = max(0, 512 * c - 256)
                        for sg in range(2):
                            st0 = w * 256 + sg * P
                            if st0 >= S:
                                continue
                            oc = wsl * 2080 + sg * 1040
                            tk.op("V", lambda oc=oc: V.reciprocal(
                                rinv[:, 0:16],
                                opart[:, oc:oc + 1040]
                                .rearrange("p (g d) -> p g d", g=16)[:, :, 64]),
                                reads=[("op", wsl, sg, 0), ("op", wsl, sg, 1),
                                       ("op", wsl, sg, 2), ("op", wsl, sg, 3)],
                                writes=["rinv"])
                            rexp = rinv[:, 0:16].unsqueeze(2).broadcast_to([P, 16, DH])
                            tk.op("V", lambda oc=oc, rexp=rexp: V.tensor_tensor(
                                onat[:, :].rearrange("p (g d) -> p g d", g=16),
                                opart[:, oc:oc + 1040]
                                .rearrange("p (g d) -> p g d", g=16)[:, :, 0:64],
                                rexp, ALU.mult),
                                reads=[("op", wsl, sg, 0), ("op", wsl, sg, 1),
                                       ("op", wsl, sg, 2), ("op", wsl, sg, 3), "rinv"],
                                writes=[("ubf", 0), ("ubf", 1)])
                            for half4 in range(2):
                                pbb = pbk[half4]
                                for j4 in range(4):
                                    it = half4 * 4 + j4
                                    tk.op("T", lambda it=it, j4=j4, pbb=pbb: T.transpose(
                                        PB[pbb[1]][:, j4 * P:(j4 + 1) * P],
                                        onat[:, it * P:(it + 1) * P], idb[:, :]),
                                        reads=[("ubf", 0), ("ubf", 1), "idb"],
                                        writes=[pbb])
                                off = st0 - pbase
                                dst = opas[:, :].rearrange(
                                    "p (it c) -> p it c", c=512)[:, half4 * 4:half4 * 4 + 4,
                                                                 off:off + P]
                                tk.op("V", lambda pbb=pbb, dst=dst: V.tensor_copy(
                                    dst, PB[pbb[1]][:, 0:512].rearrange(
                                        "p (it c) -> p it c", c=P)),
                                    reads=[pbb],
                                    writes=[("opas", half4 * 4 + j) for j in range(4)])

                    if kws and ATTN_STAGE >= 7:
                        pbase = max(0, 512 * c - 256)
                        pend = min(S, 512 * c + 256)
                        wlen = pend - pbase
                        def wo_load(do):
                            wop = do % 2
                            tk.dma("G", "dWO0" if wop == 0 else "dWO1",
                                   lambda do=do, wop=wop: GP.dma_start(
                                       W["wos"][:, wop * D:(wop + 1) * D]
                                       .rearrange("p (k c) -> p k c", k=DT),
                                       WO[l, :, do * P:(do + 1) * P]
                                       .rearrange("(k p) c -> p k c", p=P)),
                                   writes=[("wos", wop)])

                        wo_load(0)
                        wo_load(1)
                        for do in range(DT):
                            wop = do % 2
                            bank = psk[do % 2]
                            for it in range(DT):
                                tk.op("T", lambda do=do, it=it, wop=wop, bank=bank,
                                      wlen=wlen: T.matmul(
                                          PS[bank[1]][:, 0:wlen],
                                          W["wos"][:, wop * D + it * P: wop * D + (it + 1) * P],
                                          opas[:, it * 512: it * 512 + wlen],
                                          start=(it == 0), stop=(it == DT - 1)),
                                      reads=[("wos", wop), ("opas", it)],
                                      writes=[bank])
                            tk.op("V", lambda do=do, bank=bank, pbase=pbase, wlen=wlen:
                                  V.tensor_tensor(
                                      ysl(do, pbase, wlen), PS[bank[1]][:, 0:wlen],
                                      ysl(do, pbase, wlen), ALU.add),
                                  reads=[bank, ("ytr", do)], writes=[("ytr", do)])
                            if do + 2 < DT:
                                wo_load(do + 2)
                tk.barrier()

        def phase_A(b):
            with ExitStack() as ls:
                def lsb(n, sh, dt_):
                    return ls.enter_context(nc.sbuf_tensor(f"{n}_A{b}", sh, dt_))
                ktr = lsb("ktr", [P, DT * S], BF16)
                vnat = lsb("vnat", [P, NT * D], BF16)
                xnbf = lsb("xnbfA", [P, DT * 512], BF16)
                qtr = lsb("qtrA", [P, DT * 512], BF16)
                kwA = lsb("kwA", [P, 2 * D], BF16)
                xt = [lsb(f"xtA{i}", [P, D], F32) for i in range(2)]
                ast_ = lsb("astA", [P, 8], F32)
                pchk = lsb("pchkA", [P, C * 512], BF16)
                ptr = lsb("ptrA", [P, 2 * P], BF16)
                rsA = lsb("rsA", [P, C + 2], F32)
                obuf = [lsb(f"obufA{i}", [P, D], F32) for i in range(2)]
                tmpo = lsb("tmpoA", [P, D], F32)
                vwA = lsb("vwA", [P, DT * D], BF16)

                for tau in range(NT):
                    xb = xt[tau % 2]
                    sem = "dXa" if tau % 2 == 0 else "dXb"
                    tk.dma("Y", sem, lambda tau=tau, xb=xb: SY.dma_start(
                        xb[:, :], X[b, tau * P:(tau + 1) * P, :]),
                        writes=[("xt", tau % 2)])
                    tk.op("V", lambda xb=xb: V.tensor_reduce(
                        ast_[:, 0:1], xb[:, :], AX.X, ALU.add),
                        reads=[("xt", tau % 2)], writes=["astA"])
                    tk.op("S", lambda xb=xb: SC.activation(tmpo[:, :], xb[:, :], AF.Square),
                          reads=[("xt", tau % 2)], writes=["tmpoA"])
                    tk.op("V", lambda: V.tensor_reduce(
                        ast_[:, 1:2], tmpo[:, :], AX.X, ALU.add),
                        reads=["tmpoA"], writes=["astA"])
                    tk.op("V", lambda: V.tensor_scalar(
                        ast_[:, 2:3], ast_[:, 0:1], 1.0 / D, None, ALU.mult),
                        reads=["astA"], writes=["astA"])
                    tk.op("V", lambda: V.tensor_tensor(
                        ast_[:, 3:4], ast_[:, 2:3], ast_[:, 2:3], ALU.mult),
                        reads=["astA"], writes=["astA"])
                    tk.op("V", lambda: V.tensor_scalar(
                        ast_[:, 4:5], ast_[:, 1:2], 1.0 / D, EPS, ALU.mult, ALU.add),
                        reads=["astA"], writes=["astA"])
                    tk.op("V", lambda: V.tensor_tensor(
                        ast_[:, 5:6], ast_[:, 4:5], ast_[:, 3:4], ALU.subtract),
                        reads=["astA"], writes=["astA"])
                    tk.op("S", lambda: SC.activation(ast_[:, 6:7], ast_[:, 5:6], AF.Ln),
                          reads=["astA"], writes=["astB"])
                    tk.op("S", lambda: SC.activation(ast_[:, 7:8], ast_[:, 6:7],
                                                     AF.Exp, scale=-0.5),
                          reads=["astB"], writes=["astB"])
                    tk.op("V", lambda xb=xb: V.tensor_scalar(
                        obuf[0][:, :], xb[:, :], ast_[:, 2:3], ast_[:, 7:8],
                        ALU.subtract, ALU.mult),
                        reads=[("xt", tau % 2), "astA", "astB"], writes=[("obA", 0)])
                    for dt in range(DT):
                        bank = psk[4 + dt % 2]
                        tk.op("T", lambda dt=dt, bank=bank: T.transpose(
                            PS[bank[1]][:, 0:P], obuf[0][:, dt * P:(dt + 1) * P],
                            idf[:, :]),
                            reads=[("obA", 0), "idf"], writes=[bank])
                        tk.op("V", lambda dt=dt, bank=bank, tau=tau: V.tensor_copy(
                            ysl(dt, tau * P, P), PS[bank[1]][:, 0:P]),
                            reads=[bank], writes=[("ytr", dt)])

                for cc in range(C):
                    t0c = cc * 512
                    for dt in range(DT):
                        tk.op("V", lambda dt=dt, t0c=t0c: V.tensor_copy(
                            xnbf[:, dt * 512:(dt + 1) * 512], ysl(dt, t0c, 512)),
                            reads=[("ytr", dt)], writes=[("xnbfA", dt)])
                    for m in range(DT):
                        kwp = m % 2
                        tk.dma("G", "dKW0" if kwp == 0 else "dKW1",
                               lambda m=m, kwp=kwp: GP.dma_start(
                                   kwA[:, kwp * D:(kwp + 1) * D]
                                   .rearrange("p (k c) -> p k c", k=DT),
                                   Wka[:, m * P:(m + 1) * P]
                                   .rearrange("(k p) c -> p k c", p=P)),
                               writes=[("kwA", kwp)])
                        bank = psk[m % 2]
                        for dt in range(DT):
                            tk.op("T", lambda m=m, dt=dt, kwp=kwp, bank=bank: T.matmul(
                                PS[bank[1]][:, 0:512],
                                kwA[:, kwp * D + dt * P: kwp * D + (dt + 1) * P],
                                xnbf[:, dt * 512:(dt + 1) * 512],
                                start=(dt == 0), stop=(dt == DT - 1)),
                                reads=[("kwA", kwp), ("xnbfA", dt)], writes=[bank])
                        tk.op("V", lambda m=m, bank=bank, t0c=t0c: V.tensor_scalar(
                            ktr[:, m * S + t0c: m * S + t0c + 512],
                            PS[bank[1]][:, 0:512], abias[:, DT + m:DT + m + 1],
                            None, ALU.add),
                            reads=[bank, "abias"], writes=[("ktrA", m)])
                    if cc == 0:
                        for dt in range(DT):
                            tk.dma("G", "dQWa" if dt % 2 == 0 else "dQWb",
                                   lambda dt=dt: GP.dma_start(
                                       vwA[:, dt * D:(dt + 1) * D],
                                       Wva[dt * P:(dt + 1) * P, :]),
                                   writes=[("vwA", dt)])
                    for tt in range(4):
                        kt = cc * 4 + tt
                        for half in range(2):
                            bank = psk[2 + (2 * tt + half) % 2]
                            for dt in range(DT):
                                tk.op("T", lambda dt=dt, tt=tt, half=half, bank=bank:
                                      T.matmul(
                                          PS[bank[1]][:, 0:512],
                                          xnbf[:, dt * 512 + tt * P:
                                               dt * 512 + (tt + 1) * P],
                                          vwA[:, dt * D + half * 512:
                                              dt * D + (half + 1) * 512],
                                          start=(dt == 0), stop=(dt == DT - 1)),
                                      reads=[("xnbfA", dt), ("vwA", dt)], writes=[bank])
                            tk.op("V", lambda kt=kt, half=half, bank=bank: V.tensor_copy(
                                vnat[:, kt * D + half * 512: kt * D + (half + 1) * 512],
                                PS[bank[1]][:, 0:512]),
                                reads=[bank], writes=[("vnatA", kt)])

                for qc in range(C):
                    t0c = qc * 512
                    for dt in range(DT):
                        tk.op("V", lambda dt=dt, t0c=t0c: V.tensor_copy(
                            xnbf[:, dt * 512:(dt + 1) * 512], ysl(dt, t0c, 512)),
                            reads=[("ytr", dt)], writes=[("xnbfA", dt)])
                    for m in range(DT):
                        kwp = m % 2
                        tk.dma("G", "dKW0" if kwp == 0 else "dKW1",
                               lambda m=m, kwp=kwp: GP.dma_start(
                                   kwA[:, kwp * D:(kwp + 1) * D]
                                   .rearrange("p (k c) -> p k c", k=DT),
                                   Wqa[:, m * P:(m + 1) * P]
                                   .rearrange("(k p) c -> p k c", p=P)),
                               writes=[("kwA", kwp)])
                        bank = psk[m % 2]
                        for dt in range(DT):
                            tk.op("T", lambda m=m, dt=dt, kwp=kwp, bank=bank: T.matmul(
                                PS[bank[1]][:, 0:512],
                                kwA[:, kwp * D + dt * P: kwp * D + (dt + 1) * P],
                                xnbf[:, dt * 512:(dt + 1) * 512],
                                start=(dt == 0), stop=(dt == DT - 1)),
                                reads=[("kwA", kwp), ("xnbfA", dt)], writes=[bank])
                        tk.op("V", lambda m=m, bank=bank: V.tensor_scalar(
                            qtr[:, m * 512:(m + 1) * 512],
                            PS[bank[1]][:, 0:512], abias[:, m:m + 1], None, ALU.add),
                            reads=[bank, "abias"], writes=[("qtrA", m)])
                    for sg in range(4):
                        tau = qc * 4 + sg
                        for kc in range(C):
                            bank = psk[kc]
                            for m in range(DT):
                                tk.op("T", lambda m=m, kc=kc, sg=sg, bank=bank: T.matmul(
                                    PS[bank[1]][:, 0:512],
                                    qtr[:, m * 512 + sg * P: m * 512 + (sg + 1) * P],
                                    ktr[:, m * S + kc * 512: m * S + (kc + 1) * 512],
                                    start=(m == 0), stop=(m == DT - 1)),
                                    reads=[("qtrA", m), ("ktrA", m)], writes=[bank])
                            tk.op("V", lambda kc=kc, bank=bank: V.reduce_max(
                                rsA[:, kc:kc + 1], PS[bank[1]][:, 0:512],
                                AX.X, negate=True),
                                reads=[bank], writes=["rsneg"])
                        for kc in range(1, C):
                            tk.op("V", lambda kc=kc: V.tensor_tensor(
                                rsA[:, 0:1], rsA[:, 0:1], rsA[:, kc:kc + 1], ALU.min),
                                reads=["rsneg"], writes=["rsneg"])
                        tk.op("V", lambda: V.tensor_scalar(
                            rsA[:, 0:1], rsA[:, 0:1], 1.0 / 32.0, None, ALU.mult),
                            reads=["rsneg"], writes=["rsneg"])
                        for kc in range(C):
                            bank = psk[kc]
                            tk.op("S", lambda kc=kc, bank=bank: SC.activation(
                                pchk[:, kc * 512:(kc + 1) * 512],
                                PS[bank[1]][:, 0:512], AF.Exp,
                                scale=1.0 / 32.0, bias=rsA[:, 0:1],
                                accum_out=rsA[:, C + 1:C + 2]),
                                reads=[bank, "rsneg"],
                                writes=[("pchkA", kc), "rsacc"])
                            if kc == 0:
                                tk.op("V", lambda: V.tensor_copy(
                                    rsA[:, C:C + 1], rsA[:, C + 1:C + 2]),
                                    reads=["rsacc"], writes=["rstot"])
                            else:
                                tk.op("V", lambda: V.tensor_tensor(
                                    rsA[:, C:C + 1], rsA[:, C:C + 1],
                                    rsA[:, C + 1:C + 2], ALU.add),
                                    reads=["rsacc", "rstot"], writes=["rstot"])
                        tk.op("V", lambda: V.reciprocal(rsA[:, C:C + 1], rsA[:, C:C + 1]),
                              reads=["rstot"], writes=["rstot"])
                        ob = obuf[sg % 2]
                        sem = "dXa" if sg % 2 == 0 else "dXb"
                        tk.dma("Y", sem, lambda tau=tau, ob=ob: SY.dma_start(
                            ob[:, :], X[b, tau * P:(tau + 1) * P, :]),
                            writes=[("obA", sg % 2)])
                        for kt in range(C * 4):
                            pbb = pbk[kt % 2]
                            tk.op("T", lambda kt=kt, pbb=pbb: T.transpose(
                                PB[pbb[1]][:, 0:P], pchk[:, kt * P:(kt + 1) * P],
                                idb[:, :]),
                                reads=[("pchkA", kt // 4), "idb"], writes=[pbb])
                            tk.op("V", lambda kt=kt, pbb=pbb: V.tensor_copy(
                                ptr[:, (kt % 2) * P:(kt % 2 + 1) * P],
                                PB[pbb[1]][:, 0:P]),
                                reads=[pbb], writes=[("ptrA", kt % 2)])
                            for half in range(2):
                                bank = psk[4 + half]
                                tk.op("T", lambda kt=kt, half=half, bank=bank: T.matmul(
                                    PS[bank[1]][:, 0:512],
                                    ptr[:, (kt % 2) * P:(kt % 2 + 1) * P],
                                    vnat[:, kt * D + half * 512:
                                         kt * D + (half + 1) * 512],
                                    start=(kt == 0), stop=(kt == C * 4 - 1)),
                                    reads=[("ptrA", kt % 2), ("vnatA", kt)],
                                    writes=[bank])
                        for half in range(2):
                            bank = psk[4 + half]
                            tk.op("V", lambda half=half, bank=bank: V.scalar_tensor_tensor(
                                tmpo[:, half * 512:(half + 1) * 512],
                                PS[bank[1]][:, 0:512], rsA[:, C:C + 1],
                                bvbc[:, half * 512:(half + 1) * 512],
                                ALU.mult, ALU.add),
                                reads=[bank, "rstot", "bvbc"], writes=["tmpoA"])
                        for half in range(2):
                            tk.op("V", lambda half=half, ob=ob: V.tensor_tensor(
                                ob[:, half * 512:(half + 1) * 512],
                                tmpo[:, half * 512:(half + 1) * 512],
                                ob[:, half * 512:(half + 1) * 512], ALU.add),
                                reads=["tmpoA", ("obA", sg % 2)],
                                writes=[("obA", sg % 2)])
                        tk.dma("Y", "dO", lambda tau=tau, ob=ob: SY.dma_start(
                            OUT[b, tau * P:(tau + 1) * P, :], ob[:, :]),
                            reads=[("obA", sg % 2)], writes=[("out", b, tau)])
                tk.barrier()

        def tail(b):
            with ExitStack() as ls:
                ynat = ls.enter_context(nc.sbuf_tensor(f"ynatT{b}", [P, D], F32))
                for tau in range(NT):
                    for dt in range(DT):
                        bank = psk[4 + dt % 2]
                        tk.op("T", lambda dt=dt, tau=tau, bank=bank: T.transpose(
                            PS[bank[1]][:, 0:P], ysl(dt, tau * P, P), idf[:, :]),
                            reads=[("ytr", dt), "idf"], writes=[bank])
                        tk.op("V", lambda dt=dt, bank=bank: V.tensor_copy(
                            ynat[:, dt * P:(dt + 1) * P], PS[bank[1]][:, 0:P]),
                            reads=[bank], writes=["ynatT"])
                    tk.dma("G", "dAcc", lambda tau=tau: GP.dma_start(
                        OUT[b, tau * P:(tau + 1) * P, :], ynat[:, :],
                        accum_op=ALU.add),
                        reads=["ynatT", ("out", b, tau)], writes=[("out", b, tau)])
                tk.barrier()

        for b in range(BPC):
            if y0_input:
                for dt in range(DT):
                    tk.dma("Y", "dXa", lambda dt=dt: SY.dma_start(
                        ytr[:, dt * S:(dt + 1) * S], Y0[b, :, dt * S:(dt + 1) * S]),
                        writes=[("ytr", dt)])
            if not skip_a:
                phase_A(b)
            for l in range(L):
                tk.dma("Y", "dP", lambda l=l: SY.dma_start(prm[:, :], PRM[l]),
                       writes=["prm"])
                tk.dma("Y", "dP", lambda l=l: SY.dma_start(qkr[:, :], QKR[l]),
                       writes=["qkr"])
                if not skip_attn:
                    b_attn(l, b)
                if not skip_ffn:
                    b_ffn(l, b)
            if not skip_tail:
                tail(b)
            for tn in taps:
                if tn.split(":")[0] == "ytr":
                    for dt in range(DT):
                        tk.dma("Y", "dO", lambda dt=dt, tn=tn: SY.dma_start(
                            tap_t[tn][b, :, dt * S:(dt + 1) * S],
                            ytr[:, dt * S:(dt + 1) * S]),
                            reads=[("ytr", dt)])

        for sname in ["dO", "dAcc"]:
            if tk.dcnt[sname] > 0:
                SY.wait_ge(tk.dsem[sname], tk.dcnt[sname])
    return nc


_NC_CACHE = {}


def build_full():
    if "nc" not in _NC_CACHE:
        _NC_CACHE["nc"] = build_program(FULL_B // N_CORES, FULL_S, FULL_L)
    return _NC_CACHE["nc"]


def kernel(**inputs):
    S, L = FULL_S, FULL_L
    BPC = FULL_B // N_CORES
    hc = host_constants(S)
    prm, qkr = pack_params(inputs["L0_ffn_q_scale"], inputs["L0_ffn_k_scale"],
                           inputs["L0_ffn_ln_g"], inputs["L0_ffn_ln_b"],
                           inputs["L0_ffn_b1"], inputs["L0_ffn_b2"])
    abias = np.stack([np.asarray(inputs["L0_attn_b_q"]).reshape(DT, P).T,
                      np.asarray(inputs["L0_attn_b_k"]).reshape(DT, P).T]
                     ).astype(np.float32)
    X = np.ascontiguousarray(np.asarray(inputs["L0_attn_X"], dtype=np.float32))
    common = {
        "Wqa": np.ascontiguousarray(np.asarray(inputs["L0_attn_W_q"], np.float32)),
        "Wka": np.ascontiguousarray(np.asarray(inputs["L0_attn_W_k"], np.float32)),
        "Wva": np.ascontiguousarray(np.asarray(inputs["L0_attn_W_v"], np.float32)),
        "BV": np.ascontiguousarray(np.asarray(inputs["L0_attn_b_v"], np.float32)),
        "ABIAS": abias,
        "WQKV": np.ascontiguousarray(np.asarray(inputs["L0_ffn_Wqkv"], np.float32)),
        "WO": np.ascontiguousarray(np.asarray(inputs["L0_ffn_Wo"], np.float32)),
        "W1": np.ascontiguousarray(np.asarray(inputs["L0_ffn_W1"], np.float32)),
        "W2": np.ascontiguousarray(np.asarray(inputs["L0_ffn_W2"], np.float32)),
        "PRM": prm, "QKR": qkr,
        "ROT": hc["rotab"], "MSK": hc["masks"], "IDB": hc["ident_bf"],
        "IDF": hc["ident_f32"], "BLK": hc["blk64"], "ONEC": hc["ones_col"],
        "ONRF": hc["ones_row_f32"], "ROTR": hc["rotR"],
    }
    nc = build_full()
    in_maps = []
    for c in range(N_CORES):
        m = dict(common)
        m["X"] = np.ascontiguousarray(X[c * BPC:(c + 1) * BPC])
        in_maps.append(m)
    res = run_bass_kernel_spmd(nc, in_maps, core_ids=list(range(N_CORES)))
    out = np.concatenate([np.asarray(res.results[c]["OUT"]) for c in range(N_CORES)],
                         axis=0)
    return out.astype(np.float32)


# version 10
# speedup vs baseline: 1.2036x; 1.0441x over previous
import bisect
import numpy as np
import ml_dtypes
import concourse.bass as bass
import concourse.mybir as mybir
from concourse.bass_utils import run_bass_kernel_spmd
from contextlib import ExitStack

F32 = mybir.dt.float32
BF16 = mybir.dt.bfloat16
AF = mybir.ActivationFunctionType
ALU = mybir.AluOpType
AX = mybir.AxisListType

P = 128
D = 1024
DT = D // P
HEADS = 16
DH = 64
WIN = 256
FH = 2730
NOG = (FH + P - 1) // P
NEG = -1e30
EPS = 1e-5

PC_G = 0
PC_B = 8
PC_B1A = 16
PC_B1G = 16 + NOG
PC_B2 = 16 + 2 * NOG
PC_QS = 24 + 2 * NOG
PC_KS = 25 + 2 * NOG
NPARAM = 26 + 2 * NOG

N_CORES = 8
ATTN_STAGE = 9
ATTN_SUB = 9
FULL_B = 16
FULL_S = 2048
FULL_L = 6


def host_constants(S):
    half = np.arange(0, DH, 2, dtype=np.float64)
    inv_freq = 1.0 / 10000.0 ** (half / DH)
    t = np.arange(S, dtype=np.float64)
    freqs = np.concatenate([t[:, None] * inv_freq[None, :]] * 2, -1)
    base_scale = (half + 0.4 * DH) / (1.4 * DH)
    power = (t - S // 2) / (WIN // 2)
    scale = np.concatenate([base_scale[None, :] ** power[:, None]] * 2, -1)
    cos, sin = np.cos(freqs), np.sin(freqs)
    inv = 1.0 / scale
    sgn = np.concatenate([-np.ones(32), np.ones(32)])[None, :]

    def two(v):
        return np.concatenate([v.T, v.T], axis=0).astype(np.float32)

    rotab = np.stack([two(cos * scale), two(sin * scale * sgn),
                      two(cos * inv), two(sin * inv * sgn)])

    masks = np.zeros((2, P, 512), dtype=np.float32)
    for js in range(2):
        jr = np.arange(P)[:, None] + js * P
        qc = np.arange(512)[None, :]
        masks[js][(qc >= jr) & (qc <= jr + WIN)] = 1.0
    masks = masks.astype(ml_dtypes.bfloat16)

    blk = np.zeros((P, P))
    blk[:DH, :DH] = 1.0
    blk[DH:, DH:] = 1.0
    rotR = np.zeros((P, P))
    for d in range(P):
        base, r = d - d % DH, d % DH
        src = base + (r + 32 if r < 32 else r - 32)
        rotR[src, d] = 1.0
    return dict(rotab=rotab, masks=masks,
                ident_bf=np.eye(P, dtype=ml_dtypes.bfloat16),
                ident_f32=np.eye(P, dtype=np.float32),
                blk64=blk.astype(ml_dtypes.bfloat16),
                rotR=rotR.astype(ml_dtypes.bfloat16),
                ones_col=np.ones((P, 1), dtype=ml_dtypes.bfloat16),
                ones_row_f32=np.ones((1, P), dtype=np.float32))


def pack_params(q_scale, k_scale, ln_g, ln_b, b1, b2):
    L = ln_g.shape[0]
    cols, rows = [], []
    for l in range(L):
        g = ln_g[l].reshape(DT, P).T
        bb = ln_b[l].reshape(DT, P).T
        b1a = np.zeros((P, NOG), np.float32)
        b1g = np.zeros((P, NOG), np.float32)
        for og in range(NOG):
            n = min(P, FH - og * P)
            b1a[:n, og] = b1[l, og * P: og * P + n]
            b1g[:n, og] = b1[l, FH + og * P: FH + og * P + n]
        bb2 = b2[l].reshape(DT, P).T
        qs = np.concatenate([q_scale[l], q_scale[l]])[:, None]
        ks = np.concatenate([k_scale[l], k_scale[l]])[:, None]
        cols.append(np.concatenate([g, bb, b1a, b1g, bb2, qs, ks], 1).astype(np.float32))
        rows.append(np.concatenate([q_scale[l], k_scale[l]])[None, :].astype(np.float32))
    return np.stack(cols), np.stack(rows)


class Tk:

    def __init__(self, nc, stack, dma_sems):
        self.nc = nc
        self.eng = {"T": nc.tensor, "V": nc.vector, "S": nc.scalar,
                    "Y": nc.sync, "G": nc.gpsimd}
        self.esem = {}
        self.seq = {}
        self.instrs = {}
        self.incs = {}
        for k in self.eng:
            self.esem[k] = stack.enter_context(nc.semaphore("se" + k))
            self.seq[k] = 0
            self.instrs[k] = []
            self.incs[k] = []
        self.dsem = {}
        self.dcnt = {}
        for name in dma_sems:
            self.dsem[name] = stack.enter_context(nc.semaphore(name))
            self.dcnt[name] = 0
        self.buf = {}
        self.waited = {}

    def _inc_val(self, ekey, seq):
        incs = self.incs[ekey]
        if incs and incs[-1][0] >= seq:
            i = bisect.bisect_left(incs, (seq, -1))
            return incs[i][1]
        val = (incs[-1][1] + 1) if incs else 1
        self.instrs[ekey][seq - 1].then_inc(self.esem[ekey], 1)
        incs.append((seq, val))
        return val

    def _emit_waits(self, waiter, sources):
        need = {}
        for src in sources:
            if src is None:
                continue
            if src[0] == "E":
                _, ekey, seq = src
                if ekey == waiter == "T":
                    continue
                v = self._inc_val(ekey, seq)
                key = ("E", ekey)
                need[key] = max(need.get(key, 0), v)
            else:
                _, sname, cnt = src
                key = ("D", sname)
                need[key] = max(need.get(key, 0), self.dcnt[sname])
        for key, val in sorted(need.items()):
            if self.waited.get((waiter, key), 0) >= val:
                continue
            sem = self.esem[key[1]] if key[0] == "E" else self.dsem[key[1]]
            self.eng[waiter].wait_ge(sem, val)
            self.waited[(waiter, key)] = val

    def _gather(self, reads, writes):
        srcs = []
        for k in reads:
            b = self.buf.get(k)
            if b and b["w"]:
                srcs.append(b["w"])
        for k in writes:
            b = self.buf.get(k)
            if b:
                if b["w"]:
                    srcs.append(b["w"])
                srcs.extend(b["r"].values())
        return srcs

    def _update(self, reads, writes, src):
        for k in writes:
            self.buf[k] = {"w": src, "r": {}}
        for k in reads:
            b = self.buf.setdefault(k, {"w": None, "r": {}})
            b["r"][src[:2] if src[0] == "E" else ("D", src[1])] = src

    @staticmethod
    def _is_psum(k):
        return isinstance(k, tuple) and len(k) == 2 and k[0] in ("ps", "pb")

    def op(self, ekey, fn, reads=(), writes=()):
        srcs = self._gather(reads, writes)
        for k in reads:
            if self._is_psum(k):
                b = self.buf.get(k)
                if b:
                    for rk, v in b["r"].items():
                        if rk[0] == "E" and rk[1] != ekey:
                            srcs.append(v)
        self._emit_waits(ekey, srcs)
        ins = fn()
        self.seq[ekey] += 1
        self.instrs[ekey].append(ins)
        src = ("E", ekey, self.seq[ekey])
        self._update(reads, writes, src)
        return ins

    def dma(self, ekey, sname, fn, reads=(), writes=()):
        if self.dcnt[sname] > 0:
            key = ("D", sname)
            if self.waited.get((ekey, key), 0) < self.dcnt[sname]:
                self.eng[ekey].wait_ge(self.dsem[sname], self.dcnt[sname])
                self.waited[(ekey, key)] = self.dcnt[sname]
        self._emit_waits(ekey, self._gather(reads, writes))
        ins = fn()
        self.dcnt[sname] += 16
        ins.then_inc(self.dsem[sname], 16)
        src = ("D", sname, self.dcnt[sname])
        self._update(reads, writes, src)
        return ins

    def barrier_read(self, ekey, keys):
        self._emit_waits(ekey, self._gather(keys, ()))

    def barrier(self):
        srcs = []
        for ekey in self.eng:
            if self.seq[ekey] > 0:
                srcs.append(("E", ekey, self.seq[ekey]))
        for sname, cnt in self.dcnt.items():
            if cnt > 0:
                srcs.append(("D", sname, cnt))
        for waiter in self.eng:
            self._emit_waits(waiter, [s for s in srcs
                                      if not (s[0] == "E" and s[1] == waiter)])


DMA_SEMS = ["dW", "dP", "dTa", "dTb", "dXa", "dXb", "dQWa", "dQWb", "dKW0", "dKW1",
            "dVW0", "dVW1", "dWO0", "dWO1", "dW1a", "dW1b", "dW2a", "dW2b", "dO", "dAcc"]


def build_program(BPC, S, L, taps=(), skip_a=False, skip_attn=False, skip_ffn=False,
                  skip_tail=False, y0_input=False):
    C = S // 512
    NW = S // WIN
    NT = S // P
    HT = S // 2
    NU = HT // 512
    nc = bass.Bass("TRN2", target_bir_lowering=False)

    X = nc.dram_tensor("X", [BPC, S, D], F32, kind="ExternalInput").ap()
    OUT = nc.dram_tensor("OUT", [BPC, S, D], F32, kind="ExternalOutput").ap()
    Wqa = nc.dram_tensor("Wqa", [D, D], F32, kind="ExternalInput").ap()
    Wka = nc.dram_tensor("Wka", [D, D], F32, kind="ExternalInput").ap()
    Wva = nc.dram_tensor("Wva", [D, D], F32, kind="ExternalInput").ap()
    BV = nc.dram_tensor("BV", [D], F32, kind="ExternalInput").ap()
    ABIAS = nc.dram_tensor("ABIAS", [2, P, DT], F32, kind="ExternalInput").ap()
    WQKV = nc.dram_tensor("WQKV", [L, D, 3 * D], F32, kind="ExternalInput").ap()
    WO = nc.dram_tensor("WO", [L, D, D], F32, kind="ExternalInput").ap()
    W1 = nc.dram_tensor("W1", [L, D, 2 * FH], F32, kind="ExternalInput").ap()
    W2 = nc.dram_tensor("W2", [L, FH, D], F32, kind="ExternalInput").ap()
    PRM = nc.dram_tensor("PRM", [L, P, NPARAM], F32, kind="ExternalInput").ap()
    QKR = nc.dram_tensor("QKR", [L, 1, 2 * DH], F32, kind="ExternalInput").ap()
    ROT = nc.dram_tensor("ROT", [4, P, S], F32, kind="ExternalInput").ap()
    MSK = nc.dram_tensor("MSK", [2, P, 512], BF16, kind="ExternalInput").ap()
    IDB = nc.dram_tensor("IDB", [P, P], BF16, kind="ExternalInput").ap()
    IDF = nc.dram_tensor("IDF", [P, P], F32, kind="ExternalInput").ap()
    BLK = nc.dram_tensor("BLK", [P, P], BF16, kind="ExternalInput").ap()
    ONEC = nc.dram_tensor("ONEC", [P, 1], BF16, kind="ExternalInput").ap()
    ONRF = nc.dram_tensor("ONRF", [1, P], F32, kind="ExternalInput").ap()
    ROTR = nc.dram_tensor("ROTR", [P, P], BF16, kind="ExternalInput").ap()
    if y0_input:
        Y0 = nc.dram_tensor("Y0", [BPC, P, DT * S], F32, kind="ExternalInput").ap()
    tap_t = {}
    for tn in taps:
        tap_t[tn] = nc.dram_tensor("tap_" + tn.replace(":", "_"),
                                   [BPC, P, DT * S], F32, kind="ExternalOutput").ap()

    stack = ExitStack()
    with stack:
        tk = Tk(nc, stack, DMA_SEMS)
        T, V, SC, SY, GP = nc.tensor, nc.vector, nc.scalar, nc.sync, nc.gpsimd

        def sb(name, shape, dtype):
            return stack.enter_context(nc.sbuf_tensor(name, shape, dtype))

        ytr = sb("ytr", [P, DT * S], F32)
        idb = sb("idb", [P, P], BF16)
        idf = sb("idf", [P, P], F32)
        blk64 = sb("blk64", [P, P], BF16)
        onec = sb("onec", [P, 1], BF16)
        onrf = sb("onrf", [1, P], F32)
        prm = sb("prm", [P, NPARAM], F32)
        qkr = sb("qkr", [1, 2 * DH], F32)
        abias = sb("abias", [P, 2 * DT], F32)
        negb = sb("negb", [P, 1], F32)
        srow = sb("srow", [1, 4 * 512], F32)
        bvbc = sb("bvbc", [P, D], F32)
        bndry = sb("bndry", [P, DT], F32)
        epsc = sb("epsc", [P, 1], F32)
        rotr = sb("rotr", [P, P], BF16)
        blkq = sb("blkq", [P, 2 * P], BF16)
        qscol = sb("qscol", [P, 4], F32)

        PS = [stack.enter_context(nc.psum_tensor(f"ps{i}", [P, 512], F32)) for i in range(6)]
        PB = [stack.enter_context(nc.psum_tensor(f"pb{i}", [P, 1024], BF16)) for i in range(2)]
        psk = [("ps", i) for i in range(6)]
        pbk = [("pb", i) for i in range(2)]

        for dst, src_, key in [(idb[:, :], IDB, "idb"), (idf[:, :], IDF, "idf"),
                               (blk64[:, :], BLK, "blk"), (onec[:, :], ONEC, "onec"),
                               (onrf[:, :], ONRF, "onrf"),
                               (abias[:, 0:DT], ABIAS[0], "abias"),
                               (abias[:, DT:2 * DT], ABIAS[1], "abias"),
                               (rotr[:, :], ROTR, "rotr"),
                               (bvbc[:, :], BV[None, :].broadcast_to([P, D]), "bvbc")]:
            tk.dma("Y", "dW", lambda d=dst, s=src_: SY.dma_start(d, s), writes=[key])

        tk.op("V", lambda: V.memset(epsc[:, :], EPS), writes=["epsc"])

        def ysl(dt, t0, n):
            return ytr[:, dt * S + t0: dt * S + t0 + n]

        def b_ffn(l, b):
            with ExitStack() as ls:
                def lsb(n, sh, dt_):
                    return ls.enter_context(nc.sbuf_tensor(f"{n}_l{l}b{b}", sh, dt_))
                ybf = lsb("fybf", [P, NU * DT * 513], BF16)
                sqb = lsb("fsq", [P, 512], BF16)
                xnbf = lsb("fxn", [P, DT * HT], BF16)
                hbuf = lsb("fh", [P, NOG * HT], BF16)
                w1sl = [lsb(f"fw1{i}", [P, D], BF16) for i in range(4)]
                tbuf = lsb("ftmp", [P, 3 * 512], F32)
                fsrow = lsb("fsrow", [1, 1536], F32)
                w2sl = [lsb(f"fw2{i}", [P, NOG * P], BF16) for i in range(2)]

                for hb in range(2):
                    T0 = hb * HT

                    def shsl(u, dt):
                        off = 0 if dt >= 4 else 1
                        return ybf[:, (u * DT + dt) * 513 + off:
                                   (u * DT + dt) * 513 + off + 512]

                    for u in range(NU):
                        t0 = T0 + u * 512
                        for dt in range(DT):
                            if t0 == 0:
                                tk.op("V", lambda u=u, dt=dt: V.memset(
                                    ybf[:, (u * DT + dt) * 513:(u * DT + dt) * 513 + 1],
                                    0.0), writes=[("ybf", u, dt)])
                            elif dt >= 4 and t0 == HT:
                                tk.op("V", lambda u=u, dt=dt: V.tensor_copy(
                                    ybf[:, (u * DT + dt) * 513:(u * DT + dt) * 513 + 1],
                                    bndry[:, dt:dt + 1]),
                                    reads=[("bnd", dt)], writes=[("ybf", u, dt)])
                            else:
                                tk.op("V", lambda u=u, dt=dt, t0=t0: V.tensor_copy(
                                    ybf[:, (u * DT + dt) * 513:(u * DT + dt) * 513 + 1],
                                    ysl(dt, t0 - 1, 1)),
                                    reads=[("ytr", dt)], writes=[("ybf", u, dt)])
                            if dt % 2 == 1:
                                tk.op("S", lambda u=u, dt=dt, t0=t0: SC.activation(
                                    ybf[:, (u * DT + dt) * 513 + 1:(u * DT + dt + 1) * 513],
                                    ysl(dt, t0, 512), AF.Copy),
                                    reads=[("ytr", dt)], writes=[("ybf", u, dt)])
                            else:
                                tk.op("V", lambda u=u, dt=dt, t0=t0: V.tensor_copy(
                                    ybf[:, (u * DT + dt) * 513 + 1:(u * DT + dt + 1) * 513],
                                    ysl(dt, t0, 512)),
                                    reads=[("ytr", dt)], writes=[("ybf", u, dt)])
                    for u in range(NU):
                        mb_ = psk[0 + 2 * u]
                        sb_ = psk[1 + 2 * u]
                        for dt in range(DT):
                            tk.op("T", lambda u=u, dt=dt, mb_=mb_: T.matmul(
                                PS[mb_[1]][0:1, 0:512], onec[:, :], shsl(u, dt),
                                start=(dt == 0), stop=(dt == DT - 1)),
                                reads=[("ybf", u, dt), "onec"], writes=[mb_])
                        for dt in range(DT):
                            tk.op("S", lambda u=u, dt=dt: SC.activation(
                                sqb[:, :], shsl(u, dt), AF.Square),
                                reads=[("ybf", u, dt)], writes=["sqb"])
                            tk.op("T", lambda u=u, dt=dt, sb_=sb_: T.matmul(
                                PS[sb_[1]][0:1, 0:512], onec[:, :], sqb[:, :],
                                start=(dt == 0), stop=(dt == DT - 1)),
                                reads=["sqb", "onec"], writes=[sb_])
                    MCOL = [0, 512]
                    VCOL = [1024, 0]
                    for u in range(NU):
                        mb_ = psk[0 + 2 * u]
                        sb_ = psk[1 + 2 * u]
                        mc, vc = MCOL[u], VCOL[u]
                        tk.op("V", lambda u=u, mb_=mb_, mc=mc: V.tensor_scalar(
                            fsrow[:, mc:mc + 512], PS[mb_[1]][0:1, 0:512],
                            1.0 / D, None, ALU.mult),
                            reads=[mb_], writes=[("srw", mc // 512)])
                        tk.op("V", lambda u=u, mc=mc, vc=vc: V.tensor_tensor(
                            fsrow[:, vc:vc + 512], fsrow[:, mc:mc + 512],
                            fsrow[:, mc:mc + 512], ALU.mult),
                            reads=[("srw", mc // 512)], writes=[("srw", vc // 512)])
                        tk.op("V", lambda u=u, sb_=sb_, vc=vc: V.scalar_tensor_tensor(
                            fsrow[:, vc:vc + 512], PS[sb_[1]][0:1, 0:512],
                            1.0 / D, fsrow[:, vc:vc + 512],
                            ALU.mult, ALU.subtract),
                            reads=[sb_, ("srw", vc // 512)], writes=[("srw", vc // 512)])
                        mbb = psk[4 + u]
                        tk.op("T", lambda u=u, mc=mc, mbb=mbb: T.matmul(
                            PS[mbb[1]][:, 0:512], onrf[:, :], fsrow[:, mc:mc + 512],
                            start=True, stop=True),
                            reads=[("srw", mc // 512), "onrf"], writes=[mbb])
                    for u in range(NU):
                        vc = VCOL[u]
                        tk.op("S", lambda u=u, vc=vc: SC.activation(
                            fsrow[:, vc:vc + 512], fsrow[:, vc:vc + 512],
                            AF.Ln, bias=epsc[0:1, 0:1]),
                            reads=[("srw", vc // 512), "epsc"], writes=[("srw", vc // 512)])
                        tk.op("S", lambda u=u, vc=vc: SC.activation(
                            fsrow[:, vc:vc + 512], fsrow[:, vc:vc + 512],
                            AF.Exp, scale=-0.5),
                            reads=[("srw", vc // 512)], writes=[("srw", vc // 512)])
                    for u in range(NU):
                        vc = VCOL[u]
                        sbb = psk[0 + u]
                        tk.op("T", lambda u=u, vc=vc, sbb=sbb: T.matmul(
                            PS[sbb[1]][:, 0:512], onrf[:, :], fsrow[:, vc:vc + 512],
                            start=True, stop=True),
                            reads=[("srw", vc // 512), "onrf"], writes=[sbb])
                    for u in range(NU):
                        mbb = psk[4 + u]
                        sbb = psk[0 + u]
                        for dt in range(DT):
                            sl = 1 + (u * DT + dt) % 2
                            tk.op("V", lambda u=u, dt=dt, mbb=mbb: V.tensor_tensor(
                                tbuf[:, 0:512], shsl(u, dt), PS[mbb[1]][:, 0:512],
                                ALU.subtract),
                                reads=[("ybf", u, dt), mbb], writes=[("tb", 0)])
                            tk.op("V", lambda sl=sl, sbb=sbb: V.tensor_tensor(
                                tbuf[:, sl * 512:(sl + 1) * 512], tbuf[:, 0:512],
                                PS[sbb[1]][:, 0:512], ALU.mult),
                                reads=[("tb", 0), sbb], writes=[("tb", sl)])
                            tk.op("S", lambda dt=dt, u=u, sl=sl: SC.activation(
                                xnbf[:, dt * HT + u * 512: dt * HT + u * 512 + 512],
                                tbuf[:, sl * 512:(sl + 1) * 512], AF.Identity,
                                scale=prm[:, PC_G + dt:PC_G + dt + 1],
                                bias=prm[:, PC_B + dt:PC_B + dt + 1]),
                                reads=[("tb", sl), "prm"], writes=[("xnbf", dt)])

                    def w1_load(og, bi):
                        sname = "dW1a" if bi == 0 else "dW1b"
                        c0a, c0g = og * P, FH + og * P
                        n = min(P, FH - og * P)
                        tk.dma("G", sname, lambda: GP.dma_start(
                            w1sl[bi][:, :].rearrange("p (k c) -> p k c", k=DT)[:, :, 0:n],
                            W1[l, :, c0a:c0a + n].rearrange("(k p) c -> p k c", p=P)),
                            writes=[("w1a", bi)])
                        tk.dma("G", sname, lambda: GP.dma_start(
                            w1sl[bi + 2][:, :].rearrange("p (k c) -> p k c", k=DT)[:, :, 0:n],
                            W1[l, :, c0g:c0g + n].rearrange("(k p) c -> p k c", p=P)),
                            writes=[("w1g", bi)])

                    w1_load(0, 0)
                    if NOG > 1:
                        w1_load(1, 1)
                    for og in range(NOG):
                        n = min(P, FH - og * P)
                        bi = og % 2
                        rot6 = (og * 2 * NU) % 6
                        pa = [psk[(rot6 + j) % 6] for j in range(NU)]
                        pg = [psk[(rot6 + NU + j) % 6] for j in range(NU)]
                        wa = w1sl[bi][:, :].rearrange("p (k c) -> p k c", k=DT)
                        wg = w1sl[bi + 2][:, :].rearrange("p (k c) -> p k c", k=DT)
                        for u in range(NU):
                            for dt in range(DT):
                                tk.op("T", lambda u=u, dt=dt, n=n, wa=wa: T.matmul(
                                    PS[pa[u][1]][0:n, 0:512], wa[:, dt, 0:n],
                                    xnbf[:, dt * HT + u * 512: dt * HT + (u + 1) * 512],
                                    start=(dt == 0), stop=(dt == DT - 1)),
                                    reads=[("w1a", bi), ("xnbf", dt)], writes=[pa[u]])
                        for u in range(NU):
                            for dt in range(DT):
                                tk.op("T", lambda u=u, dt=dt, n=n, wg=wg: T.matmul(
                                    PS[pg[u][1]][0:n, 0:512], wg[:, dt, 0:n],
                                    xnbf[:, dt * HT + u * 512: dt * HT + (u + 1) * 512],
                                    start=(dt == 0), stop=(dt == DT - 1)),
                                    reads=[("w1g", bi), ("xnbf", dt)], writes=[pg[u]])
                        if og + 2 < NOG:
                            w1_load(og + 2, bi)
                        for u in range(NU):
                            tk.op("S", lambda u=u, og=og, n=n: SC.activation(
                                tbuf[0:n, u * 512:(u + 1) * 512],
                                PS[pg[u][1]][0:n, 0:512], AF.Gelu,
                                bias=prm[0:n, PC_B1G + og:PC_B1G + og + 1]),
                                reads=[pg[u], "prm"], writes=[("tb", u)])
                            tk.op("V", lambda u=u, og=og, n=n: V.scalar_tensor_tensor(
                                hbuf[0:n, og * HT + u * 512: og * HT + (u + 1) * 512],
                                PS[pa[u][1]][0:n, 0:512],
                                prm[0:n, PC_B1A + og:PC_B1A + og + 1],
                                tbuf[0:n, u * 512:(u + 1) * 512], ALU.add, ALU.mult),
                                reads=[pa[u], ("tb", u), "prm"], writes=[("h", og)])

                    if hb == 0:
                        for dt in range(4, DT):
                            tk.op("V", lambda dt=dt: V.tensor_copy(
                                bndry[:, dt:dt + 1], ysl(dt, HT - 1, 1)),
                                reads=[("ytr", dt)], writes=[("bnd", dt)])

                    for do in range(DT):
                        w2p = do % 2
                        tk.dma("G", "dW2a" if w2p == 0 else "dW2b",
                               lambda do=do, w2p=w2p: GP.dma_start(
                            w2sl[w2p][:, 0:(NOG - 1) * P]
                            .rearrange("p (g c) -> p g c", g=NOG - 1),
                            W2[l, 0:(NOG - 1) * P, do * P:(do + 1) * P]
                            .rearrange("(g p) c -> p g c", p=P)),
                            writes=[("w2sl", w2p)])
                        tk.dma("G", "dW2a" if w2p == 0 else "dW2b",
                               lambda do=do, w2p=w2p: GP.dma_start(
                            w2sl[w2p][0:FH - (NOG - 1) * P, (NOG - 1) * P:NOG * P],
                            W2[l, (NOG - 1) * P:FH, do * P:(do + 1) * P]),
                            writes=[("w2sl", w2p)])
                        pu = [psk[(do % 2) * 2], psk[(do % 2) * 2 + 1]][:NU]
                        for u in range(NU):
                            for og in range(NOG):
                                n = min(P, FH - og * P)
                                tk.op("T", lambda u=u, og=og, do=do, n=n, w2p=w2p: T.matmul(
                                    PS[pu[u][1]][:, 0:512],
                                    w2sl[w2p][0:n, og * P:og * P + P],
                                    hbuf[0:n, og * HT + u * 512: og * HT + (u + 1) * 512],
                                    start=(og == 0), stop=(og == NOG - 1)),
                                    reads=[("w2sl", w2p), ("h", og)], writes=[pu[u]])
                        for u in range(NU):
                            tk.op("V", lambda u=u, do=do, T0=T0: V.scalar_tensor_tensor(
                                ysl(do, T0 + u * 512, 512), PS[pu[u][1]][:, 0:512],
                                prm[:, PC_B2 + do:PC_B2 + do + 1],
                                ysl(do, T0 + u * 512, 512), ALU.add, ALU.add),
                                reads=[pu[u], "prm", ("ytr", do)], writes=[("ytr", do)])
                tk.barrier()

        def b_attn(l, b):
            with ExitStack() as ls:
                def lsb(n, sh, dt_):
                    return ls.enter_context(nc.sbuf_tensor(f"{n}_l{l}b{b}", sh, dt_))
                aybf = lsb("aybf", [P, DT * 513], BF16)
                qfin = lsb("aqf", [P, DT * 768], BF16)
                kfin = lsb("akf", [P, DT * 768], BF16)
                vrol = lsb("avr", [P, 8 * 1040], BF16)
                tk.op("V", lambda: V.memset(
                    vrol[:, :].rearrange("p (s h c) -> p (s h) c", s=8, h=16)[:, :, 64:65],
                    1.0), writes=[("vr", s) for s in range(8)] if False else ["vrones"])
                ptile = lsb("apt", [P, 4 * 2 * 512], BF16)
                sbf2 = lsb("asbf2", [P, 1536], BF16)
                onat = sbf2[:, 0:1024]
                tt1b_ = lsb("att1", [P, 1024], F32)
                rnbb_ = lsb("arnb", [P, 512], F32)
                msb = lsb("amsb", [P, 2 * 512], F32)
                rinv = lsb("arinv", [P, 16], F32)
                opas = lsb("aopas", [P, DT * 512], BF16)
                W = {
                    "qw": lsb("wqw", [P, 2 * D], BF16),
                    "kwb": lsb("wkwb", [P, 2 * D], BF16),
                    "vw": lsb("wvw", [P, DT * D], BF16),
                    "wos": lsb("wwos", [P, 2 * D], BF16),
                }
                tabs = lsb("atabs", [P, 2 * 512], F32)
                masks = lsb("amasks", [P, 2 * 512], BF16)
                tk.dma("Y", "dW", lambda: SY.dma_start(masks[:, 0:512], MSK[0]),
                       writes=["msk"])
                tk.dma("Y", "dW", lambda: SY.dma_start(masks[:, 512:1024], MSK[1]),
                       writes=["msk"])
                opart = lsb("aopart", [P, 2 * 2 * 1040], F32)

                for dt in range(DT):
                    tk.dma("G", "dQWa" if dt % 2 == 0 else "dQWb", lambda dt=dt: GP.dma_start(
                        W["vw"][:, dt * D:(dt + 1) * D],
                        WQKV[l, dt * P:(dt + 1) * P, 2 * D:3 * D]),
                        writes=[("vw", dt)])

                tk.op("V", lambda: V.reduce_max(srow[:, 0:1], qkr[:, 0:DH], AX.X,
                                                apply_absolute_value=True),
                      reads=["qkr"], writes=[("srow", 0)])
                tk.op("V", lambda: V.reduce_max(srow[:, 1:2], qkr[:, DH:2 * DH], AX.X,
                                                apply_absolute_value=True),
                      reads=["qkr"], writes=[("srow", 0)])
                tk.op("V", lambda: V.tensor_tensor(srow[:, 2:3], srow[:, 0:1],
                                                   srow[:, 1:2], ALU.mult),
                      reads=[("srow", 0)], writes=[("srow", 0)])
                tk.op("T", lambda: T.matmul(PS[0][:, 0:1], onrf[:, :], srow[:, 2:3],
                                            start=True, stop=True),
                      reads=[("srow", 0), "onrf"], writes=[psk[0]])
                tk.op("V", lambda: V.tensor_scalar(negb[:, :], PS[0][:, 0:1],
                                                   -16.0, None, ALU.mult),
                      reads=[psk[0]], writes=["negb"])

                tk.op("V", lambda: V.tensor_copy(qscol[:, 0:1], prm[:, PC_QS:PC_QS + 1]),
                      reads=["prm"], writes=["qsc"])
                tk.op("V", lambda: V.tensor_copy(qscol[:, 1:2], prm[:, PC_KS:PC_KS + 1]),
                      reads=["prm"], writes=["qsc"])
                tk.op("V", lambda: V.tensor_tensor(qscol[:, 2:3], qscol[:, 0:1],
                                                   qscol[:, 0:1], ALU.mult),
                      reads=["qsc"], writes=["qsc"])
                tk.op("V", lambda: V.tensor_tensor(qscol[:, 3:4], qscol[:, 1:2],
                                                   qscol[:, 1:2], ALU.mult),
                      reads=["qsc"], writes=["qsc"])
                tk.op("V", lambda: V.reciprocal(qscol[:, 2:3], qscol[:, 2:3]),
                      reads=["qsc"], writes=["qsc"])
                tk.op("V", lambda: V.reciprocal(qscol[:, 3:4], qscol[:, 3:4]),
                      reads=["qsc"], writes=["qsc"])
                tk.op("V", lambda: V.tensor_scalar(blkq[:, 0:P], blk64[:, :],
                                                   qscol[:, 2:3], None, ALU.mult),
                      reads=["qsc", "blk"], writes=[("blkq", 0)])
                tk.op("V", lambda: V.tensor_scalar(blkq[:, P:2 * P], blk64[:, :],
                                                   qscol[:, 3:4], None, ALU.mult),
                      reads=["qsc", "blk"], writes=[("blkq", 1)])

                def qf_read(side_buf, dt, t0, n):
                    s0 = (t0 // 256) % 3
                    s1 = (s0 + 1) % 3
                    if n <= 256 or s1 == s0 + 1:
                        return [(side_buf[:, dt * 768 + s0 * 256:
                                          dt * 768 + s0 * 256 + n], 0, n)]
                    return [(side_buf[:, dt * 768 + s0 * 256:
                                      dt * 768 + s0 * 256 + 256], 0, 256),
                            (side_buf[:, dt * 768 + s1 * 256:
                                      dt * 768 + s1 * 256 + (n - 256)], 256, n - 256)]

                for c in range(C + 1):
                    if ATTN_STAGE < 2:
                        continue
                    if c < C:
                        t0c = c * 512

                        def absl(dt, j0, n):
                            off = 0 if dt >= 4 else 1
                            return aybf[:, dt * 513 + off + j0: dt * 513 + off + j0 + n]

                        for dt in range(DT):
                            if t0c == 0:
                                tk.op("V", lambda dt=dt: V.memset(
                                    aybf[:, dt * 513:dt * 513 + 1], 0.0),
                                    writes=[("aybf", dt)])
                            else:
                                tk.op("V", lambda dt=dt: V.tensor_copy(
                                    aybf[:, dt * 513:dt * 513 + 1], ysl(dt, t0c - 1, 1)),
                                    reads=[("ytr", dt)], writes=[("aybf", dt)])
                            tk.op("V", lambda dt=dt: V.tensor_copy(
                                aybf[:, dt * 513 + 1:(dt + 1) * 513], ysl(dt, t0c, 512)),
                                reads=[("ytr", dt)], writes=[("aybf", dt)])

                        for tt in range(4):
                            tslot = ((t0c + tt * P) // P) % 8
                            for half in range(2):
                                bank = psk[4 + (2 * tt + half) % 2]
                                for dt in range(DT):
                                    tk.op("T", lambda dt=dt, tt=tt, half=half, bank=bank:
                                          T.matmul(
                                              PS[bank[1]][:, 0:512],
                                              absl(dt, tt * P, P),
                                              W["vw"][:, dt * D + half * 512:
                                                 dt * D + (half + 1) * 512],
                                              start=(dt == 0), stop=(dt == DT - 1)),
                                          reads=[("aybf", dt), ("vw", dt)],
                                          writes=[bank])
                                tk.op("V", lambda half=half, bank=bank, tslot=tslot:
                                      V.tensor_copy(
                                          vrol[:, tslot * 1040 + half * 520:
                                               tslot * 1040 + (half + 1) * 520]
                                          .rearrange("p (h c) -> p h c", h=8)[:, :, 0:64],
                                          PS[bank[1]][:, 0:512]
                                          .rearrange("p (h c) -> p h c", h=8)),
                                      reads=[bank], writes=[("vr", tslot)])

                        for side in range(2 if ATTN_STAGE >= 3 else 0):
                            tbase = 0 if side == 0 else 2
                            for i in range(2):
                                tk.dma("Y", "dTa" if i == 0 else "dTb",
                                       lambda i=i, tbase=tbase: SY.dma_start(
                                           tabs[:, i * 512:(i + 1) * 512],
                                           ROT[tbase + i][:, t0c:t0c + 512]),
                                       writes=["tabs"])
                            sc_col = qscol[:, side:side + 1]
                            dst = qfin if side == 0 else kfin
                            dk = "qf" if side == 0 else "kf"
                            def qk_slab_load(m):
                                kwp = m % 2
                                if side == 1:
                                    tk.dma("G", "dKW0" if kwp == 0 else "dKW1",
                                           lambda m=m, kwp=kwp: GP.dma_start(
                                               W["kwb"][:, kwp * D:(kwp + 1) * D]
                                               .rearrange("p (k c) -> p k c", k=DT),
                                               WQKV[l, :, D + m * P:D + (m + 1) * P]
                                               .rearrange("(k p) c -> p k c", p=P)),
                                           writes=[("kwb", kwp)])
                                else:
                                    tk.dma("G", "dQWa" if kwp == 0 else "dQWb",
                                           lambda m=m, kwp=kwp: GP.dma_start(
                                               W["qw"][:, kwp * D:(kwp + 1) * D]
                                               .rearrange("p (k c) -> p k c", k=DT),
                                               WQKV[l, :, m * P:(m + 1) * P]
                                               .rearrange("(k p) c -> p k c", p=P)),
                                           writes=[("qw", kwp)])

                            if ATTN_SUB >= 2:
                                qk_slab_load(0)
                            for m in range(DT if ATTN_SUB >= 2 else 0):
                                kwp = m % 2
                                if m + 1 < DT:
                                    qk_slab_load(m + 1)
                                bank = psk[m % 3]
                                for dt in range(DT if ATTN_SUB >= 3 else 0):
                                    if side == 0:
                                        lhs = W["qw"][:, kwp * D + dt * P: kwp * D + (dt + 1) * P]
                                        rk = ("qw", kwp)
                                    else:
                                        lhs = W["kwb"][:, kwp * D + dt * P: kwp * D + (dt + 1) * P]
                                        rk = ("kwb", kwp)
                                    tk.op("T", lambda lhs=lhs, dt=dt, bank=bank: T.matmul(
                                        PS[bank[1]][:, 0:512], lhs, absl(dt, 0, 512),
                                        start=(dt == 0), stop=(dt == DT - 1)),
                                        reads=[rk, ("aybf", dt)], writes=[bank])
                                if ATTN_SUB < 4:
                                    continue
                                par = m % 2
                                ubf = sbf2[:, par * 512:(par + 1) * 512]
                                sqb = sbf2[:, 1024:1536]
                                u2 = msb[:, (1 - par) * 512:(2 - par) * 512]
                                tt1 = tt1b_[:, par * 512:(par + 1) * 512]
                                rnb = rnbb_[:, 0:512]
                                rnscr = msb[:, par * 512:(par + 1) * 512]
                                tk.op("S", lambda ubf=ubf, bank=bank, sc_col=sc_col:
                                      SC.activation(
                                          ubf, PS[bank[1]][:, 0:512], AF.Copy,
                                          scale=sc_col),
                                      reads=[bank, "qsc"], writes=[("ubf", par)])
                                if ATTN_SUB < 5:
                                    continue
                                rotb = psk[3 + m % 2]
                                tk.op("T", lambda rotb=rotb, ubf=ubf: T.matmul(
                                    PS[rotb[1]][:, 0:512], rotr[:, :],
                                    ubf, start=True, stop=True),
                                      reads=["rotr", ("ubf", par)], writes=[rotb])
                                tk.op("G", lambda sqb=sqb, ubf=ubf: GP.tensor_tensor(
                                    sqb, ubf, ubf, ALU.mult),
                                    reads=[("ubf", par)], writes=["sqb2"])
                                tk.op("T", lambda side=side, sqb=sqb: T.matmul(
                                    PS[5][:, 0:512], blkq[:, side * P:(side + 1) * P],
                                    sqb, start=True, stop=True),
                                    reads=[("blkq", side), "sqb2"], writes=[psk[5]])
                                tk.op("S", lambda rnscr=rnscr: SC.activation(
                                    rnscr, PS[5][:, 0:512], AF.Ln),
                                      reads=[psk[5]], writes=[("msb", par)])
                                tk.op("S", lambda rnb=rnb, rnscr=rnscr: SC.activation(
                                    rnb, rnscr, AF.Exp, scale=-0.5),
                                      reads=[("msb", par)], writes=["rnb"])
                                tk.op("V", lambda tt1=tt1, bank=bank, sc_col=sc_col:
                                      V.scalar_tensor_tensor(
                                          tt1, PS[bank[1]][:, 0:512], sc_col,
                                          tabs[:, 0:512], ALU.mult, ALU.mult),
                                      reads=[bank, "qsc", "tabs"], writes=[("tt1", par)])
                                tk.op("V", lambda u2=u2, rotb=rotb: V.tensor_tensor(
                                    u2, PS[rotb[1]][:, 0:512],
                                    tabs[:, 512:1024], ALU.mult),
                                    reads=[rotb, "tabs"], writes=[("msb", 1 - par)])
                                tk.op("G", lambda tt1=tt1, u2=u2: GP.tensor_tensor(
                                    tt1, tt1, u2, ALU.add),
                                    reads=[("tt1", par), ("msb", 1 - par)],
                                    writes=[("tt1", par)])
                                sa = (t0c // 256) % 3
                                sbt = (sa + 1) % 3
                                if sbt == sa + 1:
                                    tk.op("V", lambda dst=dst, m=m, sa=sa, tt1=tt1,
                                          rnb=rnb: V.tensor_tensor(
                                        dst[:, m * 768 + sa * 256: m * 768 + sa * 256 + 512],
                                        tt1, rnb, ALU.mult),
                                        reads=[("tt1", par), "rnb"],
                                        writes=[(dk, m, sa), (dk, m, sbt)])
                                else:
                                    tk.op("V", lambda dst=dst, m=m, sa=sa, tt1=tt1,
                                          rnb=rnb: V.tensor_tensor(
                                        dst[:, m * 768 + sa * 256: m * 768 + sa * 256 + 256],
                                        tt1[:, 0:256], rnb[:, 0:256], ALU.mult),
                                        reads=[("tt1", par), "rnb"],
                                        writes=[(dk, m, sa)])
                                    tk.op("V", lambda dst=dst, m=m, sbt=sbt, tt1=tt1,
                                          rnb=rnb: V.tensor_tensor(
                                        dst[:, m * 768 + sbt * 256:
                                            m * 768 + sbt * 256 + 256],
                                        tt1[:, 256:512], rnb[:, 256:512], ALU.mult),
                                        reads=[("tt1", par), "rnb"],
                                        writes=[(dk, m, sbt)])

                    kws = []
                    if 2 * c - 1 >= 0 and c <= C:
                        kws.append(2 * c - 1)
                    if c < C:
                        kws.append(2 * c)
                    for kw in (kws if ATTN_STAGE >= 4 else []):
                        q0 = 256 * kw
                        qn = min(512, S - q0)
                        for hg in range(4):
                            for hh in range(4):
                                h = hg * 4 + hh
                                m = h // 2
                                rbase = (h % 2) * DH
                                for js in range(2):
                                    jt0 = 256 * kw + js * P
                                    ksl = (jt0 // 256) % 3
                                    koff = jt0 % 256
                                    lhs = kfin[rbase:rbase + DH,
                                               m * 768 + ksl * 256 + koff:
                                               m * 768 + ksl * 256 + koff + P]
                                    bank = psk[(hh + js) % 2]
                                    for (ap_, rel, ln_) in qf_read(qfin, m, q0, qn):
                                        tk.op("T", lambda lhs=lhs, ap_=ap_, rel=rel,
                                              ln_=ln_, bank=bank, rbase=rbase: T.matmul(
                                                  PS[bank[1]][:, rel:rel + ln_],
                                                  lhs, ap_[rbase:rbase + DH, :],
                                                  start=True, stop=True),
                                              reads=[("kf", m, ksl),
                                                     ("qf", m, (q0 // 256) % 3),
                                                     ("qf", m, (q0 // 256 + 1) % 3)],
                                              writes=[bank])
                                    ca = js * P
                                    cb = min(js * P + 384, qn)
                                    tk.op("S", lambda hh=hh, js=js, ca=ca, cb=cb,
                                          bank=bank: SC.activation(
                                              ptile[:, (hh * 2 + js) * 512 + ca:
                                                    (hh * 2 + js) * 512 + cb],
                                              PS[bank[1]][:, ca:cb],
                                              AF.Exp, scale=8.0, bias=negb[:, 0:1]),
                                          reads=[bank, "negb"],
                                          writes=[("pt", hh, js)])
                                    meng = V if (hh * 2 + js) % 2 == 0 else GP
                                    tk.op("V" if (hh * 2 + js) % 2 == 0 else "G",
                                          lambda hh=hh, js=js, ca=ca, cb=cb,
                                          meng=meng: meng.tensor_tensor(
                                              ptile[:, (hh * 2 + js) * 512 + ca:
                                                    (hh * 2 + js) * 512 + cb],
                                              ptile[:, (hh * 2 + js) * 512 + ca:
                                                    (hh * 2 + js) * 512 + cb],
                                              masks[:, js * 512 + ca: js * 512 + cb],
                                              ALU.mult),
                                          reads=[("pt", hh, js), "msk"],
                                          writes=[("pt", hh, js)])
                            for w in ((kw, kw + 1) if ATTN_STAGE >= 5 else ()):
                                if w >= NW:
                                    continue
                                wsl = w % 2
                                first = (kw == max(w - 1, 0))
                                for sg in range(2):
                                    st0 = w * 256 + sg * P
                                    if st0 >= S:
                                        continue
                                    coff = st0 - 256 * kw
                                    ob = psk[2 + ((w + sg) % 2) * 0 + sg % 1] if False                                         else psk[2 + (sg % 2)]
                                    rsb = psk[4 + (sg % 2)]
                                    if w == kw and sg == 0:
                                        js_list = [0]
                                    elif w == kw + 1 and sg == 1:
                                        js_list = [1]
                                    else:
                                        js_list = [0, 1]
                                    for hh in range(4):
                                        h = hg * 4 + hh
                                        for js in js_list:
                                            jt0 = 256 * kw + js * P
                                            vsl = (jt0 // P) % 8
                                            lhs = ptile[:, (hh * 2 + js) * 512 + coff:
                                                        (hh * 2 + js) * 512 + coff + P]
                                            tk.op("T", lambda lhs=lhs, h=h, vsl=vsl,
                                                  ob=ob, js=js, hh=hh,
                                                  js_list=js_list: T.matmul(
                                                      PS[ob[1]][:, hh * 65:hh * 65 + 65],
                                                      lhs,
                                                      vrol[:, vsl * 1040 + h * 65:
                                                           vsl * 1040 + h * 65 + 65],
                                                      start=(js == js_list[0]),
                                                      stop=(js == js_list[-1])),
                                                  reads=[("pt", hh, js), ("vr", vsl),
                                                         "vrones"],
                                                  writes=[ob])
                                    oc = wsl * 2080 + sg * 1040 + hg * 260
                                    if first:
                                        tk.op("V", lambda ob=ob, oc=oc: V.tensor_copy(
                                            opart[:, oc:oc + 260], PS[ob[1]][:, 0:260]),
                                            reads=[ob], writes=[("op", wsl, sg, hg)])
                                    else:
                                        tk.op("V", lambda ob=ob, oc=oc: V.tensor_tensor(
                                            opart[:, oc:oc + 260], PS[ob[1]][:, 0:260],
                                            opart[:, oc:oc + 260], ALU.add),
                                            reads=[ob, ("op", wsl, sg, hg)],
                                            writes=[("op", wsl, sg, hg)])

                        if ATTN_STAGE < 6:
                            continue
                        w = kw
                        wsl = w % 2
                        pbase = max(0, 512 * c - 256)
                        for sg in range(2):
                            st0 = w * 256 + sg * P
                            if st0 >= S:
                                continue
                            oc = wsl * 2080 + sg * 1040
                            tk.op("V", lambda oc=oc: V.reciprocal(
                                rinv[:, 0:16],
                                opart[:, oc:oc + 1040]
                                .rearrange("p (g d) -> p g d", g=16)[:, :, 64]),
                                reads=[("op", wsl, sg, 0), ("op", wsl, sg, 1),
                                       ("op", wsl, sg, 2), ("op", wsl, sg, 3)],
                                writes=["rinv"])
                            rexp = rinv[:, 0:16].unsqueeze(2).broadcast_to([P, 16, DH])
                            tk.op("V", lambda oc=oc, rexp=rexp: V.tensor_tensor(
                                onat[:, :].rearrange("p (g d) -> p g d", g=16),
                                opart[:, oc:oc + 1040]
                                .rearrange("p (g d) -> p g d", g=16)[:, :, 0:64],
                                rexp, ALU.mult),
                                reads=[("op", wsl, sg, 0), ("op", wsl, sg, 1),
                                       ("op", wsl, sg, 2), ("op", wsl, sg, 3), "rinv"],
                                writes=[("ubf", 0), ("ubf", 1)])
                            for half4 in range(2):
                                pbb = pbk[half4]
                                for j4 in range(4):
                                    it = half4 * 4 + j4
                                    tk.op("T", lambda it=it, j4=j4, pbb=pbb: T.transpose(
                                        PB[pbb[1]][:, j4 * P:(j4 + 1) * P],
                                        onat[:, it * P:(it + 1) * P], idb[:, :]),
                                        reads=[("ubf", 0), ("ubf", 1), "idb"],
                                        writes=[pbb])
                                off = st0 - pbase
                                dst = opas[:, :].rearrange(
                                    "p (it c) -> p it c", c=512)[:, half4 * 4:half4 * 4 + 4,
                                                                 off:off + P]
                                tk.op("V", lambda pbb=pbb, dst=dst: V.tensor_copy(
                                    dst, PB[pbb[1]][:, 0:512].rearrange(
                                        "p (it c) -> p it c", c=P)),
                                    reads=[pbb],
                                    writes=[("opas", half4 * 4 + j) for j in range(4)])

                    if kws and ATTN_STAGE >= 7:
                        pbase = max(0, 512 * c - 256)
                        pend = min(S, 512 * c + 256)
                        wlen = pend - pbase
                        def wo_load(do):
                            wop = do % 2
                            tk.dma("G", "dWO0" if wop == 0 else "dWO1",
                                   lambda do=do, wop=wop: GP.dma_start(
                                       W["wos"][:, wop * D:(wop + 1) * D]
                                       .rearrange("p (k c) -> p k c", k=DT),
                                       WO[l, :, do * P:(do + 1) * P]
                                       .rearrange("(k p) c -> p k c", p=P)),
                                   writes=[("wos", wop)])

                        wo_load(0)
                        wo_load(1)
                        for do in range(DT):
                            wop = do % 2
                            bank = psk[do % 2]
                            for it in range(DT):
                                tk.op("T", lambda do=do, it=it, wop=wop, bank=bank,
                                      wlen=wlen: T.matmul(
                                          PS[bank[1]][:, 0:wlen],
                                          W["wos"][:, wop * D + it * P: wop * D + (it + 1) * P],
                                          opas[:, it * 512: it * 512 + wlen],
                                          start=(it == 0), stop=(it == DT - 1)),
                                      reads=[("wos", wop), ("opas", it)],
                                      writes=[bank])
                            tk.op("V", lambda do=do, bank=bank, pbase=pbase, wlen=wlen:
                                  V.tensor_tensor(
                                      ysl(do, pbase, wlen), PS[bank[1]][:, 0:wlen],
                                      ysl(do, pbase, wlen), ALU.add),
                                  reads=[bank, ("ytr", do)], writes=[("ytr", do)])
                            if do + 2 < DT:
                                wo_load(do + 2)
                tk.barrier()

        def phase_A(b):
            with ExitStack() as ls:
                def lsb(n, sh, dt_):
                    return ls.enter_context(nc.sbuf_tensor(f"{n}_A{b}", sh, dt_))
                ktr = lsb("ktr", [P, DT * S], BF16)
                vnat = lsb("vnat", [P, NT * D], BF16)
                xnbf = lsb("xnbfA", [P, DT * 512], BF16)
                qtr = lsb("qtrA", [P, DT * 512], BF16)
                kwA = lsb("kwA", [P, 2 * D], BF16)
                xt = [lsb(f"xtA{i}", [P, D], F32) for i in range(2)]
                ast_ = lsb("astA", [P, 8], F32)
                pchk = lsb("pchkA", [P, C * 512], BF16)
                ptr = lsb("ptrA", [P, 2 * P], BF16)
                rsA = lsb("rsA", [P, C + 2], F32)
                obuf = [lsb(f"obufA{i}", [P, D], F32) for i in range(2)]
                tmpo = lsb("tmpoA", [P, D], F32)
                vwA = lsb("vwA", [P, DT * D], BF16)

                for tau in range(NT):
                    xb = xt[tau % 2]
                    sem = "dXa" if tau % 2 == 0 else "dXb"
                    tk.dma("Y", sem, lambda tau=tau, xb=xb: SY.dma_start(
                        xb[:, :], X[b, tau * P:(tau + 1) * P, :]),
                        writes=[("xt", tau % 2)])
                    tk.op("V", lambda xb=xb: V.tensor_reduce(
                        ast_[:, 0:1], xb[:, :], AX.X, ALU.add),
                        reads=[("xt", tau % 2)], writes=["astA"])
                    tk.op("S", lambda xb=xb: SC.activation(tmpo[:, :], xb[:, :], AF.Square),
                          reads=[("xt", tau % 2)], writes=["tmpoA"])
                    tk.op("V", lambda: V.tensor_reduce(
                        ast_[:, 1:2], tmpo[:, :], AX.X, ALU.add),
                        reads=["tmpoA"], writes=["astA"])
                    tk.op("V", lambda: V.tensor_scalar(
                        ast_[:, 2:3], ast_[:, 0:1], 1.0 / D, None, ALU.mult),
                        reads=["astA"], writes=["astA"])
                    tk.op("V", lambda: V.tensor_tensor(
                        ast_[:, 3:4], ast_[:, 2:3], ast_[:, 2:3], ALU.mult),
                        reads=["astA"], writes=["astA"])
                    tk.op("V", lambda: V.tensor_scalar(
                        ast_[:, 4:5], ast_[:, 1:2], 1.0 / D, EPS, ALU.mult, ALU.add),
                        reads=["astA"], writes=["astA"])
                    tk.op("V", lambda: V.tensor_tensor(
                        ast_[:, 5:6], ast_[:, 4:5], ast_[:, 3:4], ALU.subtract),
                        reads=["astA"], writes=["astA"])
                    tk.op("S", lambda: SC.activation(ast_[:, 6:7], ast_[:, 5:6], AF.Ln),
                          reads=["astA"], writes=["astB"])
                    tk.op("S", lambda: SC.activation(ast_[:, 7:8], ast_[:, 6:7],
                                                     AF.Exp, scale=-0.5),
                          reads=["astB"], writes=["astB"])
                    tk.op("V", lambda xb=xb: V.tensor_scalar(
                        obuf[0][:, :], xb[:, :], ast_[:, 2:3], ast_[:, 7:8],
                        ALU.subtract, ALU.mult),
                        reads=[("xt", tau % 2), "astA", "astB"], writes=[("obA", 0)])
                    for dt in range(DT):
                        bank = psk[4 + dt % 2]
                        tk.op("T", lambda dt=dt, bank=bank: T.transpose(
                            PS[bank[1]][:, 0:P], obuf[0][:, dt * P:(dt + 1) * P],
                            idf[:, :]),
                            reads=[("obA", 0), "idf"], writes=[bank])
                        tk.op("V", lambda dt=dt, bank=bank, tau=tau: V.tensor_copy(
                            ysl(dt, tau * P, P), PS[bank[1]][:, 0:P]),
                            reads=[bank], writes=[("ytr", dt)])

                for cc in range(C):
                    t0c = cc * 512
                    for dt in range(DT):
                        tk.op("V", lambda dt=dt, t0c=t0c: V.tensor_copy(
                            xnbf[:, dt * 512:(dt + 1) * 512], ysl(dt, t0c, 512)),
                            reads=[("ytr", dt)], writes=[("xnbfA", dt)])
                    for m in range(DT):
                        kwp = m % 2
                        tk.dma("G", "dKW0" if kwp == 0 else "dKW1",
                               lambda m=m, kwp=kwp: GP.dma_start(
                                   kwA[:, kwp * D:(kwp + 1) * D]
                                   .rearrange("p (k c) -> p k c", k=DT),
                                   Wka[:, m * P:(m + 1) * P]
                                   .rearrange("(k p) c -> p k c", p=P)),
                               writes=[("kwA", kwp)])
                        bank = psk[m % 2]
                        for dt in range(DT):
                            tk.op("T", lambda m=m, dt=dt, kwp=kwp, bank=bank: T.matmul(
                                PS[bank[1]][:, 0:512],
                                kwA[:, kwp * D + dt * P: kwp * D + (dt + 1) * P],
                                xnbf[:, dt * 512:(dt + 1) * 512],
                                start=(dt == 0), stop=(dt == DT - 1)),
                                reads=[("kwA", kwp), ("xnbfA", dt)], writes=[bank])
                        tk.op("V", lambda m=m, bank=bank, t0c=t0c: V.tensor_scalar(
                            ktr[:, m * S + t0c: m * S + t0c + 512],
                            PS[bank[1]][:, 0:512], abias[:, DT + m:DT + m + 1],
                            None, ALU.add),
                            reads=[bank, "abias"], writes=[("ktrA", m)])
                    if cc == 0:
                        for dt in range(DT):
                            tk.dma("G", "dQWa" if dt % 2 == 0 else "dQWb",
                                   lambda dt=dt: GP.dma_start(
                                       vwA[:, dt * D:(dt + 1) * D],
                                       Wva[dt * P:(dt + 1) * P, :]),
                                   writes=[("vwA", dt)])
                    for tt in range(4):
                        kt = cc * 4 + tt
                        for half in range(2):
                            bank = psk[2 + (2 * tt + half) % 2]
                            for dt in range(DT):
                                tk.op("T", lambda dt=dt, tt=tt, half=half, bank=bank:
                                      T.matmul(
                                          PS[bank[1]][:, 0:512],
                                          xnbf[:, dt * 512 + tt * P:
                                               dt * 512 + (tt + 1) * P],
                                          vwA[:, dt * D + half * 512:
                                              dt * D + (half + 1) * 512],
                                          start=(dt == 0), stop=(dt == DT - 1)),
                                      reads=[("xnbfA", dt), ("vwA", dt)], writes=[bank])
                            tk.op("V", lambda kt=kt, half=half, bank=bank: V.tensor_copy(
                                vnat[:, kt * D + half * 512: kt * D + (half + 1) * 512],
                                PS[bank[1]][:, 0:512]),
                                reads=[bank], writes=[("vnatA", kt)])

                for qc in range(C):
                    t0c = qc * 512
                    for dt in range(DT):
                        tk.op("V", lambda dt=dt, t0c=t0c: V.tensor_copy(
                            xnbf[:, dt * 512:(dt + 1) * 512], ysl(dt, t0c, 512)),
                            reads=[("ytr", dt)], writes=[("xnbfA", dt)])
                    for m in range(DT):
                        kwp = m % 2
                        tk.dma("G", "dKW0" if kwp == 0 else "dKW1",
                               lambda m=m, kwp=kwp: GP.dma_start(
                                   kwA[:, kwp * D:(kwp + 1) * D]
                                   .rearrange("p (k c) -> p k c", k=DT),
                                   Wqa[:, m * P:(m + 1) * P]
                                   .rearrange("(k p) c -> p k c", p=P)),
                               writes=[("kwA", kwp)])
                        bank = psk[m % 2]
                        for dt in range(DT):
                            tk.op("T", lambda m=m, dt=dt, kwp=kwp, bank=bank: T.matmul(
                                PS[bank[1]][:, 0:512],
                                kwA[:, kwp * D + dt * P: kwp * D + (dt + 1) * P],
                                xnbf[:, dt * 512:(dt + 1) * 512],
                                start=(dt == 0), stop=(dt == DT - 1)),
                                reads=[("kwA", kwp), ("xnbfA", dt)], writes=[bank])
                        tk.op("V", lambda m=m, bank=bank: V.tensor_scalar(
                            qtr[:, m * 512:(m + 1) * 512],
                            PS[bank[1]][:, 0:512], abias[:, m:m + 1], None, ALU.add),
                            reads=[bank, "abias"], writes=[("qtrA", m)])
                    for sg in range(4):
                        tau = qc * 4 + sg
                        for kc in range(C):
                            bank = psk[kc]
                            for m in range(DT):
                                tk.op("T", lambda m=m, kc=kc, sg=sg, bank=bank: T.matmul(
                                    PS[bank[1]][:, 0:512],
                                    qtr[:, m * 512 + sg * P: m * 512 + (sg + 1) * P],
                                    ktr[:, m * S + kc * 512: m * S + (kc + 1) * 512],
                                    start=(m == 0), stop=(m == DT - 1)),
                                    reads=[("qtrA", m), ("ktrA", m)], writes=[bank])
                            tk.op("V", lambda kc=kc, bank=bank: V.reduce_max(
                                rsA[:, kc:kc + 1], PS[bank[1]][:, 0:512],
                                AX.X, negate=True),
                                reads=[bank], writes=["rsneg"])
                        for kc in range(1, C):
                            tk.op("V", lambda kc=kc: V.tensor_tensor(
                                rsA[:, 0:1], rsA[:, 0:1], rsA[:, kc:kc + 1], ALU.min),
                                reads=["rsneg"], writes=["rsneg"])
                        tk.op("V", lambda: V.tensor_scalar(
                            rsA[:, 0:1], rsA[:, 0:1], 1.0 / 32.0, None, ALU.mult),
                            reads=["rsneg"], writes=["rsneg"])
                        for kc in range(C):
                            bank = psk[kc]
                            tk.op("S", lambda kc=kc, bank=bank: SC.activation(
                                pchk[:, kc * 512:(kc + 1) * 512],
                                PS[bank[1]][:, 0:512], AF.Exp,
                                scale=1.0 / 32.0, bias=rsA[:, 0:1],
                                accum_out=rsA[:, C + 1:C + 2]),
                                reads=[bank, "rsneg"],
                                writes=[("pchkA", kc), "rsacc"])
                            if kc == 0:
                                tk.op("V", lambda: V.tensor_copy(
                                    rsA[:, C:C + 1], rsA[:, C + 1:C + 2]),
                                    reads=["rsacc"], writes=["rstot"])
                            else:
                                tk.op("V", lambda: V.tensor_tensor(
                                    rsA[:, C:C + 1], rsA[:, C:C + 1],
                                    rsA[:, C + 1:C + 2], ALU.add),
                                    reads=["rsacc", "rstot"], writes=["rstot"])
                        tk.op("V", lambda: V.reciprocal(rsA[:, C:C + 1], rsA[:, C:C + 1]),
                              reads=["rstot"], writes=["rstot"])
                        ob = obuf[sg % 2]
                        sem = "dXa" if sg % 2 == 0 else "dXb"
                        tk.dma("Y", sem, lambda tau=tau, ob=ob: SY.dma_start(
                            ob[:, :], X[b, tau * P:(tau + 1) * P, :]),
                            writes=[("obA", sg % 2)])
                        for kt in range(C * 4):
                            pbb = pbk[kt % 2]
                            tk.op("T", lambda kt=kt, pbb=pbb: T.transpose(
                                PB[pbb[1]][:, 0:P], pchk[:, kt * P:(kt + 1) * P],
                                idb[:, :]),
                                reads=[("pchkA", kt // 4), "idb"], writes=[pbb])
                            tk.op("V", lambda kt=kt, pbb=pbb: V.tensor_copy(
                                ptr[:, (kt % 2) * P:(kt % 2 + 1) * P],
                                PB[pbb[1]][:, 0:P]),
                                reads=[pbb], writes=[("ptrA", kt % 2)])
                            for half in range(2):
                                bank = psk[4 + half]
                                tk.op("T", lambda kt=kt, half=half, bank=bank: T.matmul(
                                    PS[bank[1]][:, 0:512],
                                    ptr[:, (kt % 2) * P:(kt % 2 + 1) * P],
                                    vnat[:, kt * D + half * 512:
                                         kt * D + (half + 1) * 512],
                                    start=(kt == 0), stop=(kt == C * 4 - 1)),
                                    reads=[("ptrA", kt % 2), ("vnatA", kt)],
                                    writes=[bank])
                        for half in range(2):
                            bank = psk[4 + half]
                            tk.op("V", lambda half=half, bank=bank: V.scalar_tensor_tensor(
                                tmpo[:, half * 512:(half + 1) * 512],
                                PS[bank[1]][:, 0:512], rsA[:, C:C + 1],
                                bvbc[:, half * 512:(half + 1) * 512],
                                ALU.mult, ALU.add),
                                reads=[bank, "rstot", "bvbc"], writes=["tmpoA"])
                        for half in range(2):
                            tk.op("V", lambda half=half, ob=ob: V.tensor_tensor(
                                ob[:, half * 512:(half + 1) * 512],
                                tmpo[:, half * 512:(half + 1) * 512],
                                ob[:, half * 512:(half + 1) * 512], ALU.add),
                                reads=["tmpoA", ("obA", sg % 2)],
                                writes=[("obA", sg % 2)])
                        tk.dma("Y", "dO", lambda tau=tau, ob=ob: SY.dma_start(
                            OUT[b, tau * P:(tau + 1) * P, :], ob[:, :]),
                            reads=[("obA", sg % 2)], writes=[("out", b, tau)])
                tk.barrier()

        def tail(b):
            with ExitStack() as ls:
                ynat = ls.enter_context(nc.sbuf_tensor(f"ynatT{b}", [P, D], F32))
                for tau in range(NT):
                    for dt in range(DT):
                        bank = psk[4 + dt % 2]
                        tk.op("T", lambda dt=dt, tau=tau, bank=bank: T.transpose(
                            PS[bank[1]][:, 0:P], ysl(dt, tau * P, P), idf[:, :]),
                            reads=[("ytr", dt), "idf"], writes=[bank])
                        tk.op("V", lambda dt=dt, bank=bank: V.tensor_copy(
                            ynat[:, dt * P:(dt + 1) * P], PS[bank[1]][:, 0:P]),
                            reads=[bank], writes=["ynatT"])
                    tk.dma("G", "dAcc", lambda tau=tau: GP.dma_start(
                        OUT[b, tau * P:(tau + 1) * P, :], ynat[:, :],
                        accum_op=ALU.add),
                        reads=["ynatT", ("out", b, tau)], writes=[("out", b, tau)])
                tk.barrier()

        for b in range(BPC):
            if y0_input:
                for dt in range(DT):
                    tk.dma("Y", "dXa", lambda dt=dt: SY.dma_start(
                        ytr[:, dt * S:(dt + 1) * S], Y0[b, :, dt * S:(dt + 1) * S]),
                        writes=[("ytr", dt)])
            if not skip_a:
                phase_A(b)
            for l in range(L):
                tk.dma("Y", "dP", lambda l=l: SY.dma_start(prm[:, :], PRM[l]),
                       writes=["prm"])
                tk.dma("Y", "dP", lambda l=l: SY.dma_start(qkr[:, :], QKR[l]),
                       writes=["qkr"])
                if not skip_attn:
                    b_attn(l, b)
                if not skip_ffn:
                    b_ffn(l, b)
            if not skip_tail:
                tail(b)
            for tn in taps:
                if tn.split(":")[0] == "ytr":
                    for dt in range(DT):
                        tk.dma("Y", "dO", lambda dt=dt, tn=tn: SY.dma_start(
                            tap_t[tn][b, :, dt * S:(dt + 1) * S],
                            ytr[:, dt * S:(dt + 1) * S]),
                            reads=[("ytr", dt)])

        for sname in ["dO", "dAcc"]:
            if tk.dcnt[sname] > 0:
                SY.wait_ge(tk.dsem[sname], tk.dcnt[sname])
    return nc


_NC_CACHE = {}


def build_full():
    if "nc" not in _NC_CACHE:
        _NC_CACHE["nc"] = build_program(FULL_B // N_CORES, FULL_S, FULL_L)
    return _NC_CACHE["nc"]


def kernel(**inputs):
    S, L = FULL_S, FULL_L
    BPC = FULL_B // N_CORES
    hc = host_constants(S)
    prm, qkr = pack_params(inputs["L0_ffn_q_scale"], inputs["L0_ffn_k_scale"],
                           inputs["L0_ffn_ln_g"], inputs["L0_ffn_ln_b"],
                           inputs["L0_ffn_b1"], inputs["L0_ffn_b2"])
    abias = np.stack([np.asarray(inputs["L0_attn_b_q"]).reshape(DT, P).T,
                      np.asarray(inputs["L0_attn_b_k"]).reshape(DT, P).T]
                     ).astype(np.float32)
    X = np.ascontiguousarray(np.asarray(inputs["L0_attn_X"], dtype=np.float32))
    common = {
        "Wqa": np.ascontiguousarray(np.asarray(inputs["L0_attn_W_q"], np.float32)),
        "Wka": np.ascontiguousarray(np.asarray(inputs["L0_attn_W_k"], np.float32)),
        "Wva": np.ascontiguousarray(np.asarray(inputs["L0_attn_W_v"], np.float32)),
        "BV": np.ascontiguousarray(np.asarray(inputs["L0_attn_b_v"], np.float32)),
        "ABIAS": abias,
        "WQKV": np.ascontiguousarray(np.asarray(inputs["L0_ffn_Wqkv"], np.float32)),
        "WO": np.ascontiguousarray(np.asarray(inputs["L0_ffn_Wo"], np.float32)),
        "W1": np.ascontiguousarray(np.asarray(inputs["L0_ffn_W1"], np.float32)),
        "W2": np.ascontiguousarray(np.asarray(inputs["L0_ffn_W2"], np.float32)),
        "PRM": prm, "QKR": qkr,
        "ROT": hc["rotab"], "MSK": hc["masks"], "IDB": hc["ident_bf"],
        "IDF": hc["ident_f32"], "BLK": hc["blk64"], "ONEC": hc["ones_col"],
        "ONRF": hc["ones_row_f32"], "ROTR": hc["rotR"],
    }
    nc = build_full()
    in_maps = []
    for c in range(N_CORES):
        m = dict(common)
        m["X"] = np.ascontiguousarray(X[c * BPC:(c + 1) * BPC])
        in_maps.append(m)
    res = run_bass_kernel_spmd(nc, in_maps, core_ids=list(range(N_CORES)))
    out = np.concatenate([np.asarray(res.results[c]["OUT"]) for c in range(N_CORES)],
                         axis=0)
    return out.astype(np.float32)
